# version 12
# baseline (speedup 1.0000x reference)
import contextlib
import numpy as np

import concourse.bass as bass
import concourse.mybir as mybir
from concourse.bass_utils import run_bass_kernel_spmd

F32 = mybir.dt.float32
BF16 = mybir.dt.bfloat16
I32 = mybir.dt.int32
ALU = mybir.AluOpType
ACT = mybir.ActivationFunctionType
AX = mybir.AxisListType

P = 128
D = 1024
H = 8
DH = 128
I3 = 3072
DCH = D // P
ICH = I3 // P
DSC = float(DH) ** -0.5
EPS = 1e-5
QM = 32767.0
U16 = 65536.0
WQM = 7.0


STRIPED = {"aL": 2, "aSp": 2, "uL": 2, "dnL": 2, "fL": 2, "pSp": 2,
           "qL": 2, "rLp": 2, "rLu": 2, "cK": 8}


class Sched:
    def __init__(self):
        self.tab = {}
        self.cnt = {}
        self.arr = {}
        self.seq = {}

    def ev(self, sem, key, k):
        r = STRIPED.get(sem, 1)
        a = self.arr.get(sem, 0)
        self.arr[sem] = a + 1
        st = a % r
        c = self.cnt.get((sem, st), 0) + k
        self.cnt[(sem, st)] = c
        assert (sem, key) not in self.tab, f"dup event {sem} {key}"
        self.tab[(sem, key)] = (st, c, k, a)
        self.seq.setdefault(sem, []).append((key, st, c))
        return c

    def waits_for(self, sem, key):
        st, c, k, a = self.tab[(sem, key)]
        if sem not in STRIPED:
            return [(0, c)]
        out = {}
        for i in range(a + 1):
            _, sti, ci = self.seq[sem][i]
            out[sti] = ci
        return sorted(out.items())


class _Dummy:
    def then_inc(self, *a, **k):
        return self


AUTO_SS = [False]
_SSOPS = frozenset((
    "activation", "tensor_tensor", "tensor_scalar", "scalar_tensor_tensor",
    "tensor_reduce", "tensor_copy", "bn_stats", "bn_aggr", "reciprocal",
    "memset", "matmul", "transpose", "ldweights", "iota"))


class Eng:

    def __init__(self, sched, sems, real=None, ename=None):
        self.sched = sched
        self.sems = sems
        self.real = real
        self.dry = real is None
        self.ename = ename

    def _settle(self):
        p = self.__dict__.get("_pend")
        if p is None:
            return
        self.__dict__["_pend"] = None
        ins, inc_info = p
        if inc_info is None:
            sem = self.sems["SS_" + self.ename]
            c = self.sched.cnt["SS_" + self.ename] = (
                self.sched.cnt.get("SS_" + self.ename, 0) + 1)
            ins.then_inc(sem, 1)
            self.real.wait_ge(sem, c)
        else:
            semname, st, cum = inc_info
            self.real.wait_ge(self._sem(semname, st), cum)

    def __getattr__(self, name):
        if self.dry:
            return lambda *a, **k: _Dummy()
        fn = getattr(self.real, name)
        if AUTO_SS[0]:
            def wrap(*a, **k):
                self._settle()
                ins = fn(*a, **k)
                if name in _SSOPS:
                    self.__dict__["_pend"] = (ins, None)
                return ins
            return wrap
        return fn

    def _sem(self, sem, st):
        return self.sems[(sem, st)] if (sem, st) in self.sems else self.sems[sem]

    def inc(self, instr, sem, key, k=1):
        if self.dry:
            self.sched.ev(sem, key, k)
        else:
            st, c, kk, a = self.sched.tab[(sem, key)]
            instr.then_inc(self._sem(sem, st), kk)
            p = self.__dict__.get("_pend")
            if p is not None and p[0] is instr and kk == 1:
                self.__dict__["_pend"] = (instr, (sem, st, c))
        return instr

    def wait(self, sem, key):
        if not self.dry:
            self._settle()
            for st, c in self.sched.waits_for(sem, key):
                self.real.wait_ge(self._sem(sem, st), c)

    def wait_n(self, sem, n):
        if not self.dry:
            self._settle()
            self.real.wait_ge(self.sems[sem], n)

    def ss(self, instr):
        if self.dry or AUTO_SS[0]:
            return instr
        sem = self.sems["SS_" + self.ename]
        c = self.sched.cnt["SS_" + self.ename] = (
            self.sched.cnt.get("SS_" + self.ename, 0) + 1)
        instr.then_inc(sem, 1)
        self.real.wait_ge(sem, c)
        return instr


def ap_mk(t_ap, off_elems, dims):
    return bass.AP(tensor=t_ap.tensor, offset=t_ap.offset + off_elems,
                   ap=[list(d) for d in dims])


def pstep(sb):
    return sb.ap().ap[0][0]


def run_phase(nc, fns, sched, sems):
    for name in ("sync", "gpsimd", "vector", "scalar", "tensor"):
        if name in fns:
            fns[name](Eng(sched, sems, None, name))
    def emit(eng, fn, name):
        e = Eng(sched, sems, eng, name)
        fn(e)
        e._settle()

    with nc.Block() as blk:
        for name in ("sync", "gpsimd", "vector", "scalar", "tensor"):
            if name in fns:
                fn = fns[name]
                getattr(blk, name)(
                    lambda eng, fn=fn, name=name: emit(eng, fn, name))


def build_program(n_seq=4, T=2048, n_cores=8, detect_races=False, dbg=False):
    KI = "ExternalOutput" if dbg else "Internal"
    assert T % 512 == 0
    NTOK = n_seq * T
    NT = T // P
    NQC = T // 512
    TCH = min(2048, NTOK)
    NTC = NTOK // TCH
    TSL = TCH // 512
    NTILE = NTOK // P

    nc = bass.Bass("TRN2", target_bir_lowering=False, debug=False,
                   detect_race_conditions=detect_races)

    x_tm = nc.dram_tensor("x_tm", [NTOK, D], F32, kind="ExternalInput").ap()
    wq_d = nc.dram_tensor("wq", [D, D], F32, kind="ExternalInput").ap()
    wk_d = nc.dram_tensor("wk", [D, D], F32, kind="ExternalInput").ap()
    wv_d = nc.dram_tensor("wv", [D, D], F32, kind="ExternalInput").ap()
    wo_d = nc.dram_tensor("wo", [D, D], F32, kind="ExternalInput").ap()
    wg_d = nc.dram_tensor("w_gate", [I3, D], F32, kind="ExternalInput").ap()
    wu_d = nc.dram_tensor("w_up", [I3, D], F32, kind="ExternalInput").ap()
    wd_d = nc.dram_tensor("w_down", [D, I3], F32, kind="ExternalInput").ap()
    ropeC = nc.dram_tensor("rope_cos", [P, NT * 32], F32, kind="ExternalInput").ap()
    ropeS = nc.dram_tensor("rope_sin", [P, NT * 32], F32, kind="ExternalInput").ap()
    cmask = nc.dram_tensor("cmask", [P, P], F32, kind="ExternalInput").ap()
    ident = nc.dram_tensor("ident", [P, P], BF16, kind="ExternalInput").ap()
    out_d = nc.dram_tensor("out", [NTOK, D], F32, kind="ExternalOutput").ap()

    xn_sp = nc.dram_tensor("xn_sp", [2 * T, D], BF16, kind=KI).ap()
    y_sp = nc.dram_tensor("y_sp", [NTOK, D], BF16, kind=KI).ap()
    x2_sp = nc.dram_tensor("x2_sp", [NTOK, D], F32, kind=KI).ap()
    q_sp = nc.dram_tensor("q_sp", [2 * T, D], BF16, kind=KI).ap()
    k_sp = nc.dram_tensor("k_sp", [2 * T, D], BF16, kind=KI).ap()
    upr_sp = nc.dram_tensor("upr_sp", [I3, NTOK], BF16, kind=KI).ap()
    gtr_sp = nc.dram_tensor("gtr_sp", [I3, NTOK], BF16, kind=KI).ap()
    p_sp = nc.dram_tensor("p_sp", [I3, NTOK], BF16, kind=KI).ap()
    qr_sp = nc.dram_tensor("qr_sp", [I3, NTOK], BF16, kind=KI).ap()
    cc_in = [nc.dram_tensor(f"cc_in{i}", [1, 2], F32, kind="Internal").ap()
             for i in range(4)]
    cc_out = [nc.dram_tensor(f"cc_out{i}", [1, 2], F32, kind="Internal").ap()
              for i in range(4)]
    groups = [list(range(n_cores))]

    sched = Sched()
    es = contextlib.ExitStack()

    def SB(name, shape, dt=F32):
        return es.enter_context(nc.sbuf_tensor(name, shape, dt))

    with es:
        sc = {n: SB("sc_" + n, [P, 1]) for n in
              ("raw", "s_x", "inv_sx", "s_up", "s_gate", "c_up", "c_gt",
               "s_g2", "c_g2", "f_g2", "s_o", "c_o", "f_o", "so2", "raw2",
               "rec")}
        acc = {n: SB("acc_" + n, [P, 1]) for n in ("mx", "up", "gt", "g2", "o")}
        red1 = SB("red1", [P, 1])
        scal1 = SB("scal1", [1, 2])
        ident_sb = SB("ident_sb", [P, P], BF16)

        semnames = (
            "tab", "w16", "aL", "aV1", "aS1", "aV2", "aSx", "aSp",
            "bX", "bM", "bQ", "bV", "bR", "bS",
            "cK", "dQ", "dS", "dM", "dE", "dP", "dZ", "dO",
            "eT", "eW", "eV1", "eS1", "eV2", "eX", "eS2", "eV3", "eS3",
            "eV4", "eS4", "eV5", "eSp",
            "gd", "gc", "cs", "vs", "uL", "uV", "uT", "uE",
            "pY", "pV", "pM", "pE", "pSp",
            "dnL", "dnV", "dnT", "dnE",
            "qL", "qI", "qV", "qG", "qSp",
            "rLp", "rLu", "rI", "rV", "rSp",
            "fL", "fV0", "fM", "fV1", "fS1", "fV2", "fX", "fS2", "fV3", "fO",
            "SS_vector", "SS_scalar", "SS_gpsimd", "SS_tensor",
        )
        sems = {}
        for n in semnames:
            if n in STRIPED:
                for p in range(STRIPED[n]):
                    sems[(n, p)] = es.enter_context(nc.semaphore(f"{n}_{p}"))
            else:
                sems[n] = es.enter_context(nc.semaphore(n))

        def ktmax(qc):
            return 4 * qc + 3

        def prev_h(b, qc, h):
            i = (b * NQC + qc) * H + h - 1
            bq, ph = divmod(i, H)
            pb, pqc = divmod(bq, NQC)
            return (pb, pqc, ph)

        p1es = contextlib.ExitStack()

        def SB1(name, shape, dt=F32):
            return p1es.enter_context(nc.sbuf_tensor(name, shape, dt))

        with p1es:
            pp = [p1es.enter_context(nc.psum_tensor(f"pp{i}", [P, 1024], F32))
                  for i in range(4)]
            wqkv_sb = SB1("wqkv_sb", [P, 3 * DCH * 1024], BF16)
            wo_sb = SB1("wo_sb", [P, DCH * 1024], BF16)
            ropC_sb = SB1("ropC_sb", [P, NT * 32])
            ropS_sb = SB1("ropS_sb", [P, NT * 32])
            cmask_sb = SB1("cmask_sb", [P, P])
            xa = [SB1(f"xa{i}", [P, D]) for i in range(2)]
            st6 = SB1("st6", [P, 12])
            ag2e = SB1("ag2e", [P, 2])
            vepsA = [SB1(f"vepsA{i}", [P, 1]) for i in range(2)]
            rstdA = [SB1(f"rstdA{i}", [P, 1]) for i in range(2)]
            ag2a = [SB1(f"ag2a{i}", [P, 2]) for i in range(2)]
            nmrA = SB1("nmrA", [P, 1])
            veps3 = SB1("veps3", [P, 1])
            rstd3 = SB1("rstd3", [P, 1]); nmr3 = SB1("nmr3", [P, 1])
            veps2 = SB1("veps2", [P, 1])
            rstd2 = SB1("rstd2", [P, 1]); nmr2 = SB1("nmr2", [P, 1])
            xn_tm = [SB1(f"xn_tm{i}", [P, D], BF16) for i in range(2)]
            xnf = SB1("xnf", [P, DCH * 512], BF16)
            q_tm = SB1("q_tm", [P, D], BF16)
            k_tm = SB1("k_tm", [P, D], BF16)
            rsc = SB1("rsc", [P, 1024], BF16)
            vaug = SB1("vaug", [P, NT * 1032], BF16)
            kfm = SB1("kfm", [P, H * T], BF16)
            qfm = SB1("qfm", [P, H * 512], BF16)
            et = [SB1(f"et{i}", [P, 512], BF16) for i in range(2)]
            zrec = SB1("zrec", [P, 4])
            o_tm = SB1("o_tm", [P, 4 * 1024], BF16)
            o_fm = SB1("o_fm", [P, 1024], BF16)
            xe = SB1("xe", [P, D])
            t4sb = SB1("t4sb", [P, D])
            x2sb = SB1("x2sb", [P, D])
            ysb = SB1("ysb", [P, D], BF16)

            def p1_sync(e):
                e.inc(e.dma_start(out=ropC_sb[:, :], in_=ropeC), "tab", "rc", 16)
                e.inc(e.dma_start(out=ropS_sb[:, :], in_=ropeS), "tab", "rs", 16)
                e.inc(e.dma_start(out=cmask_sb[:, :], in_=cmask), "tab", "cm", 16)
                e.inc(e.dma_start(out=ident_sb[:, :], in_=ident), "tab", "id", 16)
                def xload(g):
                    if g >= 2:
                        e.wait("aSx", ("xn", g - 2))
                    e.inc(e.dma_start(out=xa[g % 2][:, :],
                                      in_=x_tm[g * P:(g + 1) * P, :]),
                          "aL", ("x", g), 16)

                for t in range(NT):
                    xload(t)
                for b in range(n_seq):
                    if b + 1 < n_seq:
                        for t in range(NT):
                            xload((b + 1) * NT + t)
                    e.wait("bS", ("k", b, NT - 1))
                    if b >= 1:
                        e.wait("dS", ("s", b - 1, NQC - 1, H - 1,
                                      ktmax(NQC - 1)))
                    for h in range(H):
                        e.inc(e.dma_start_transpose(
                            out=kfm[:, h * T:(h + 1) * T],
                            in_=k_sp[(b % 2) * T:(b % 2) * T + T,
                                     h * DH:(h + 1) * DH]),
                            "cK", ("k", b, h), 16)
                    for qc in range(NQC):
                        if (b, qc) != (0, 0):
                            i = b * NQC + qc - 1
                            pb, pqc = divmod(i, NQC)
                            e.wait("dS", ("s", pb, pqc, H - 1, ktmax(pqc)))
                        for h in range(H):
                            e.inc(e.dma_start_transpose(
                                out=qfm[:, h * 512:(h + 1) * 512],
                                in_=q_sp[(b % 2) * T + qc * 512:
                                         (b % 2) * T + (qc + 1) * 512,
                                         h * DH:(h + 1) * DH]),
                                "dQ", ("q", b, qc, h), 16)
                        for qt in range(4):
                            gq = (b * NQC + qc) * 4 + qt
                            e.wait("dO", ("o", b, qc, H - 1))
                            if gq >= 1:
                                e.wait("eW", ("wo", gq - 1))
                            for c in range(DCH):
                                e.inc(e.dma_start_transpose(
                                    out=o_fm[:, c * P:(c + 1) * P],
                                    in_=o_tm[:, qt * 1024 + c * P:
                                             qt * 1024 + (c + 1) * P]),
                                    "eT", ("o", gq, c), 16)
                            if gq >= 1:
                                e.wait("eV3", ("x2st", gq - 1))
                            r0 = b * T + qc * 512 + qt * P
                            e.inc(e.dma_start(out=xe[:, :],
                                              in_=x_tm[r0:r0 + P, :]),
                                  "eX", ("x", gq), 16)

            def p1_gp(e):
                for wi, wd in enumerate((wq_d, wk_d, wv_d)):
                    src = ap_mk(wd, 0, [[D, P], [P * D, DCH], [1, 1024]])
                    e.inc(e.dma_start(out=wqkv_sb[:, wi * DCH * 1024:
                                                  (wi + 1) * DCH * 1024],
                                      in_=src), "w16", ("w", wi), 16)
                src = ap_mk(wo_d, 0, [[D, P], [P * D, DCH], [1, 1024]])
                e.inc(e.dma_start(out=wo_sb[:, :], in_=src), "w16", ("w", 3), 16)
                e.inc(e.memset(vaug[:, :], 1.0), "bV", "ones", 1)
                def xnspill(g):
                    bb, t = divmod(g, NT)
                    e.wait("aSx", ("xn", g))
                    e.inc(e.dma_start(
                        out=xn_sp[(bb % 2) * T + t * P:
                                  (bb % 2) * T + (t + 1) * P, :],
                        in_=xn_tm[g % 2][:, :]),
                        "aSp", ("xn", g), 16)

                for t in range(NT):
                    xnspill(t)
                for b in range(n_seq):
                    if b >= 2:
                        e.wait("cK", ("k", b - 2, H - 1))
                        e.wait("dQ", ("q", b - 2, NQC - 1, H - 1))
                    if b >= 1 and b + 1 < n_seq:
                        e.wait("bX", ("xn", b - 1, NQC - 1, DCH - 1))
                    for kt in range(NT):
                        e.wait("bR", ("rope", b, kt))
                        r0 = (b % 2) * T + kt * P
                        e.inc(e.dma_start(out=q_sp[r0:r0 + P, :],
                                          in_=q_tm[:, :]),
                              "bS", ("q", b, kt), 16)
                        e.inc(e.dma_start(out=k_sp[r0:r0 + P, :],
                                          in_=k_tm[:, :]),
                              "bS", ("k", b, kt), 16)
                        if b + 1 < n_seq:
                            xnspill((b + 1) * NT + kt)
                    for qc in range(NQC):
                        for qt in range(4):
                            gq = (b * NQC + qc) * 4 + qt
                            e.wait("eS4", ("y", gq))
                            e.wait("eV5", ("mx", gq))
                            r0 = b * T + qc * 512 + qt * P
                            e.inc(e.dma_start(out=y_sp[r0:r0 + P, :],
                                              in_=ysb[:, :]),
                                  "eSp", ("y", gq), 16)
                            e.inc(e.dma_start(out=x2_sp[r0:r0 + P, :],
                                              in_=x2sb[:, :]),
                                  "eSp", ("x2", gq), 16)

            def p1_vec(e):
                e.memset(acc["mx"][:, :], 0.0)
                e.wait("tab", "id")

                def sa_front(g):
                    e.wait("aL", ("x", g))
                    e.bn_stats(st6[:, 0:6], xa[g % 2][:, 0:512])
                    e.ss(e.bn_stats(st6[:, 6:12], xa[g % 2][:, 512:1024]))
                    e.ss(e.bn_aggr(ag2a[g % 2][:, :], st6[:, :]))
                    e.inc(e.tensor_scalar(vepsA[g % 2][:, :],
                                          ag2a[g % 2][:, 1:2], EPS,
                                          None, ALU.add),
                          "aV1", ("st", g), 1)

                def sa_back(g):
                    e.wait("aS1", ("rstd", g))
                    e.inc(e.scalar_tensor_tensor(
                        nmrA[:, :], ag2a[g % 2][:, 0:1], -1.0,
                        rstdA[g % 2][:, :],
                        ALU.mult, ALU.mult), "aV2", ("nmr", g), 1)
                    e.wait("aV2", ("nmr", g))
                    if g >= 2:
                        e.wait("aSp", ("xn", g - 2))
                    e.inc(e.tensor_scalar(xn_tm[g % 2][:, :], xa[g % 2][:, :],
                                          rstdA[g % 2][:, 0:1], nmrA[:, 0:1],
                                          ALU.mult, ALU.add),
                          "aSx", ("xn", g), 1)

                for t in range(NT):
                    sa_front(t)
                    if t >= 1:
                        sa_back(t - 1)
                sa_back(NT - 1)
                for b in range(n_seq):
                    if b >= 1:
                        e.wait("dP", ("pv", b - 1, NQC - 1, H - 1,
                                      ktmax(NQC - 1)))
                    for kt in range(NT):
                        ti = b * NT + kt
                        e.wait("bM", ("qkv", ti))
                        dst = ap_mk(vaug.ap(), kt * 1032,
                                    [[pstep(vaug), P], [129, H], [1, DH]])
                        srcp = ap_mk(pp[2].ap(), 0,
                                     [[pstep(pp[2]), P], [DH, H], [1, DH]])
                        e.inc(e.tensor_copy(dst, srcp), "bV", ("v", ti), 1)
                        if ti >= 1:
                            e.wait("bS", ("k", b, kt - 1) if kt
                                   else ("k", b - 1, NT - 1))
                        e.tensor_scalar(q_tm[:, :], pp[0][:, :], DSC, None,
                                        ALU.mult)
                        e.inc(e.tensor_scalar(k_tm[:, :], pp[1][:, :], DSC,
                                              None, ALU.mult),
                              "bQ", ("qk", ti), 1)
                        for ten in (q_tm, k_tm):
                            ev = ap_mk(ten.ap(), 0,
                                       [[pstep(ten), P], [DH, H], [1, 32]])
                            od = ap_mk(ten.ap(), 32,
                                       [[pstep(ten), P], [DH, H], [1, 32]])
                            cosap = ap_mk(ropC_sb.ap(), kt * 32,
                                          [[pstep(ropC_sb), P], [0, H], [1, 32]])
                            sinap = ap_mk(ropS_sb.ap(), kt * 32,
                                          [[pstep(ropS_sb), P], [0, H], [1, 32]])
                            t1 = rsc[:, 0:256]; t2 = rsc[:, 256:512]
                            t3 = rsc[:, 512:768]; t4_ = rsc[:, 768:1024]
                            e.tensor_tensor(t1, ev, cosap, ALU.mult)
                            e.tensor_tensor(t2, od, sinap, ALU.mult)
                            e.tensor_tensor(t3, ev, sinap, ALU.mult)
                            e.tensor_tensor(t4_, od, cosap, ALU.mult)
                            e.tensor_tensor(ev, t1, t2, ALU.subtract)
                            ins = e.tensor_tensor(od, t3, t4_, ALU.add)
                        e.inc(ins, "bR", ("rope", b, kt), 1)
                        if b + 1 < n_seq:
                            g2 = (b + 1) * NT + kt
                            sa_front(g2)
                            if kt >= 1:
                                sa_back(g2 - 1)
                    if b + 1 < n_seq:
                        sa_back((b + 1) * NT + NT - 1)
                    for qc in range(NQC):
                        for h in range(H):
                            for kt in range(4 * qc, ktmax(qc) + 1):
                                qs = kt - 4 * qc
                                e.wait("dS", ("s", b, qc, h, kt))
                                pb_ = pp[2][:, (kt % 2) * 512 + qs * P:
                                            (kt % 2) * 512 + (qs + 1) * P]
                                e.inc(e.tensor_tensor(pb_, pb_, cmask_sb[:, :],
                                                      ALU.add),
                                      "dM", ("m", b, qc, h, kt), 1)
                            e.wait("dP", ("pv", b, qc, h, ktmax(qc)))
                            for qs in range(4):
                                psoz = pp[qs // 2][:, (qs % 2) * 512 + 128:
                                                   (qs % 2) * 512 + 129]
                                e.ss(e.reciprocal(zrec[:, qs:qs + 1], psoz))
                            if h == 0 and gq_prev(b, qc, NQC) is not None:
                                e.wait("eT", ("o", gq_prev(b, qc, NQC),
                                              DCH - 1))
                            ins = None
                            for qs in range(4):
                                pso = pp[qs // 2][:, (qs % 2) * 512:
                                                  (qs % 2) * 512 + 128]
                                ins = e.tensor_scalar(
                                    o_tm[:, qs * 1024 + h * DH:
                                         qs * 1024 + (h + 1) * DH],
                                    pso, zrec[:, qs:qs + 1], None, ALU.mult)
                            e.inc(ins, "dO", ("o", b, qc, h), 1)
                        for qt in range(4):
                            gq = (b * NQC + qc) * 4 + qt
                            e.wait("eW", ("wo", gq))
                            e.bn_stats(st6[:, 0:6], pp[3][:, 0:512])
                            e.ss(e.bn_stats(st6[:, 6:12], pp[3][:, 512:1024]))
                            e.ss(e.bn_aggr(ag2e[:, :], st6[:, :]))
                            e.inc(e.tensor_scalar(veps3[:, :], ag2e[:, 1:2],
                                                  EPS, None, ALU.add),
                                  "eV1", ("st3", gq), 1)
                            e.wait("eS1", ("r3", gq))
                            e.inc(e.scalar_tensor_tensor(
                                nmr3[:, :], ag2e[:, 0:1], -1.0, rstd3[:, :],
                                ALU.mult, ALU.mult), "eV2", ("nmr3", gq), 1)
                            e.wait("eV2", ("nmr3", gq))
                            e.inc(e.tensor_scalar(t4sb[:, :], pp[3][:, :],
                                                  rstd3[:, 0:1], nmr3[:, 0:1],
                                                  ALU.mult, ALU.add),
                                  "eS2", ("t4", gq), 1)
                            e.wait("eX", ("x", gq))
                            if gq >= 1:
                                e.wait("eSp", ("x2", gq - 1))
                            e.tensor_tensor(x2sb[:, :], t4sb[:, :], xe[:, :],
                                            ALU.add)
                            e.bn_stats(st6[:, 0:6], x2sb[:, 0:512])
                            e.ss(e.bn_stats(st6[:, 6:12], x2sb[:, 512:1024]))
                            e.ss(e.bn_aggr(ag2e[:, :], st6[:, :]))
                            e.inc(e.tensor_scalar(veps2[:, :], ag2e[:, 1:2],
                                                  EPS, None, ALU.add),
                                  "eV3", ("x2st", gq), 1)
                            e.wait("eS3", ("r2", gq))
                            e.inc(e.scalar_tensor_tensor(
                                nmr2[:, :], ag2e[:, 0:1], -1.0, rstd2[:, :],
                                ALU.mult, ALU.mult), "eV4", ("nmr2", gq), 1)
                            e.wait("eV4", ("nmr2", gq))
                            e.inc(e.tensor_scalar(ysb[:, :], x2sb[:, :],
                                                  rstd2[:, 0:1], nmr2[:, 0:1],
                                                  ALU.mult, ALU.add),
                                  "eS4", ("y", gq), 1)
                            e.ss(e.tensor_reduce(red1[:, :], ysb[:, :], AX.X,
                                                 ALU.max,
                                                 apply_absolute_value=True))
                            e.inc(e.tensor_tensor(acc["mx"][:, :],
                                                  acc["mx"][:, :], red1[:, :],
                                                  ALU.max),
                                  "eV5", ("mx", gq), 1)

            def p1_scl(e):
                def chain(g):
                    e.wait("aV1", ("st", g))
                    e.ss(e.activation(rstdA[g % 2][:, :], vepsA[g % 2][:, :],
                                      ACT.Ln))
                    e.inc(e.activation(rstdA[g % 2][:, :], rstdA[g % 2][:, :],
                                       ACT.Exp, bias=0.0, scale=-0.5),
                          "aS1", ("rstd", g), 1)

                for t in range(NT):
                    chain(t)
                for b in range(n_seq):
                    e.wait("aSp", ("xn", b * NT + NT - 1))
                    for tc in range(NQC):
                        if (b, tc) != (0, 0):
                            i = b * NQC + tc - 1
                            pb2, ptc = divmod(i, NQC)
                            e.wait("bM", ("qkv", pb2 * NT + ptc * 4 + 3))
                        for dc in range(DCH):
                            e.inc(e.dma_start_transpose(
                                out=xnf[:, dc * 512:(dc + 1) * 512],
                                in_=xn_sp[(b % 2) * T + tc * 512:
                                          (b % 2) * T + (tc + 1) * 512,
                                          dc * P:(dc + 1) * P]),
                                "bX", ("xn", b, tc, dc), 16)
                        if b + 1 < n_seq:
                            for tt in range(4):
                                chain((b + 1) * NT + tc * 4 + tt)
                    for qc in range(NQC):
                        for h in range(H):
                            if not (b == 0 and qc == 0 and h == 0):
                                ph = prev_h(b, qc, h)
                                e.wait("dP", ("pv", *ph, ktmax(ph[1])))
                            for kt in range(ktmax(qc) + 1):
                                if kt >= 4 * qc:
                                    e.wait("dM", ("m", b, qc, h, kt))
                                else:
                                    e.wait("dS", ("s", b, qc, h, kt))
                                if kt >= 2:
                                    e.wait("dP", ("pv", b, qc, h, kt - 2))
                                e.inc(e.activation(
                                    et[kt % 2][:, :],
                                    pp[2][:, (kt % 2) * 512:(kt % 2 + 1) * 512],
                                    ACT.Exp), "dE", ("e", b, qc, h, kt), 1)
                        for qt in range(4):
                            gq = (b * NQC + qc) * 4 + qt
                            e.wait("eV1", ("st3", gq))
                            e.ss(e.activation(rstd3[:, :], veps3[:, :],
                                              ACT.Ln))
                            e.inc(e.activation(rstd3[:, :], rstd3[:, :],
                                               ACT.Exp, bias=0.0, scale=-0.5),
                                  "eS1", ("r3", gq), 1)
                            e.wait("eV3", ("x2st", gq))
                            e.ss(e.activation(rstd2[:, :], veps2[:, :],
                                              ACT.Ln))
                            e.inc(e.activation(rstd2[:, :], rstd2[:, :],
                                               ACT.Exp, bias=0.0, scale=-0.5),
                                  "eS3", ("r2", gq), 1)

            def p1_ten(e):
                e.wait("w16", ("w", 3))
                for b in range(n_seq):
                    if b >= 1:
                        e.wait("dE", ("e", b - 1, NQC - 1, H - 1,
                                      ktmax(NQC - 1)))
                        e.wait("dO", ("o", b - 1, NQC - 1, H - 1))
                    for tc in range(NQC):
                        for tt in range(4):
                            ti = b * NT + tc * 4 + tt
                            e.wait("bX", ("xn", b, tc, DCH - 1))
                            if ti >= 1:
                                e.wait("bQ", ("qk", ti - 1))
                                e.wait("bV", ("v", ti - 1))
                            else:
                                e.wait("bV", "ones")
                            mm = None
                            for dc in range(DCH):
                                lt = xnf[:, dc * 512 + tt * P:
                                         dc * 512 + tt * P + P]
                                e.ldweights(lt)
                                for wi in range(3):
                                    for hf in range(2):
                                        mm = e.matmul(
                                            pp[wi][:, hf * 512:(hf + 1) * 512],
                                            lt,
                                            wqkv_sb[:, wi * DCH * 1024 +
                                                    dc * 1024 + hf * 512:
                                                    wi * DCH * 1024 +
                                                    dc * 1024 + (hf + 1) * 512],
                                            start=(dc == 0),
                                            stop=(dc == DCH - 1))
                            e.inc(mm, "bM", ("qkv", ti), 1)
                    e.wait("bV", ("v", b * NT + NT - 1))
                    e.wait("bQ", ("qk", b * NT + NT - 1))
                    for qc in range(NQC):
                        e.wait("dQ", ("q", b, qc, H - 1))
                        for h in range(H):
                            if qc == 0:
                                e.wait("cK", ("k", b, h))
                            if not (b == 0 and qc == 0 and h == 0):
                                ph = prev_h(b, qc, h)
                                e.wait("dO", ("o", *ph))
                                e.wait("dE", ("e", *ph, ktmax(ph[1])))

                            def s_mm(kt):
                                if kt >= 2:
                                    e.wait("dE", ("e", b, qc, h, kt - 2))
                                ks = kfm[:, h * T + kt * P:
                                         h * T + (kt + 1) * P]
                                e.ldweights(ks)
                                e.inc(e.matmul(pp[2][:, (kt % 2) * 512:
                                                     (kt % 2 + 1) * 512],
                                               ks,
                                               qfm[:, h * 512:(h + 1) * 512],
                                               start=True, stop=True),
                                      "dS", ("s", b, qc, h, kt), 1)

                            def pv_mm(kt):
                                e.wait("dE", ("e", b, qc, h, kt))
                                mm = None
                                for qs in range(max(0, kt - 4 * qc), 4):
                                    els = et[kt % 2][:, qs * P:(qs + 1) * P]
                                    e.ldweights(els)
                                    mm = e.matmul(
                                        pp[qs // 2][:, (qs % 2) * 512:
                                                    (qs % 2) * 512 + 129],
                                        els,
                                        vaug[:, kt * 1032 + h * 129:
                                             kt * 1032 + (h + 1) * 129],
                                        start=(kt == 0),
                                        stop=(kt == 4 * qc + qs))
                                e.inc(mm, "dP", ("pv", b, qc, h, kt), 1)

                            K = ktmax(qc) + 1
                            s_mm(0)
                            for kt in range(1, K):
                                s_mm(kt)
                                pv_mm(kt - 1)
                            pv_mm(K - 1)
                        for qt in range(4):
                            gq = (b * NQC + qc) * 4 + qt
                            e.wait("eT", ("o", gq, DCH - 1))
                            if gq >= 1:
                                e.wait("eS2", ("t4", gq - 1))
                            mm = None
                            for c in range(DCH):
                                ofs = o_fm[:, c * P:(c + 1) * P]
                                e.ldweights(ofs)
                                for hf in range(2):
                                    mm = e.matmul(
                                        pp[3][:, hf * 512:(hf + 1) * 512],
                                        ofs,
                                        wo_sb[:, c * 1024 + hf * 512:
                                              c * 1024 + (hf + 1) * 512],
                                        start=(c == 0), stop=(c == DCH - 1))
                            e.inc(mm, "eW", ("wo", gq), 1)

            run_phase(nc, {"sync": p1_sync, "gpsimd": p1_gp, "vector": p1_vec,
                           "scalar": p1_scl, "tensor": p1_ten}, sched, sems)

        p2es = contextlib.ExitStack()

        def SB2(name, shape, dt=F32):
            return p2es.enter_context(nc.sbuf_tensor(name, shape, dt))

        with p2es:
            b1ps = contextlib.ExitStack()
            ppt = b1ps.enter_context(nc.psum_tensor("ppt", [P, 1024], BF16))
            wupT_sb = SB2("wupT_sb", [P, DCH * I3], BF16)
            wgtT_sb = SB2("wgtT_sb", [P, DCH * I3], BF16)
            wst = [SB2(f"wst{i}", [P, D]) for i in range(2)]
            am_t = SB2("am_t", [P, 32])
            sblk = SB2("sblk", [P, 32])
            invb = SB2("invb", [P, 32])
            uq_t = SB2("uq_t", [P, D])
            iq_t = SB2("iq_t", [P, D], I32)
            rq_t = SB2("rq_t", [P, D])
            wdq_t = SB2("wdq_t", [P, D], BF16)
            yf = SB2("yf", [P, DCH * TCH], BF16)
            ux = SB2("ux", [P, TCH], I32)
            rx = SB2("rx", [P, DCH * TCH], BF16)
            upst = SB2("upst", [P, TCH], BF16)
            gtst = SB2("gtst", [P, TCH], BF16)
            redh2 = SB2("redh2", [P, TCH // 2], BF16)

            def b1_gp(e):
                e.wait("eSp", ("x2", NTILE - 1))
                e.wait("eV5", ("mx", NTILE - 1))
                e.ss(e.tensor_reduce(scal1[0:1, 0:1], acc["mx"][:, :],
                                     AX.XYZWC, ALU.max))
                e.inc(e.dma_start(out=cc_in[0][0:1, 0:1],
                                  in_=scal1[0:1, 0:1]), "gd", ("in", 1), 16)
                e.wait("gd", ("in", 1))
                e.inc(e.collective_compute(
                    "AllReduce", ALU.max, replica_groups=groups,
                    ins=[cc_in[0][0:1, 0:1]], outs=[cc_out[0][0:1, 0:1]]),
                    "gc", ("cc", 1), 1)
                e.wait("gc", ("cc", 1))

            def b1_sync(e):
                e.wait("gc", ("cc", 1))
                src = ap_mk(cc_out[0], 0, [[0, P], [1, 1]])
                e.inc(e.dma_start(out=sc["raw"][:, :], in_=src), "cs", ("c", 1), 16)
                for wi, wd in enumerate((wu_d, wg_d)):
                    for ic in range(ICH):
                        j = wi * ICH + ic
                        if j >= 2:
                            e.wait("uV", ("dq", j - 2))
                        e.inc(e.dma_start(out=wst[j % 2][:, :],
                                          in_=wd[ic * P:(ic + 1) * P, :]),
                              "uL", ("w", j), 16)

            def b1_vec(e):
                e.wait("cs", ("c", 1))
                e.ss(e.tensor_scalar(sc["s_x"][:, :], sc["raw"][:, :],
                                     1.0 / QM, 1e-12, ALU.mult, ALU.max))
                e.inc(e.reciprocal(sc["inv_sx"][:, :], sc["s_x"][:, :]),
                      "vs", ("s", 1), 1)
                for j in range(2 * ICH):
                    e.wait("uL", ("w", j))
                    w3 = ap_mk(wst[j % 2].ap(), 0,
                               [[pstep(wst[j % 2]), P], [32, 32], [1, 32]])
                    e.ss(e.tensor_reduce(am_t[:, :], w3, AX.X, ALU.max,
                                         apply_absolute_value=True))
                    e.ss(e.tensor_scalar(sblk[:, :], am_t[:, :], 1.0 / WQM,
                                         1e-12, ALU.mult, ALU.max))
                    e.ss(e.reciprocal(invb[:, :], sblk[:, :]))
                    iv_b = ap_mk(invb.ap(), 0, [[pstep(invb), P], [1, 32], [0, 32]])
                    sb_b = ap_mk(sblk.ap(), 0, [[pstep(sblk), P], [1, 32], [0, 32]])
                    e.tensor_tensor(uq_t[:, :], wst[j % 2][:, :], iv_b, ALU.mult)
                    e.tensor_scalar(iq_t[:, :], uq_t[:, :], 1.0, None,
                                    ALU.mult)
                    if j >= 1:
                        e.wait("uT", ("tr", j - 1))
                    e.inc(e.tensor_tensor(wdq_t[:, :], iq_t[:, :], sb_b,
                                          ALU.mult), "uV", ("dq", j), 1)
                    e.wait("uT", ("tr", j))
                    wT = wupT_sb if j < ICH else wgtT_sb
                    ic = j % ICH
                    dst = ap_mk(wT.ap(), ic * P,
                                [[pstep(wT), P], [ICH * P, DCH], [1, P]])
                    srcp = ap_mk(ppt.ap(), 0,
                                 [[pstep(ppt), P], [P, DCH], [1, P]])
                    e.inc(e.tensor_copy(dst, srcp), "uE", ("ev", j), 1)

            def b1_ten(e):
                e.wait("tab", "id")
                for j in range(2 * ICH):
                    e.wait("uV", ("dq", j))
                    if j >= 1:
                        e.wait("uE", ("ev", j - 1))
                    tr = None
                    for c in range(DCH):
                        tr = e.transpose(ppt[:, c * P:(c + 1) * P],
                                         wdq_t[:, c * P:(c + 1) * P],
                                         ident_sb[:, :])
                    e.inc(tr, "uT", ("tr", j), 1)

            run_phase(nc, {"gpsimd": b1_gp, "sync": b1_sync, "vector": b1_vec,
                           "tensor": b1_ten}, sched, sems)
            b1ps.close()
            pp = [p2es.enter_context(nc.psum_tensor(f"pq{i}", [P, 1024], F32))
                  for i in range(4)]

            def p2_sync(e):
                e.wait("eSp", ("x2", NTOK // P - 1))
                for tc in range(NTC):
                    if tc >= 1:
                        e.wait("pV", ("rx", tc - 1))
                    for dc in range(DCH):
                        e.inc(e.dma_start_transpose(
                            out=yf[:, dc * TCH:(dc + 1) * TCH],
                            in_=y_sp[tc * TCH:(tc + 1) * TCH,
                                     dc * P:(dc + 1) * P]),
                            "pY", ("y", tc, dc), 16)

            def p2_vec(e):
                e.memset(acc["up"][:, :], 0.0)
                e.memset(acc["gt"][:, :], 0.0)
                e.wait("vs", ("s", 1))
                for tc in range(NTC):
                    e.wait("pY", ("y", tc, DCH - 1))
                    for dc in range(DCH):
                        e.tensor_scalar(ux[:, :], yf[:, dc * TCH:(dc + 1) * TCH],
                                        sc["inv_sx"][:, 0:1], None, ALU.mult)
                        ins = e.tensor_scalar(rx[:, dc * TCH:(dc + 1) * TCH],
                                              ux[:, :], 1.0, None,
                                              ALU.mult)
                    e.inc(ins, "pV", ("rx", tc), 1)
                    for ic in range(ICH):
                        for wi in range(2):
                            stg = upst if wi == 0 else gtst
                            a_ = acc["up"] if wi == 0 else acc["gt"]
                            e.wait("pM", ("mm", tc, ic, wi))
                            if tc + ic > 0:
                                e.wait("pSp", ("sp", *prev_icw(tc, ic, wi)))
                            ppa, ppb = (pp[0], pp[1]) if wi == 0 else (pp[2], pp[3])
                            e.tensor_copy(stg[:, 0:min(1024, TCH)], ppa[:, 0:min(1024, TCH)])
                            if TCH > 1024:
                                e.tensor_copy(stg[:, 1024:2048], ppb[:, 0:1024])
                            e.ss(e.tensor_reduce(red1[:, :], stg[:, :], AX.X,
                                                 ALU.max,
                                                 apply_absolute_value=True))
                            e.inc(e.tensor_tensor(a_[:, :], a_[:, :],
                                                  red1[:, :], ALU.max),
                                  "pV", ("ev", tc, ic, wi), 1)

            def p2_gp(e):
                for tc in range(NTC):
                    for ic in range(ICH):
                        for wi in range(2):
                            stg = upst if wi == 0 else gtst
                            spd = upr_sp if wi == 0 else gtr_sp
                            e.wait("pV", ("ev", tc, ic, wi))
                            e.inc(e.dma_start(
                                out=spd[ic * P:(ic + 1) * P,
                                        tc * TCH:(tc + 1) * TCH],
                                in_=stg[:, :]), "pSp", ("sp", tc, ic, wi), 16)

            def p2_ten(e):
                e.wait("uE", ("ev", 2 * ICH - 1))
                for tc in range(NTC):
                    e.wait("pV", ("rx", tc))
                    for ic in range(ICH):
                        for wi in range(2):
                            wT = wupT_sb if wi == 0 else wgtT_sb
                            ppa, ppb = (pp[0], pp[1]) if wi == 0 else (pp[2], pp[3])
                            if tc + ic > 0:
                                e.wait("pV", ("ev", *prev_ic2(tc, ic, wi)))
                            mm = None
                            for dc in range(DCH):
                                lt = wT[:, (dc * ICH + ic) * P:
                                        (dc * ICH + ic + 1) * P]
                                e.ldweights(lt)
                                for sl in range(TSL):
                                    ppx = ppa if sl < 2 else ppb
                                    mm = e.matmul(
                                        ppx[:, (sl % 2) * 512:(sl % 2 + 1) * 512],
                                        lt,
                                        rx[:, dc * TCH + sl * 512:
                                           dc * TCH + (sl + 1) * 512],
                                        start=(dc == 0), stop=(dc == DCH - 1))
                            e.inc(mm, "pM", ("mm", tc, ic, wi), 1)

            def prev_icw(tc, ic, wi):
                i = tc * ICH + ic - 1
                t2, ic2 = divmod(i, ICH)
                return (t2, ic2, wi)

            def prev_ic2(tc, ic, wi):
                i = tc * ICH + ic - 1
                t2, ic2 = divmod(i, ICH)
                return (t2, ic2, wi)

            run_phase(nc, {"sync": p2_sync, "vector": p2_vec, "gpsimd": p2_gp,
                           "tensor": p2_ten}, sched, sems)

        p3es = contextlib.ExitStack()

        def SB3(name, shape, dt=F32):
            return p3es.enter_context(nc.sbuf_tensor(name, shape, dt))

        with p3es:
            wdnT_sb = SB3("wdnT_sb", [P, ICH * 1024], BF16)
            uia = [SB3(f"uia{i}", [P, TCH], I32) for i in range(2)]
            uib = [SB3(f"uib{i}", [P, TCH], I32) for i in range(2)]
            ui_t = uia[0]
            m_t = SB3("m_t", [P, TCH])
            x2t = SB3("x2t", [P, D])
            t4f = SB3("t4f", [P, D])
            outsb = SB3("outsb", [P, D])
            veps4 = SB3("veps4", [P, 1])
            rstd4 = SB3("rstd4", [P, 1]); nmr4 = SB3("nmr4", [P, 1])
            r4t = SB3("r4t", [P, 1])
            st3c = SB3("st3c", [P, 12])
            ag3c = SB3("ag3c", [P, 2])
            b2s = contextlib.ExitStack()
            wst3 = [b2s.enter_context(nc.sbuf_tensor(f"wst3{i}", [P, I3], F32))
                    for i in range(2)]
            am3 = b2s.enter_context(nc.sbuf_tensor("am3", [P, 96], F32))
            sblk3 = b2s.enter_context(nc.sbuf_tensor("sblk3", [P, 96], F32))
            invb3 = b2s.enter_context(nc.sbuf_tensor("invb3", [P, 96], F32))
            uq3 = b2s.enter_context(nc.sbuf_tensor("uq3", [P, I3], F32))
            iq3 = b2s.enter_context(nc.sbuf_tensor("iq3", [P, I3], I32))
            rq3 = b2s.enter_context(nc.sbuf_tensor("rq3", [P, I3], F32))
            wdq3 = b2s.enter_context(nc.sbuf_tensor("wdq3", [P, I3], BF16))
            ppt3 = b2s.enter_context(nc.psum_tensor("ppt3", [P, 1024], BF16))

            def b2_gp(e):
                e.wait("pSp", ("sp", NTC - 1, ICH - 1, 1))
                e.wait("pV", ("ev", NTC - 1, ICH - 1, 1))
                e.tensor_reduce(scal1[0:1, 0:1], acc["up"][:, :], AX.XYZWC,
                                ALU.max)
                e.ss(e.tensor_reduce(scal1[0:1, 1:2], acc["gt"][:, :],
                                     AX.XYZWC, ALU.max))
                e.inc(e.dma_start(out=cc_in[1][0:1, 0:2],
                                  in_=scal1[0:1, 0:2]), "gd", ("in", 2), 16)
                e.wait("gd", ("in", 2))
                e.inc(e.collective_compute(
                    "AllReduce", ALU.max, replica_groups=groups,
                    ins=[cc_in[1][0:1, 0:2]], outs=[cc_out[1][0:1, 0:2]]),
                    "gc", ("cc", 2), 1)
                e.wait("gc", ("cc", 2))

            def b2_sync(e):
                e.wait("gc", ("cc", 2))
                src = ap_mk(cc_out[1], 0, [[0, P], [1, 1]])
                e.inc(e.dma_start(out=sc["raw"][:, :], in_=src), "cs", ("c", 2, "up"), 16)
                src = ap_mk(cc_out[1], 1, [[0, P], [1, 1]])
                e.inc(e.dma_start(out=sc["raw2"][:, :], in_=src), "cs", ("c", 2, "gt"), 16)
                for mc in range(DCH):
                    if mc >= 2:
                        e.wait("dnV", ("dq", mc - 2))
                    e.inc(e.dma_start(out=wst3[mc % 2][:, :],
                                      in_=wd_d[mc * P:(mc + 1) * P, :]),
                          "dnL", ("w", mc), 16)

            def b2_vec(e):
                e.wait("cs", ("c", 2, "gt"))
                e.ss(e.tensor_tensor(sc["raw"][:, :], sc["raw"][:, :],
                                     sc["s_x"][:, :], ALU.mult))
                e.ss(e.tensor_scalar(sc["s_up"][:, :], sc["raw"][:, :],
                                     1.0 / QM, 1e-12, ALU.mult, ALU.max))
                e.ss(e.reciprocal(sc["rec"][:, :], sc["s_up"][:, :]))
                e.ss(e.tensor_tensor(sc["c_up"][:, :], sc["s_x"][:, :],
                                     sc["rec"][:, :], ALU.mult))
                e.ss(e.tensor_tensor(sc["raw2"][:, :], sc["raw2"][:, :],
                                     sc["s_x"][:, :], ALU.mult))
                e.ss(e.tensor_scalar(sc["s_gate"][:, :], sc["raw2"][:, :],
                                     1.0 / QM, 1e-12, ALU.mult, ALU.max))
                e.ss(e.reciprocal(sc["rec"][:, :], sc["s_gate"][:, :]))
                e.inc(e.tensor_tensor(sc["c_gt"][:, :], sc["s_x"][:, :],
                                      sc["rec"][:, :], ALU.mult),
                      "vs", ("s", 2), 1)
                for mc in range(DCH):
                    e.wait("dnL", ("w", mc))
                    w3 = ap_mk(wst3[mc % 2].ap(), 0,
                               [[pstep(wst3[mc % 2]), P], [32, 96], [1, 32]])
                    e.ss(e.tensor_reduce(am3[:, :], w3, AX.X, ALU.max,
                                         apply_absolute_value=True))
                    e.ss(e.tensor_scalar(sblk3[:, :], am3[:, :], 1.0 / WQM,
                                         1e-12, ALU.mult, ALU.max))
                    e.ss(e.reciprocal(invb3[:, :], sblk3[:, :]))
                    iv_b = ap_mk(invb3.ap(), 0, [[pstep(invb3), P], [1, 96], [0, 32]])
                    sb_b = ap_mk(sblk3.ap(), 0, [[pstep(sblk3), P], [1, 96], [0, 32]])
                    e.tensor_tensor(uq3[:, :], wst3[mc % 2][:, :], iv_b, ALU.mult)
                    e.tensor_scalar(iq3[:, :], uq3[:, :], 1.0, None,
                                    ALU.mult)
                    if mc >= 1:
                        e.wait("dnT", ("tr", mc - 1, 2))
                    e.inc(e.tensor_tensor(wdq3[:, :], iq3[:, :], sb_b,
                                          ALU.mult), "dnV", ("dq", mc), 1)
                    for g3 in range(3):
                        e.wait("dnT", ("tr", mc, g3))
                        dst = ap_mk(wdnT_sb.ap(), g3 * DCH * 1024 + mc * P,
                                    [[pstep(wdnT_sb), P], [1024, DCH], [1, P]])
                        srcp = ap_mk(ppt3.ap(), 0,
                                     [[pstep(ppt3), P], [P, DCH], [1, P]])
                        e.inc(e.tensor_copy(dst, srcp), "dnE",
                              ("ev", mc, g3), 1)

            def b2_ten(e):
                for mc in range(DCH):
                    e.wait("dnV", ("dq", mc))
                    for g3 in range(3):
                        if mc + g3 > 0:
                            pm, pg = (mc, g3 - 1) if g3 else (mc - 1, 2)
                            e.wait("dnE", ("ev", pm, pg))
                        tr = None
                        for c in range(DCH):
                            ic = g3 * DCH + c
                            tr = e.transpose(ppt3[:, c * P:(c + 1) * P],
                                             wdq3[:, ic * P:(ic + 1) * P],
                                             ident_sb[:, :])
                        e.inc(tr, "dnT", ("tr", mc, g3), 1)

            run_phase(nc, {"gpsimd": b2_gp, "sync": b2_sync, "vector": b2_vec,
                           "tensor": b2_ten}, sched, sems)
            b2s.close()

            abs_ = contextlib.ExitStack()
            gst = [abs_.enter_context(nc.sbuf_tensor(f"gst{i}", [P, TCH], BF16))
                   for i in range(2)]
            rg_t = abs_.enter_context(nc.sbuf_tensor("rg_t", [P, TCH], F32))
            sg_t = abs_.enter_context(nc.sbuf_tensor("sg_t", [P, TCH], F32))
            rs_t = abs_.enter_context(nc.sbuf_tensor("rs_t", [P, TCH], F32))
            pt_t = abs_.enter_context(nc.sbuf_tensor("pt_t", [P, TCH], BF16))
            ld_p = [abs_.enter_context(nc.sbuf_tensor(f"ld_p{i}", [P, TCH], BF16))
                    for i in range(2)]
            ld_u = [abs_.enter_context(nc.sbuf_tensor(f"ld_u{i}", [P, TCH], BF16))
                    for i in range(2)]
            ru_t = abs_.enter_context(nc.sbuf_tensor("ru_t", [P, TCH], F32))
            qt_t = abs_.enter_context(nc.sbuf_tensor("qt_t", [P, TCH], BF16))

            def a3_sync(e):
                for i in range(ICH * NTC):
                    if i >= 2:
                        e.wait("qI", ("a", i - 2))
                    ic, tc = divmod(i, NTC)
                    e.inc(e.dma_start(out=gst[i % 2][:, :],
                                      in_=gtr_sp[ic * P:(ic + 1) * P,
                                                 tc * TCH:(tc + 1) * TCH]),
                          "qL", ("g", i), 16)

            def a3_vec(e):
                e.memset(acc["g2"][:, :], 0.0)
                for i in range(ICH * NTC):
                    e.wait("qI", ("a", i))
                    e.wait("qI", ("b", i))
                    e.tensor_scalar(rs_t[:, :], uib[i % 2][:, :], 1.0,
                                    U16 - 1.0, ALU.mult, ALU.min)
                    if i >= 1:
                        e.wait("qSp", ("p", i - 1))
                    e.tensor_tensor(pt_t[:, :], uia[i % 2][:, :], rs_t[:, :],
                                    ALU.mult)
                    e.ss(e.tensor_reduce(red1[:, :], pt_t[:, :], AX.X,
                                         ALU.max, apply_absolute_value=True))
                    e.inc(e.tensor_tensor(acc["g2"][:, :], acc["g2"][:, :],
                                          red1[:, :], ALU.max),
                          "qV", ("p", i), 1)

            def a3_scl(e):
                e.wait("vs", ("s", 2))
                for i in range(ICH * NTC):
                    e.wait("qL", ("g", i))
                    if i >= 2:
                        e.wait("qV", ("p", i - 2))
                    e.inc(e.activation(uia[i % 2][:, :], gst[i % 2][:, :],
                                       ACT.Identity, bias=0.0,
                                       scale=sc["c_gt"][:, 0:1]),
                          "qI", ("a", i), 1)
                    e.wait("qI", ("a", i))
                    e.ss(e.activation(sg_t[:, :], uia[i % 2][:, :],
                                      ACT.Sigmoid, bias=0.0,
                                      scale=sc["s_gate"][:, 0:1]))
                    if i >= 2:
                        e.wait("qV", ("p", i - 2))
                    e.inc(e.activation(uib[i % 2][:, :], sg_t[:, :],
                                       ACT.Identity, bias=0.0, scale=U16),
                          "qI", ("b", i), 1)
                    e.wait("qI", ("b", i))

            def a3_gp(e):
                for i in range(ICH * NTC):
                    e.wait("qV", ("p", i))
                    ic, tc = divmod(i, NTC)
                    e.inc(e.dma_start(out=p_sp[ic * P:(ic + 1) * P,
                                               tc * TCH:(tc + 1) * TCH],
                                      in_=pt_t[:, :]), "qSp", ("p", i), 16)

            run_phase(nc, {"sync": a3_sync, "vector": a3_vec, "scalar": a3_scl,
                           "gpsimd": a3_gp}, sched, sems)

            def b3_gp(e):
                e.wait("qSp", ("p", ICH * NTC - 1))
                e.ss(e.tensor_reduce(scal1[0:1, 0:1], acc["g2"][:, :],
                                     AX.XYZWC, ALU.max))
                e.inc(e.dma_start(out=cc_in[2][0:1, 0:1],
                                  in_=scal1[0:1, 0:1]), "gd", ("in", 3), 16)
                e.wait("gd", ("in", 3))
                e.inc(e.collective_compute(
                    "AllReduce", ALU.max, replica_groups=groups,
                    ins=[cc_in[2][0:1, 0:1]], outs=[cc_out[2][0:1, 0:1]]),
                    "gc", ("cc", 3), 1)
                e.wait("gc", ("cc", 3))

            def b3_sync(e):
                e.wait("gc", ("cc", 3))
                src = ap_mk(cc_out[2], 0, [[0, P], [1, 1]])
                e.inc(e.dma_start(out=sc["raw"][:, :], in_=src), "cs", ("c", 3), 16)

            def b3_vec(e):
                e.wait("cs", ("c", 3))
                e.ss(e.tensor_scalar(sc["f_g2"][:, :], sc["s_gate"][:, :],
                                     1.0 / U16, None, ALU.mult))
                e.ss(e.tensor_tensor(sc["raw"][:, :], sc["raw"][:, :],
                                     sc["f_g2"][:, :], ALU.mult))
                e.ss(e.tensor_scalar(sc["s_g2"][:, :], sc["raw"][:, :],
                                     1.0 / QM, 1e-12, ALU.mult, ALU.max))
                e.ss(e.reciprocal(sc["rec"][:, :], sc["s_g2"][:, :]))
                e.inc(e.tensor_tensor(sc["c_g2"][:, :], sc["f_g2"][:, :],
                                      sc["rec"][:, :], ALU.mult),
                      "vs", ("s", 3), 1)

            run_phase(nc, {"gpsimd": b3_gp, "sync": b3_sync, "vector": b3_vec},
                      sched, sems)

            def b_sync(e):
                for i in range(ICH * NTC):
                    if i >= 2:
                        e.wait("rI", ("a", i - 2))
                    ic, tc = divmod(i, NTC)
                    e.inc(e.dma_start(out=ld_p[i % 2][:, :],
                                      in_=p_sp[ic * P:(ic + 1) * P,
                                               tc * TCH:(tc + 1) * TCH]),
                          "rLp", ("p", i), 16)

            def b_scl(e):
                e.wait("vs", ("s", 3))
                for i in range(ICH * NTC):
                    e.wait("rLp", ("p", i))
                    if i >= 2:
                        e.wait("rV", ("q", i - 2))
                    e.inc(e.activation(uia[i % 2][:, :], ld_p[i % 2][:, :],
                                       ACT.Identity, bias=0.0,
                                       scale=sc["c_g2"][:, 0:1]),
                          "rI", ("a", i), 1)
                    e.wait("rLu", ("u", i))
                    e.inc(e.activation(uib[i % 2][:, :], ld_u[i % 2][:, :],
                                       ACT.Identity, bias=0.0,
                                       scale=sc["c_up"][:, 0:1]),
                          "rI", ("b", i), 1)

            def b_vec(e):
                e.memset(acc["o"][:, :], 0.0)
                for i in range(ICH * NTC):
                    e.wait("rI", ("a", i))
                    e.wait("rI", ("b", i))
                    if i >= 1:
                        e.wait("rSp", ("q", i - 1))
                    e.tensor_tensor(qt_t[:, :], uia[i % 2][:, :], uib[i % 2][:, :],
                                    ALU.mult)
                    e.ss(e.tensor_reduce(red1[:, :], qt_t[:, :], AX.X,
                                         ALU.max, apply_absolute_value=True))
                    e.inc(e.tensor_tensor(acc["o"][:, :], acc["o"][:, :],
                                          red1[:, :], ALU.max),
                          "rV", ("q", i), 1)

            def b_gp(e):
                N = ICH * NTC
                for i in range(N):
                    ic, tc = divmod(i, NTC)
                    if i >= 2:
                        e.wait("rI", ("b", i - 2))
                    e.inc(e.dma_start(out=ld_u[i % 2][:, :],
                                      in_=upr_sp[ic * P:(ic + 1) * P,
                                                 tc * TCH:(tc + 1) * TCH]),
                          "rLu", ("u", i), 16)
                    if i >= 1:
                        jc, js = divmod(i - 1, NTC)
                        e.wait("rV", ("q", i - 1))
                        e.inc(e.dma_start(out=qr_sp[jc * P:(jc + 1) * P,
                                                   js * TCH:(js + 1) * TCH],
                                          in_=qt_t[:, :]),
                              "rSp", ("q", i - 1), 16)
                e.wait("rV", ("q", N - 1))
                jc, js = divmod(N - 1, NTC)
                e.inc(e.dma_start(out=qr_sp[jc * P:(jc + 1) * P,
                                           js * TCH:(js + 1) * TCH],
                                  in_=qt_t[:, :]), "rSp", ("q", N - 1), 16)

            run_phase(nc, {"sync": b_sync, "scalar": b_scl, "vector": b_vec,
                           "gpsimd": b_gp}, sched, sems)

            def b4_gp(e):
                e.wait("rSp", ("q", ICH * NTC - 1))
                e.ss(e.tensor_reduce(scal1[0:1, 0:1], acc["o"][:, :],
                                     AX.XYZWC, ALU.max))
                e.inc(e.dma_start(out=cc_in[3][0:1, 0:1],
                                  in_=scal1[0:1, 0:1]), "gd", ("in", 4), 16)
                e.wait("gd", ("in", 4))
                e.inc(e.collective_compute(
                    "AllReduce", ALU.max, replica_groups=groups,
                    ins=[cc_in[3][0:1, 0:1]], outs=[cc_out[3][0:1, 0:1]]),
                    "gc", ("cc", 4), 1)
                e.wait("gc", ("cc", 4))

            def b4_sync(e):
                e.wait("gc", ("cc", 4))
                src = ap_mk(cc_out[3], 0, [[0, P], [1, 1]])
                e.inc(e.dma_start(out=sc["raw"][:, :], in_=src), "cs", ("c", 4), 16)

            def b4_vec(e):
                e.wait("cs", ("c", 4))
                e.ss(e.tensor_tensor(sc["f_o"][:, :], sc["s_g2"][:, :],
                                     sc["s_up"][:, :], ALU.mult))
                e.ss(e.tensor_tensor(sc["raw"][:, :], sc["raw"][:, :],
                                     sc["f_o"][:, :], ALU.mult))
                e.ss(e.tensor_scalar(sc["s_o"][:, :], sc["raw"][:, :],
                                     1.0 / QM, 1e-12, ALU.mult, ALU.max))
                e.ss(e.reciprocal(sc["rec"][:, :], sc["s_o"][:, :]))
                e.ss(e.tensor_tensor(sc["c_o"][:, :], sc["f_o"][:, :],
                                     sc["rec"][:, :], ALU.mult))
                e.inc(e.tensor_tensor(sc["so2"][:, :], sc["s_o"][:, :],
                                      sc["s_o"][:, :], ALU.mult), "vs", ("s", 4), 1)

            run_phase(nc, {"gpsimd": b4_gp, "sync": b4_sync, "vector": b4_vec},
                      sched, sems)
            abs_.close()

            csc = contextlib.ExitStack()
            qst = csc.enter_context(
                nc.sbuf_tensor("qst", [P, ICH * TCH], BF16))
            ppc = [csc.enter_context(nc.psum_tensor(f"ppc{i}", [P, 1024], F32))
                   for i in range(2)]
            NTT = TCH // P

            def c_sync(e):
                for tc in range(NTC):
                    for ic in range(ICH):
                        if tc >= 1:
                            e.wait("fM", ("mm", tc * NTT - 1))
                        lin = tc * ICH + ic
                        if lin >= 2:
                            e.wait("fV0", ("ro", *divmod(lin - 2, ICH)))
                        e.inc(e.dma_start(
                            out=qst[:, ic * TCH:(ic + 1) * TCH],
                            in_=qr_sp[ic * P:(ic + 1) * P,
                                      tc * TCH:(tc + 1) * TCH]),
                            "fL", ("q", tc, ic), 16)
                    for tt in range(NTT):
                        gt_ = tc * NTT + tt
                        if gt_ >= 1:
                            e.wait("fV3", ("out", gt_ - 1))
                        r0 = tc * TCH + tt * P
                        e.inc(e.dma_start(out=x2t[:, :],
                                          in_=x2_sp[r0:r0 + P, :]),
                              "fX", ("x2", gt_), 16)

            def c_vec(e):
                e.wait("vs", ("s", 4))
                for tc in range(NTC):
                    for ic in range(ICH):
                        e.wait("fL", ("q", tc, ic))
                        e.tensor_scalar(ui_t[:, :],
                                        qst[:, ic * TCH:(ic + 1) * TCH],
                                        sc["c_o"][:, 0:1], None, ALU.mult)
                        e.inc(e.tensor_scalar(
                            qst[:, ic * TCH:(ic + 1) * TCH], ui_t[:, :], 1.0,
                            None, ALU.mult),
                            "fV0", ("ro", tc, ic), 1)
                    for tt in range(NTT):
                        gt_ = tc * NTT + tt
                        e.wait("fM", ("mm", gt_))
                        e.bn_stats(st3c[:, 0:6], ppc[gt_ % 2][:, 0:512])
                        e.ss(e.bn_stats(st3c[:, 6:12], ppc[gt_ % 2][:, 512:1024]))
                        e.ss(e.bn_aggr(ag3c[:, :], st3c[:, :]))
                        e.inc(e.tensor_scalar(veps4[:, :], ag3c[:, 1:2],
                                              sc["so2"][:, 0:1], EPS, ALU.mult,
                                              ALU.add), "fV1", ("st", gt_), 1)
                        e.wait("fS1", ("r4", gt_))
                        e.ss(e.tensor_tensor(rstd4[:, :], r4t[:, :],
                                             sc["s_o"][:, :], ALU.mult))
                        e.inc(e.scalar_tensor_tensor(
                            nmr4[:, :], ag3c[:, 0:1], -1.0, rstd4[:, :],
                            ALU.mult, ALU.mult), "fV2", ("nmr", gt_), 1)
                        e.wait("fS2", ("t4", gt_))
                        e.wait("fX", ("x2", gt_))
                        if gt_ >= 1:
                            e.wait("fO", ("st", gt_ - 1))
                        e.inc(e.tensor_tensor(outsb[:, :], t4f[:, :],
                                              x2t[:, :], ALU.add),
                              "fV3", ("out", gt_), 1)

            def c_scl(e):
                for tc in range(NTC):
                    for tt in range(NTT):
                        gt_ = tc * NTT + tt
                        e.wait("fV1", ("st", gt_))
                        e.ss(e.activation(r4t[:, :], veps4[:, :], ACT.Ln))
                        e.inc(e.activation(r4t[:, :], r4t[:, :], ACT.Exp,
                                           bias=0.0, scale=-0.5),
                              "fS1", ("r4", gt_), 1)
                        e.wait("fV2", ("nmr", gt_))
                        e.inc(e.activation(t4f[:, :], ppc[gt_ % 2][:, :],
                                           ACT.Identity, bias=nmr4[:, :],
                                           scale=rstd4[:, :]),
                              "fS2", ("t4", gt_), 1)

            def c_ten(e):
                e.wait("dnE", ("ev", DCH - 1, 2))
                for tc in range(NTC):
                    for tt in range(NTT):
                        gt_ = tc * NTT + tt
                        if gt_ >= 2:
                            e.wait("fS2", ("t4", gt_ - 2))
                        mm = None
                        for ic in range(ICH):
                            if tt == 0:
                                e.wait("fV0", ("ro", tc, ic))
                            lt = qst[:, ic * TCH + tt * P:ic * TCH + tt * P + P]
                            e.ldweights(lt)
                            for hf in range(2):
                                mm = e.matmul(
                                    ppc[gt_ % 2][:, hf * 512:(hf + 1) * 512],
                                    lt,
                                    wdnT_sb[:, ic * 1024 + hf * 512:
                                            ic * 1024 + (hf + 1) * 512],
                                    start=(ic == 0), stop=(ic == ICH - 1))
                        e.inc(mm, "fM", ("mm", gt_), 1)

            def c_gp(e):
                for tc in range(NTC):
                    for tt in range(NTT):
                        gt_ = tc * NTT + tt
                        e.wait("fV3", ("out", gt_))
                        r0 = tc * TCH + tt * P
                        e.inc(e.dma_start(out=out_d[r0:r0 + P, :],
                                          in_=outsb[:, :]),
                              "fO", ("st", gt_), 16)
                e.wait_n("fO", 16 * NTILE)

            run_phase(nc, {"sync": c_sync, "vector": c_vec, "scalar": c_scl,
                           "tensor": c_ten, "gpsimd": c_gp}, sched, sems)
            csc.close()

    return nc


def gq_prev(b, qc, NQC):
    gq0 = (b * NQC + qc) * 4
    return gq0 - 1 if gq0 >= 1 else None


def make_tables(T):
    NT = T // P
    pos = np.arange(T, dtype=np.float64)
    dims = np.arange(32, dtype=np.float64)
    ang = np.power(10000.0, -dims / 32.0)
    rad = pos[:, None] * ang[None, :]
    cos = np.cos(rad); sin = np.sin(rad)
    cosl = cos.reshape(NT, P, 32).transpose(1, 0, 2).reshape(P, NT * 32)
    sinl = sin.reshape(NT, P, 32).transpose(1, 0, 2).reshape(P, NT * 32)
    k = np.arange(P)[:, None]
    q = np.arange(P)[None, :]
    cm = np.where(q < k, np.float32(-30000.0), np.float32(0.0)).astype(np.float32)
    import ml_dtypes
    idm = np.eye(P, dtype=np.float32).astype(ml_dtypes.bfloat16)
    return (np.ascontiguousarray(cosl.astype(np.float32)),
            np.ascontiguousarray(sinl.astype(np.float32)), cm, idm)


def make_in_map(xc, wq, wk, wv, wo, wg, wu, wd, T):
    cosl, sinl, cm, idm = make_tables(T)
    return {
        "x_tm": np.ascontiguousarray(xc.astype(np.float32)),
        "wq": np.ascontiguousarray(wq.reshape(D, D).astype(np.float32)),
        "wk": np.ascontiguousarray(wk.reshape(D, D).astype(np.float32)),
        "wv": np.ascontiguousarray(wv.reshape(D, D).astype(np.float32)),
        "wo": np.ascontiguousarray(wo.reshape(D, D).astype(np.float32)),
        "w_gate": np.ascontiguousarray(wg.astype(np.float32)),
        "w_up": np.ascontiguousarray(wu.astype(np.float32)),
        "w_down": np.ascontiguousarray(wd.astype(np.float32)),
        "rope_cos": cosl, "rope_sin": sinl, "cmask": cm, "ident": idm,
    }


def kernel(L0_attn_x, L0_attn_wq, L0_attn_wk, L0_attn_wv, L0_attn_wo,
           L0_ffn_w_gate, L0_ffn_w_up, L0_ffn_w_down):
    n_cores = 8
    B, T, _ = L0_attn_x.shape
    n_seq = B // n_cores
    x = np.asarray(L0_attn_x, dtype=np.float32)
    nc = build_program(n_seq=n_seq, T=T, n_cores=n_cores)
    in_maps = []
    for c in range(n_cores):
        xc = x[c * n_seq:(c + 1) * n_seq].reshape(n_seq * T, D)
        in_maps.append(make_in_map(xc, np.asarray(L0_attn_wq),
                                   np.asarray(L0_attn_wk),
                                   np.asarray(L0_attn_wv),
                                   np.asarray(L0_attn_wo),
                                   np.asarray(L0_ffn_w_gate),
                                   np.asarray(L0_ffn_w_up),
                                   np.asarray(L0_ffn_w_down), T))
    res = run_bass_kernel_spmd(nc, in_maps, core_ids=list(range(n_cores)))
    out = np.concatenate([np.asarray(r["out"], dtype=np.float32)
                          for r in res.results], axis=0)
    return out.reshape(B, T, D)


# version 13 (final)
# speedup vs baseline: 1.0304x; 1.0304x over previous
import contextlib
import numpy as np

import concourse.bass as bass
import concourse.mybir as mybir
from concourse.bass_utils import run_bass_kernel_spmd

F32 = mybir.dt.float32
BF16 = mybir.dt.bfloat16
I32 = mybir.dt.int32
ALU = mybir.AluOpType
ACT = mybir.ActivationFunctionType
AX = mybir.AxisListType

P = 128
D = 1024
H = 8
DH = 128
I3 = 3072
DCH = D // P
ICH = I3 // P
DSC = float(DH) ** -0.5
EPS = 1e-5
QM = 32767.0
U16 = 65536.0
WQM = 7.0


STRIPED = {"aL": 2, "aSp": 2, "uL": 2, "dnL": 2, "fL": 2, "pSp": 2,
           "qL": 2, "rLp": 2, "rLu": 2, "cK": 8}


class Sched:
    def __init__(self):
        self.tab = {}
        self.cnt = {}
        self.arr = {}
        self.seq = {}

    def ev(self, sem, key, k):
        r = STRIPED.get(sem, 1)
        a = self.arr.get(sem, 0)
        self.arr[sem] = a + 1
        st = a % r
        c = self.cnt.get((sem, st), 0) + k
        self.cnt[(sem, st)] = c
        assert (sem, key) not in self.tab, f"dup event {sem} {key}"
        self.tab[(sem, key)] = (st, c, k, a)
        self.seq.setdefault(sem, []).append((key, st, c))
        return c

    def waits_for(self, sem, key):
        st, c, k, a = self.tab[(sem, key)]
        if sem not in STRIPED:
            return [(0, c)]
        out = {}
        for i in range(a + 1):
            _, sti, ci = self.seq[sem][i]
            out[sti] = ci
        return sorted(out.items())


class _Dummy:
    def then_inc(self, *a, **k):
        return self


AUTO_SS = [False]
_SSOPS = frozenset((
    "activation", "tensor_tensor", "tensor_scalar", "scalar_tensor_tensor",
    "tensor_reduce", "tensor_copy", "bn_stats", "bn_aggr", "reciprocal",
    "memset", "matmul", "transpose", "ldweights", "iota"))


class Eng:

    def __init__(self, sched, sems, real=None, ename=None):
        self.sched = sched
        self.sems = sems
        self.real = real
        self.dry = real is None
        self.ename = ename

    def _settle(self):
        p = self.__dict__.get("_pend")
        if p is None:
            return
        self.__dict__["_pend"] = None
        ins, inc_info = p
        if inc_info is None:
            sem = self.sems["SS_" + self.ename]
            c = self.sched.cnt["SS_" + self.ename] = (
                self.sched.cnt.get("SS_" + self.ename, 0) + 1)
            ins.then_inc(sem, 1)
            self.real.wait_ge(sem, c)
        else:
            semname, st, cum = inc_info
            self.real.wait_ge(self._sem(semname, st), cum)

    def __getattr__(self, name):
        if self.dry:
            return lambda *a, **k: _Dummy()
        fn = getattr(self.real, name)
        if AUTO_SS[0]:
            def wrap(*a, **k):
                self._settle()
                ins = fn(*a, **k)
                if name in _SSOPS:
                    self.__dict__["_pend"] = (ins, None)
                return ins
            return wrap
        return fn

    def _sem(self, sem, st):
        return self.sems[(sem, st)] if (sem, st) in self.sems else self.sems[sem]

    def inc(self, instr, sem, key, k=1):
        if self.dry:
            self.sched.ev(sem, key, k)
        else:
            st, c, kk, a = self.sched.tab[(sem, key)]
            instr.then_inc(self._sem(sem, st), kk)
            p = self.__dict__.get("_pend")
            if p is not None and p[0] is instr and kk == 1:
                self.__dict__["_pend"] = (instr, (sem, st, c))
        return instr

    def wait(self, sem, key):
        if not self.dry:
            self._settle()
            for st, c in self.sched.waits_for(sem, key):
                self.real.wait_ge(self._sem(sem, st), c)

    def wait_n(self, sem, n):
        if not self.dry:
            self._settle()
            self.real.wait_ge(self.sems[sem], n)

    def ss(self, instr):
        if self.dry or AUTO_SS[0]:
            return instr
        sem = self.sems["SS_" + self.ename]
        c = self.sched.cnt["SS_" + self.ename] = (
            self.sched.cnt.get("SS_" + self.ename, 0) + 1)
        instr.then_inc(sem, 1)
        self.real.wait_ge(sem, c)
        return instr


def ap_mk(t_ap, off_elems, dims):
    return bass.AP(tensor=t_ap.tensor, offset=t_ap.offset + off_elems,
                   ap=[list(d) for d in dims])


def pstep(sb):
    return sb.ap().ap[0][0]


def run_phase(nc, fns, sched, sems):
    for name in ("sync", "gpsimd", "vector", "scalar", "tensor"):
        if name in fns:
            fns[name](Eng(sched, sems, None, name))
    def emit(eng, fn, name):
        e = Eng(sched, sems, eng, name)
        fn(e)
        e._settle()

    with nc.Block() as blk:
        for name in ("sync", "gpsimd", "vector", "scalar", "tensor"):
            if name in fns:
                fn = fns[name]
                getattr(blk, name)(
                    lambda eng, fn=fn, name=name: emit(eng, fn, name))


def build_program(n_seq=4, T=2048, n_cores=8, detect_races=False, dbg=False):
    KI = "ExternalOutput" if dbg else "Internal"
    assert T % 512 == 0
    NTOK = n_seq * T
    NT = T // P
    NQC = T // 512
    TCH = min(2048, NTOK)
    NTC = NTOK // TCH
    TSL = TCH // 512
    NTILE = NTOK // P

    nc = bass.Bass("TRN2", target_bir_lowering=False, debug=False,
                   detect_race_conditions=detect_races)

    x_tm = nc.dram_tensor("x_tm", [NTOK, D], F32, kind="ExternalInput").ap()
    wq_d = nc.dram_tensor("wq", [D, D], F32, kind="ExternalInput").ap()
    wk_d = nc.dram_tensor("wk", [D, D], F32, kind="ExternalInput").ap()
    wv_d = nc.dram_tensor("wv", [D, D], F32, kind="ExternalInput").ap()
    wo_d = nc.dram_tensor("wo", [D, D], F32, kind="ExternalInput").ap()
    wg_d = nc.dram_tensor("w_gate", [I3, D], F32, kind="ExternalInput").ap()
    wu_d = nc.dram_tensor("w_up", [I3, D], F32, kind="ExternalInput").ap()
    wd_d = nc.dram_tensor("w_down", [D, I3], F32, kind="ExternalInput").ap()
    ropeC = nc.dram_tensor("rope_cos", [P, NT * 32], F32, kind="ExternalInput").ap()
    ropeS = nc.dram_tensor("rope_sin", [P, NT * 32], F32, kind="ExternalInput").ap()
    cmask = nc.dram_tensor("cmask", [P, P], F32, kind="ExternalInput").ap()
    ident = nc.dram_tensor("ident", [P, P], BF16, kind="ExternalInput").ap()
    out_d = nc.dram_tensor("out", [NTOK, D], F32, kind="ExternalOutput").ap()

    xn_sp = nc.dram_tensor("xn_sp", [2 * T, D], BF16, kind=KI).ap()
    y_sp = nc.dram_tensor("y_sp", [NTOK, D], BF16, kind=KI).ap()
    x2_sp = nc.dram_tensor("x2_sp", [NTOK, D], F32, kind=KI).ap()
    q_sp = nc.dram_tensor("q_sp", [2 * T, D], BF16, kind=KI).ap()
    k_sp = nc.dram_tensor("k_sp", [2 * T, D], BF16, kind=KI).ap()
    upr_sp = nc.dram_tensor("upr_sp", [I3, NTOK], BF16, kind=KI).ap()
    gtr_sp = nc.dram_tensor("gtr_sp", [I3, NTOK], BF16, kind=KI).ap()
    p_sp = nc.dram_tensor("p_sp", [I3, NTOK], BF16, kind=KI).ap()
    qr_sp = nc.dram_tensor("qr_sp", [I3, NTOK], BF16, kind=KI).ap()
    cc_in = [nc.dram_tensor(f"cc_in{i}", [1, 2], F32, kind="Internal").ap()
             for i in range(4)]
    cc_out = [nc.dram_tensor(f"cc_out{i}", [1, 2], F32, kind="Internal").ap()
              for i in range(4)]
    groups = [list(range(n_cores))]

    sched = Sched()
    es = contextlib.ExitStack()

    def SB(name, shape, dt=F32):
        return es.enter_context(nc.sbuf_tensor(name, shape, dt))

    with es:
        sc = {n: SB("sc_" + n, [P, 1]) for n in
              ("raw", "s_x", "inv_sx", "s_up", "s_gate", "c_up", "c_gt",
               "s_g2", "c_g2", "f_g2", "s_o", "c_o", "f_o", "so2", "raw2",
               "rec")}
        acc = {n: SB("acc_" + n, [P, 1]) for n in ("mx", "up", "gt", "g2", "o")}
        red1 = SB("red1", [P, 1])
        scal1 = SB("scal1", [1, 2])
        ident_sb = SB("ident_sb", [P, P], BF16)

        semnames = (
            "tab", "w16", "aL", "aV1", "aS1", "aV2", "aSx", "aSp",
            "bX", "bM", "bQ", "bV", "bR", "bS",
            "cK", "dQ", "dS", "dM", "dE", "dP", "dZ", "dO",
            "eT", "eW", "eV1", "eS1", "eV2", "eX", "eS2", "eV3", "eS3",
            "eV4", "eS4", "eV5", "eSp",
            "gd", "gc", "cs", "vs", "uL", "uV", "uT", "uE",
            "pY", "pV", "pM", "pE", "pSp",
            "dnL", "dnV", "dnT", "dnE",
            "qL", "qI", "qV", "qG", "qSp",
            "rLp", "rLu", "rI", "rV", "rSp",
            "fL", "fV0", "fM", "fV1", "fS1", "fV2", "fX", "fS2", "fV3", "fO",
            "SS_vector", "SS_scalar", "SS_gpsimd", "SS_tensor",
        )
        sems = {}
        for n in semnames:
            if n in STRIPED:
                for p in range(STRIPED[n]):
                    sems[(n, p)] = es.enter_context(nc.semaphore(f"{n}_{p}"))
            else:
                sems[n] = es.enter_context(nc.semaphore(n))

        def ktmax(qc):
            return 4 * qc + 3

        def prev_h(b, qc, h):
            i = (b * NQC + qc) * H + h - 1
            bq, ph = divmod(i, H)
            pb, pqc = divmod(bq, NQC)
            return (pb, pqc, ph)

        p1es = contextlib.ExitStack()

        def SB1(name, shape, dt=F32):
            return p1es.enter_context(nc.sbuf_tensor(name, shape, dt))

        with p1es:
            pp = [p1es.enter_context(nc.psum_tensor(f"pp{i}", [P, 1024], F32))
                  for i in range(4)]
            wqkv_sb = SB1("wqkv_sb", [P, 3 * DCH * 1024], BF16)
            wo_sb = SB1("wo_sb", [P, DCH * 1024], BF16)
            ropC_sb = SB1("ropC_sb", [P, NT * 32])
            ropS_sb = SB1("ropS_sb", [P, NT * 32])
            cmask_sb = SB1("cmask_sb", [P, P])
            xa = [SB1(f"xa{i}", [P, D]) for i in range(2)]
            st6 = SB1("st6", [P, 12])
            ag2e = SB1("ag2e", [P, 2])
            vepsA = [SB1(f"vepsA{i}", [P, 1]) for i in range(2)]
            rstdA = [SB1(f"rstdA{i}", [P, 1]) for i in range(2)]
            ag2a = [SB1(f"ag2a{i}", [P, 2]) for i in range(2)]
            nmrA = SB1("nmrA", [P, 1])
            veps3 = SB1("veps3", [P, 1])
            rstd3 = SB1("rstd3", [P, 1]); nmr3 = SB1("nmr3", [P, 1])
            veps2 = SB1("veps2", [P, 1])
            rstd2 = SB1("rstd2", [P, 1]); nmr2 = SB1("nmr2", [P, 1])
            xn_tm = [SB1(f"xn_tm{i}", [P, D], BF16) for i in range(2)]
            xnf = SB1("xnf", [P, DCH * 512], BF16)
            q_tm = SB1("q_tm", [P, D], BF16)
            k_tm = SB1("k_tm", [P, D], BF16)
            rsc = SB1("rsc", [P, 1024], BF16)
            vaug = SB1("vaug", [P, NT * 1032], BF16)
            kfm = SB1("kfm", [P, H * T], BF16)
            qfm = SB1("qfm", [P, H * 512], BF16)
            et = [SB1(f"et{i}", [P, 512], BF16) for i in range(2)]
            zrec = SB1("zrec", [P, 4])
            o_tm = SB1("o_tm", [P, 4 * 1024], BF16)
            o_fm = SB1("o_fm", [P, 1024], BF16)
            xe = SB1("xe", [P, D])
            t4sb = SB1("t4sb", [P, D])
            x2sb = SB1("x2sb", [P, D])
            ysb = SB1("ysb", [P, D], BF16)

            def p1_sync(e):
                e.inc(e.dma_start(out=ropC_sb[:, :], in_=ropeC), "tab", "rc", 16)
                e.inc(e.dma_start(out=ropS_sb[:, :], in_=ropeS), "tab", "rs", 16)
                e.inc(e.dma_start(out=cmask_sb[:, :], in_=cmask), "tab", "cm", 16)
                e.inc(e.dma_start(out=ident_sb[:, :], in_=ident), "tab", "id", 16)
                def xload(g):
                    if g >= 2:
                        e.wait("aSx", ("xn", g - 2))
                    e.inc(e.dma_start(out=xa[g % 2][:, :],
                                      in_=x_tm[g * P:(g + 1) * P, :]),
                          "aL", ("x", g), 16)

                for t in range(NT):
                    xload(t)
                for b in range(n_seq):
                    if b + 1 < n_seq:
                        for t in range(NT):
                            xload((b + 1) * NT + t)
                    e.wait("bS", ("k", b, NT - 1))
                    if b >= 1:
                        e.wait("dS", ("s", b - 1, NQC - 1, H - 1,
                                      ktmax(NQC - 1)))
                    for h in range(H):
                        e.inc(e.dma_start_transpose(
                            out=kfm[:, h * T:(h + 1) * T],
                            in_=k_sp[(b % 2) * T:(b % 2) * T + T,
                                     h * DH:(h + 1) * DH]),
                            "cK", ("k", b, h), 16)
                    for qc in range(NQC):
                        if (b, qc) != (0, 0):
                            i = b * NQC + qc - 1
                            pb, pqc = divmod(i, NQC)
                            e.wait("dS", ("s", pb, pqc, H - 1, ktmax(pqc)))
                        for h in range(H):
                            e.inc(e.dma_start_transpose(
                                out=qfm[:, h * 512:(h + 1) * 512],
                                in_=q_sp[(b % 2) * T + qc * 512:
                                         (b % 2) * T + (qc + 1) * 512,
                                         h * DH:(h + 1) * DH]),
                                "dQ", ("q", b, qc, h), 16)
                        for qt in range(4):
                            gq = (b * NQC + qc) * 4 + qt
                            e.wait("dO", ("o", b, qc, H - 1))
                            if gq >= 1:
                                e.wait("eW", ("wo", gq - 1))
                            for c in range(DCH):
                                e.inc(e.dma_start_transpose(
                                    out=o_fm[:, c * P:(c + 1) * P],
                                    in_=o_tm[:, qt * 1024 + c * P:
                                             qt * 1024 + (c + 1) * P]),
                                    "eT", ("o", gq, c), 16)
                            if gq >= 1:
                                e.wait("eV3", ("x2st", gq - 1))
                            r0 = b * T + qc * 512 + qt * P
                            e.inc(e.dma_start(out=xe[:, :],
                                              in_=x_tm[r0:r0 + P, :]),
                                  "eX", ("x", gq), 16)

            def p1_gp(e):
                for wi, wd in enumerate((wq_d, wk_d, wv_d)):
                    src = ap_mk(wd, 0, [[D, P], [P * D, DCH], [1, 1024]])
                    e.inc(e.dma_start(out=wqkv_sb[:, wi * DCH * 1024:
                                                  (wi + 1) * DCH * 1024],
                                      in_=src), "w16", ("w", wi), 16)
                src = ap_mk(wo_d, 0, [[D, P], [P * D, DCH], [1, 1024]])
                e.inc(e.dma_start(out=wo_sb[:, :], in_=src), "w16", ("w", 3), 16)
                e.inc(e.memset(vaug[:, :], 1.0), "bV", "ones", 1)
                def xnspill(g):
                    bb, t = divmod(g, NT)
                    e.wait("aSx", ("xn", g))
                    e.inc(e.dma_start(
                        out=xn_sp[(bb % 2) * T + t * P:
                                  (bb % 2) * T + (t + 1) * P, :],
                        in_=xn_tm[g % 2][:, :]),
                        "aSp", ("xn", g), 16)

                for t in range(NT):
                    xnspill(t)
                for b in range(n_seq):
                    if b >= 2:
                        e.wait("cK", ("k", b - 2, H - 1))
                        e.wait("dQ", ("q", b - 2, NQC - 1, H - 1))
                    if b >= 1 and b + 1 < n_seq:
                        e.wait("bX", ("xn", b - 1, NQC - 1, DCH - 1))
                    for kt in range(NT):
                        e.wait("bR", ("rope", b, kt))
                        r0 = (b % 2) * T + kt * P
                        e.inc(e.dma_start(out=q_sp[r0:r0 + P, :],
                                          in_=q_tm[:, :]),
                              "bS", ("q", b, kt), 16)
                        e.inc(e.dma_start(out=k_sp[r0:r0 + P, :],
                                          in_=k_tm[:, :]),
                              "bS", ("k", b, kt), 16)
                        if b + 1 < n_seq:
                            xnspill((b + 1) * NT + kt)
                    for qc in range(NQC):
                        for qt in range(4):
                            gq = (b * NQC + qc) * 4 + qt
                            e.wait("eS4", ("y", gq))
                            e.wait("eV5", ("mx", gq))
                            r0 = b * T + qc * 512 + qt * P
                            e.inc(e.dma_start(out=y_sp[r0:r0 + P, :],
                                              in_=ysb[:, :]),
                                  "eSp", ("y", gq), 16)
                            e.inc(e.dma_start(out=x2_sp[r0:r0 + P, :],
                                              in_=x2sb[:, :]),
                                  "eSp", ("x2", gq), 16)

            def p1_vec(e):
                e.memset(acc["mx"][:, :], 0.0)
                e.wait("tab", "id")

                def sa_front(g):
                    e.wait("aL", ("x", g))
                    e.bn_stats(st6[:, 0:6], xa[g % 2][:, 0:512])
                    e.ss(e.bn_stats(st6[:, 6:12], xa[g % 2][:, 512:1024]))
                    e.ss(e.bn_aggr(ag2a[g % 2][:, :], st6[:, :]))
                    e.inc(e.tensor_scalar(vepsA[g % 2][:, :],
                                          ag2a[g % 2][:, 1:2], EPS,
                                          None, ALU.add),
                          "aV1", ("st", g), 1)

                def sa_back(g):
                    e.wait("aS1", ("rstd", g))
                    e.inc(e.scalar_tensor_tensor(
                        nmrA[:, :], ag2a[g % 2][:, 0:1], -1.0,
                        rstdA[g % 2][:, :],
                        ALU.mult, ALU.mult), "aV2", ("nmr", g), 1)
                    e.wait("aV2", ("nmr", g))
                    if g >= 2:
                        e.wait("aSp", ("xn", g - 2))
                    e.inc(e.tensor_scalar(xn_tm[g % 2][:, :], xa[g % 2][:, :],
                                          rstdA[g % 2][:, 0:1], nmrA[:, 0:1],
                                          ALU.mult, ALU.add),
                          "aSx", ("xn", g), 1)

                for t in range(NT):
                    sa_front(t)
                    if t >= 1:
                        sa_back(t - 1)
                sa_back(NT - 1)
                for b in range(n_seq):
                    if b >= 1:
                        e.wait("dP", ("pv", b - 1, NQC - 1, H - 1,
                                      ktmax(NQC - 1)))
                    for kt in range(NT):
                        ti = b * NT + kt
                        e.wait("bM", ("qkv", ti))
                        dst = ap_mk(vaug.ap(), kt * 1032,
                                    [[pstep(vaug), P], [129, H], [1, DH]])
                        srcp = ap_mk(pp[2].ap(), 0,
                                     [[pstep(pp[2]), P], [DH, H], [1, DH]])
                        e.inc(e.tensor_copy(dst, srcp), "bV", ("v", ti), 1)
                        if ti >= 1:
                            e.wait("bS", ("k", b, kt - 1) if kt
                                   else ("k", b - 1, NT - 1))
                        e.tensor_scalar(q_tm[:, :], pp[0][:, :], DSC, None,
                                        ALU.mult)
                        e.inc(e.tensor_scalar(k_tm[:, :], pp[1][:, :], DSC,
                                              None, ALU.mult),
                              "bQ", ("qk", ti), 1)
                        for ten in (q_tm, k_tm):
                            ev = ap_mk(ten.ap(), 0,
                                       [[pstep(ten), P], [DH, H], [1, 32]])
                            od = ap_mk(ten.ap(), 32,
                                       [[pstep(ten), P], [DH, H], [1, 32]])
                            cosap = ap_mk(ropC_sb.ap(), kt * 32,
                                          [[pstep(ropC_sb), P], [0, H], [1, 32]])
                            sinap = ap_mk(ropS_sb.ap(), kt * 32,
                                          [[pstep(ropS_sb), P], [0, H], [1, 32]])
                            t1 = rsc[:, 0:256]; t2 = rsc[:, 256:512]
                            t3 = rsc[:, 512:768]; t4_ = rsc[:, 768:1024]
                            e.tensor_tensor(t1, ev, cosap, ALU.mult)
                            e.tensor_tensor(t2, od, sinap, ALU.mult)
                            e.tensor_tensor(t3, ev, sinap, ALU.mult)
                            e.tensor_tensor(t4_, od, cosap, ALU.mult)
                            e.tensor_tensor(ev, t1, t2, ALU.subtract)
                            ins = e.tensor_tensor(od, t3, t4_, ALU.add)
                        e.inc(ins, "bR", ("rope", b, kt), 1)
                        if b + 1 < n_seq:
                            g2 = (b + 1) * NT + kt
                            sa_front(g2)
                            if kt >= 1:
                                sa_back(g2 - 1)
                    if b + 1 < n_seq:
                        sa_back((b + 1) * NT + NT - 1)
                    for qc in range(NQC):
                        for h in range(H):
                            for kt in range(4 * qc, ktmax(qc) + 1):
                                qs = kt - 4 * qc
                                e.wait("dS", ("s", b, qc, h, kt))
                                pb_ = pp[2][:, (kt % 2) * 512 + qs * P:
                                            (kt % 2) * 512 + (qs + 1) * P]
                                e.inc(e.tensor_tensor(pb_, pb_, cmask_sb[:, :],
                                                      ALU.add),
                                      "dM", ("m", b, qc, h, kt), 1)
                            e.wait("dP", ("pv", b, qc, h, ktmax(qc)))
                            for qs in range(4):
                                psoz = pp[qs // 2][:, (qs % 2) * 512 + 128:
                                                   (qs % 2) * 512 + 129]
                                e.ss(e.reciprocal(zrec[:, qs:qs + 1], psoz))
                            if h == 0 and gq_prev(b, qc, NQC) is not None:
                                e.wait("eT", ("o", gq_prev(b, qc, NQC),
                                              DCH - 1))
                            ins = None
                            for qs in range(4):
                                pso = pp[qs // 2][:, (qs % 2) * 512:
                                                  (qs % 2) * 512 + 128]
                                ins = e.tensor_scalar(
                                    o_tm[:, qs * 1024 + h * DH:
                                         qs * 1024 + (h + 1) * DH],
                                    pso, zrec[:, qs:qs + 1], None, ALU.mult)
                            e.inc(ins, "dO", ("o", b, qc, h), 1)
                        for qt in range(4):
                            gq = (b * NQC + qc) * 4 + qt
                            e.wait("eW", ("wo", gq))
                            e.bn_stats(st6[:, 0:6], pp[3][:, 0:512])
                            e.ss(e.bn_stats(st6[:, 6:12], pp[3][:, 512:1024]))
                            e.ss(e.bn_aggr(ag2e[:, :], st6[:, :]))
                            e.inc(e.tensor_scalar(veps3[:, :], ag2e[:, 1:2],
                                                  EPS, None, ALU.add),
                                  "eV1", ("st3", gq), 1)
                            e.wait("eS1", ("r3", gq))
                            e.inc(e.scalar_tensor_tensor(
                                nmr3[:, :], ag2e[:, 0:1], -1.0, rstd3[:, :],
                                ALU.mult, ALU.mult), "eV2", ("nmr3", gq), 1)
                            e.wait("eV2", ("nmr3", gq))
                            e.inc(e.tensor_scalar(t4sb[:, :], pp[3][:, :],
                                                  rstd3[:, 0:1], nmr3[:, 0:1],
                                                  ALU.mult, ALU.add),
                                  "eS2", ("t4", gq), 1)
                            e.wait("eX", ("x", gq))
                            if gq >= 1:
                                e.wait("eSp", ("x2", gq - 1))
                            e.tensor_tensor(x2sb[:, :], t4sb[:, :], xe[:, :],
                                            ALU.add)
                            e.bn_stats(st6[:, 0:6], x2sb[:, 0:512])
                            e.ss(e.bn_stats(st6[:, 6:12], x2sb[:, 512:1024]))
                            e.ss(e.bn_aggr(ag2e[:, :], st6[:, :]))
                            e.inc(e.tensor_scalar(veps2[:, :], ag2e[:, 1:2],
                                                  EPS, None, ALU.add),
                                  "eV3", ("x2st", gq), 1)
                            e.wait("eS3", ("r2", gq))
                            e.inc(e.scalar_tensor_tensor(
                                nmr2[:, :], ag2e[:, 0:1], -1.0, rstd2[:, :],
                                ALU.mult, ALU.mult), "eV4", ("nmr2", gq), 1)
                            e.wait("eV4", ("nmr2", gq))
                            e.inc(e.tensor_scalar(ysb[:, :], x2sb[:, :],
                                                  rstd2[:, 0:1], nmr2[:, 0:1],
                                                  ALU.mult, ALU.add),
                                  "eS4", ("y", gq), 1)
                            e.ss(e.tensor_reduce(red1[:, :], ysb[:, :], AX.X,
                                                 ALU.max,
                                                 apply_absolute_value=True))
                            e.inc(e.tensor_tensor(acc["mx"][:, :],
                                                  acc["mx"][:, :], red1[:, :],
                                                  ALU.max),
                                  "eV5", ("mx", gq), 1)

            def p1_scl(e):
                def chain(g):
                    e.wait("aV1", ("st", g))
                    e.ss(e.activation(rstdA[g % 2][:, :], vepsA[g % 2][:, :],
                                      ACT.Ln))
                    e.inc(e.activation(rstdA[g % 2][:, :], rstdA[g % 2][:, :],
                                       ACT.Exp, bias=0.0, scale=-0.5),
                          "aS1", ("rstd", g), 1)

                for t in range(NT):
                    chain(t)
                for b in range(n_seq):
                    e.wait("aSp", ("xn", b * NT + NT - 1))
                    for tc in range(NQC):
                        if (b, tc) != (0, 0):
                            i = b * NQC + tc - 1
                            pb2, ptc = divmod(i, NQC)
                            e.wait("bM", ("qkv", pb2 * NT + ptc * 4 + 3))
                        for dc in range(DCH):
                            e.inc(e.dma_start_transpose(
                                out=xnf[:, dc * 512:(dc + 1) * 512],
                                in_=xn_sp[(b % 2) * T + tc * 512:
                                          (b % 2) * T + (tc + 1) * 512,
                                          dc * P:(dc + 1) * P]),
                                "bX", ("xn", b, tc, dc), 16)
                        if b + 1 < n_seq:
                            for tt in range(4):
                                chain((b + 1) * NT + tc * 4 + tt)
                    for qc in range(NQC):
                        for h in range(H):
                            if not (b == 0 and qc == 0 and h == 0):
                                ph = prev_h(b, qc, h)
                                e.wait("dP", ("pv", *ph, ktmax(ph[1])))
                            for kt in range(ktmax(qc) + 1):
                                if kt >= 4 * qc:
                                    e.wait("dM", ("m", b, qc, h, kt))
                                else:
                                    e.wait("dS", ("s", b, qc, h, kt))
                                if kt >= 2:
                                    e.wait("dP", ("pv", b, qc, h, kt - 2))
                                e.inc(e.activation(
                                    et[kt % 2][:, :],
                                    pp[2][:, (kt % 2) * 512:(kt % 2 + 1) * 512],
                                    ACT.Exp), "dE", ("e", b, qc, h, kt), 1)
                        for qt in range(4):
                            gq = (b * NQC + qc) * 4 + qt
                            e.wait("eV1", ("st3", gq))
                            e.ss(e.activation(rstd3[:, :], veps3[:, :],
                                              ACT.Ln))
                            e.inc(e.activation(rstd3[:, :], rstd3[:, :],
                                               ACT.Exp, bias=0.0, scale=-0.5),
                                  "eS1", ("r3", gq), 1)
                            e.wait("eV3", ("x2st", gq))
                            e.ss(e.activation(rstd2[:, :], veps2[:, :],
                                              ACT.Ln))
                            e.inc(e.activation(rstd2[:, :], rstd2[:, :],
                                               ACT.Exp, bias=0.0, scale=-0.5),
                                  "eS3", ("r2", gq), 1)

            def p1_ten(e):
                e.wait("w16", ("w", 3))
                for b in range(n_seq):
                    if b >= 1:
                        e.wait("dE", ("e", b - 1, NQC - 1, H - 1,
                                      ktmax(NQC - 1)))
                        e.wait("dO", ("o", b - 1, NQC - 1, H - 1))
                    for tc in range(NQC):
                        for tt in range(4):
                            ti = b * NT + tc * 4 + tt
                            e.wait("bX", ("xn", b, tc, DCH - 1))
                            if ti >= 1:
                                e.wait("bQ", ("qk", ti - 1))
                                e.wait("bV", ("v", ti - 1))
                            else:
                                e.wait("bV", "ones")
                            mm = None
                            for dc in range(DCH):
                                lt = xnf[:, dc * 512 + tt * P:
                                         dc * 512 + tt * P + P]
                                e.ldweights(lt)
                                for wi in range(3):
                                    for hf in range(2):
                                        mm = e.matmul(
                                            pp[wi][:, hf * 512:(hf + 1) * 512],
                                            lt,
                                            wqkv_sb[:, wi * DCH * 1024 +
                                                    dc * 1024 + hf * 512:
                                                    wi * DCH * 1024 +
                                                    dc * 1024 + (hf + 1) * 512],
                                            start=(dc == 0),
                                            stop=(dc == DCH - 1))
                            e.inc(mm, "bM", ("qkv", ti), 1)
                    e.wait("bV", ("v", b * NT + NT - 1))
                    e.wait("bQ", ("qk", b * NT + NT - 1))
                    for qc in range(NQC):
                        e.wait("dQ", ("q", b, qc, H - 1))
                        for h in range(H):
                            if qc == 0:
                                e.wait("cK", ("k", b, h))
                            if not (b == 0 and qc == 0 and h == 0):
                                ph = prev_h(b, qc, h)
                                e.wait("dO", ("o", *ph))
                                e.wait("dE", ("e", *ph, ktmax(ph[1])))

                            def s_mm(kt):
                                if kt >= 2:
                                    e.wait("dE", ("e", b, qc, h, kt - 2))
                                ks = kfm[:, h * T + kt * P:
                                         h * T + (kt + 1) * P]
                                e.ldweights(ks)
                                e.inc(e.matmul(pp[2][:, (kt % 2) * 512:
                                                     (kt % 2 + 1) * 512],
                                               ks,
                                               qfm[:, h * 512:(h + 1) * 512],
                                               start=True, stop=True),
                                      "dS", ("s", b, qc, h, kt), 1)

                            def pv_mm(kt):
                                e.wait("dE", ("e", b, qc, h, kt))
                                mm = None
                                for qs in range(max(0, kt - 4 * qc), 4):
                                    els = et[kt % 2][:, qs * P:(qs + 1) * P]
                                    e.ldweights(els)
                                    mm = e.matmul(
                                        pp[qs // 2][:, (qs % 2) * 512:
                                                    (qs % 2) * 512 + 129],
                                        els,
                                        vaug[:, kt * 1032 + h * 129:
                                             kt * 1032 + (h + 1) * 129],
                                        start=(kt == 0),
                                        stop=(kt == 4 * qc + qs))
                                e.inc(mm, "dP", ("pv", b, qc, h, kt), 1)

                            K = ktmax(qc) + 1
                            s_mm(0)
                            for kt in range(1, K):
                                s_mm(kt)
                                pv_mm(kt - 1)
                            pv_mm(K - 1)
                        for qt in range(4):
                            gq = (b * NQC + qc) * 4 + qt
                            e.wait("eT", ("o", gq, DCH - 1))
                            if gq >= 1:
                                e.wait("eS2", ("t4", gq - 1))
                            mm = None
                            for c in range(DCH):
                                ofs = o_fm[:, c * P:(c + 1) * P]
                                e.ldweights(ofs)
                                for hf in range(2):
                                    mm = e.matmul(
                                        pp[3][:, hf * 512:(hf + 1) * 512],
                                        ofs,
                                        wo_sb[:, c * 1024 + hf * 512:
                                              c * 1024 + (hf + 1) * 512],
                                        start=(c == 0), stop=(c == DCH - 1))
                            e.inc(mm, "eW", ("wo", gq), 1)

            run_phase(nc, {"sync": p1_sync, "gpsimd": p1_gp, "vector": p1_vec,
                           "scalar": p1_scl, "tensor": p1_ten}, sched, sems)

        p2es = contextlib.ExitStack()

        def SB2(name, shape, dt=F32):
            return p2es.enter_context(nc.sbuf_tensor(name, shape, dt))

        with p2es:
            b1ps = contextlib.ExitStack()
            ppt = b1ps.enter_context(nc.psum_tensor("ppt", [P, 1024], BF16))
            wupT_sb = SB2("wupT_sb", [P, DCH * I3], BF16)
            wgtT_sb = SB2("wgtT_sb", [P, DCH * I3], BF16)
            wst = [SB2(f"wst{i}", [P, D]) for i in range(2)]
            am_t = SB2("am_t", [P, 32])
            sblk = SB2("sblk", [P, 32])
            invb = SB2("invb", [P, 32])
            uq_t = SB2("uq_t", [P, D])
            iq_t = SB2("iq_t", [P, D], I32)
            rq_t = SB2("rq_t", [P, D])
            wdq_t = SB2("wdq_t", [P, D], BF16)
            yf = SB2("yf", [P, DCH * TCH], BF16)
            ux = SB2("ux", [P, TCH], I32)
            rx = SB2("rx", [P, DCH * TCH], BF16)
            upst = SB2("upst", [P, TCH], BF16)
            gtst = SB2("gtst", [P, TCH], BF16)
            redh2 = SB2("redh2", [P, TCH // 2], BF16)

            def b1_gp(e):
                e.wait("eSp", ("x2", NTILE - 1))
                e.wait("eV5", ("mx", NTILE - 1))
                e.ss(e.tensor_reduce(scal1[0:1, 0:1], acc["mx"][:, :],
                                     AX.XYZWC, ALU.max))
                e.inc(e.dma_start(out=cc_in[0][0:1, 0:1],
                                  in_=scal1[0:1, 0:1]), "gd", ("in", 1), 16)
                e.wait("gd", ("in", 1))
                e.inc(e.collective_compute(
                    "AllReduce", ALU.max, replica_groups=groups,
                    ins=[cc_in[0][0:1, 0:1]], outs=[cc_out[0][0:1, 0:1]]),
                    "gc", ("cc", 1), 1)
                e.wait("gc", ("cc", 1))

            def b1_sync(e):
                e.wait("gc", ("cc", 1))
                src = ap_mk(cc_out[0], 0, [[0, P], [1, 1]])
                e.inc(e.dma_start(out=sc["raw"][:, :], in_=src), "cs", ("c", 1), 16)
                for wi, wd in enumerate((wu_d, wg_d)):
                    for ic in range(ICH):
                        j = wi * ICH + ic
                        if j >= 2:
                            e.wait("uV", ("dq", j - 2))
                        e.inc(e.dma_start(out=wst[j % 2][:, :],
                                          in_=wd[ic * P:(ic + 1) * P, :]),
                              "uL", ("w", j), 16)

            def b1_vec(e):
                e.wait("cs", ("c", 1))
                e.ss(e.tensor_scalar(sc["s_x"][:, :], sc["raw"][:, :],
                                     1.0 / QM, 1e-12, ALU.mult, ALU.max))
                e.inc(e.reciprocal(sc["inv_sx"][:, :], sc["s_x"][:, :]),
                      "vs", ("s", 1), 1)
                for j in range(2 * ICH):
                    e.wait("uL", ("w", j))
                    w3 = ap_mk(wst[j % 2].ap(), 0,
                               [[pstep(wst[j % 2]), P], [32, 32], [1, 32]])
                    e.ss(e.tensor_reduce(am_t[:, :], w3, AX.X, ALU.max,
                                         apply_absolute_value=True))
                    e.ss(e.tensor_scalar(sblk[:, :], am_t[:, :], 1.0 / WQM,
                                         1e-12, ALU.mult, ALU.max))
                    e.ss(e.reciprocal(invb[:, :], sblk[:, :]))
                    iv_b = ap_mk(invb.ap(), 0, [[pstep(invb), P], [1, 32], [0, 32]])
                    sb_b = ap_mk(sblk.ap(), 0, [[pstep(sblk), P], [1, 32], [0, 32]])
                    e.tensor_tensor(uq_t[:, :], wst[j % 2][:, :], iv_b, ALU.mult)
                    e.tensor_scalar(iq_t[:, :], uq_t[:, :], 1.0, None,
                                    ALU.mult)
                    if j >= 1:
                        e.wait("uT", ("tr", j - 1))
                    e.inc(e.tensor_tensor(wdq_t[:, :], iq_t[:, :], sb_b,
                                          ALU.mult), "uV", ("dq", j), 1)
                    e.wait("uT", ("tr", j))
                    wT = wupT_sb if j < ICH else wgtT_sb
                    ic = j % ICH
                    dst = ap_mk(wT.ap(), ic * P,
                                [[pstep(wT), P], [ICH * P, DCH], [1, P]])
                    srcp = ap_mk(ppt.ap(), 0,
                                 [[pstep(ppt), P], [P, DCH], [1, P]])
                    e.inc(e.tensor_copy(dst, srcp), "uE", ("ev", j), 1)

            def b1_ten(e):
                e.wait("tab", "id")
                for j in range(2 * ICH):
                    e.wait("uV", ("dq", j))
                    if j >= 1:
                        e.wait("uE", ("ev", j - 1))
                    tr = None
                    for c in range(DCH):
                        tr = e.transpose(ppt[:, c * P:(c + 1) * P],
                                         wdq_t[:, c * P:(c + 1) * P],
                                         ident_sb[:, :])
                    e.inc(tr, "uT", ("tr", j), 1)

            run_phase(nc, {"gpsimd": b1_gp, "sync": b1_sync, "vector": b1_vec,
                           "tensor": b1_ten}, sched, sems)
            b1ps.close()
            pp = [p2es.enter_context(nc.psum_tensor(f"pq{i}", [P, 1024], F32))
                  for i in range(4)]

            def p2_sync(e):
                e.wait("eSp", ("x2", NTOK // P - 1))
                for tc in range(NTC):
                    if tc >= 1:
                        e.wait("pV", ("rx", tc - 1))
                    for dc in range(DCH):
                        e.inc(e.dma_start_transpose(
                            out=yf[:, dc * TCH:(dc + 1) * TCH],
                            in_=y_sp[tc * TCH:(tc + 1) * TCH,
                                     dc * P:(dc + 1) * P]),
                            "pY", ("y", tc, dc), 16)

            def p2_vec(e):
                e.memset(acc["up"][:, :], 0.0)
                e.memset(acc["gt"][:, :], 0.0)
                e.wait("vs", ("s", 1))
                for tc in range(NTC):
                    e.wait("pY", ("y", tc, DCH - 1))
                    for dc in range(DCH):
                        e.tensor_scalar(ux[:, :], yf[:, dc * TCH:(dc + 1) * TCH],
                                        sc["inv_sx"][:, 0:1], None, ALU.mult)
                        ins = e.tensor_scalar(rx[:, dc * TCH:(dc + 1) * TCH],
                                              ux[:, :], 1.0, None,
                                              ALU.mult)
                    e.inc(ins, "pV", ("rx", tc), 1)
                    for ic in range(ICH):
                        for wi in range(2):
                            stg = upst if wi == 0 else gtst
                            a_ = acc["up"] if wi == 0 else acc["gt"]
                            e.wait("pM", ("mm", tc, ic, wi))
                            if tc + ic > 0:
                                e.wait("pSp", ("sp", *prev_icw(tc, ic, wi)))
                            ppa, ppb = (pp[0], pp[1]) if wi == 0 else (pp[2], pp[3])
                            e.tensor_copy(stg[:, 0:min(1024, TCH)], ppa[:, 0:min(1024, TCH)])
                            if TCH > 1024:
                                e.tensor_copy(stg[:, 1024:2048], ppb[:, 0:1024])
                            e.ss(e.tensor_reduce(red1[:, :], stg[:, :], AX.X,
                                                 ALU.max,
                                                 apply_absolute_value=True))
                            e.inc(e.tensor_tensor(a_[:, :], a_[:, :],
                                                  red1[:, :], ALU.max),
                                  "pV", ("ev", tc, ic, wi), 1)

            def p2_gp(e):
                for tc in range(NTC):
                    for ic in range(ICH):
                        for wi in range(2):
                            stg = upst if wi == 0 else gtst
                            spd = upr_sp if wi == 0 else gtr_sp
                            e.wait("pV", ("ev", tc, ic, wi))
                            e.inc(e.dma_start(
                                out=spd[ic * P:(ic + 1) * P,
                                        tc * TCH:(tc + 1) * TCH],
                                in_=stg[:, :]), "pSp", ("sp", tc, ic, wi), 16)

            def p2_ten(e):
                e.wait("uE", ("ev", 2 * ICH - 1))
                for tc in range(NTC):
                    e.wait("pV", ("rx", tc))
                    for ic in range(ICH):
                        for wi in range(2):
                            wT = wupT_sb if wi == 0 else wgtT_sb
                            ppa, ppb = (pp[0], pp[1]) if wi == 0 else (pp[2], pp[3])
                            if tc + ic > 0:
                                e.wait("pV", ("ev", *prev_ic2(tc, ic, wi)))
                            mm = None
                            for dc in range(DCH):
                                lt = wT[:, (dc * ICH + ic) * P:
                                        (dc * ICH + ic + 1) * P]
                                e.ldweights(lt)
                                for sl in range(TSL):
                                    ppx = ppa if sl < 2 else ppb
                                    mm = e.matmul(
                                        ppx[:, (sl % 2) * 512:(sl % 2 + 1) * 512],
                                        lt,
                                        rx[:, dc * TCH + sl * 512:
                                           dc * TCH + (sl + 1) * 512],
                                        start=(dc == 0), stop=(dc == DCH - 1))
                            e.inc(mm, "pM", ("mm", tc, ic, wi), 1)

            def prev_icw(tc, ic, wi):
                i = tc * ICH + ic - 1
                t2, ic2 = divmod(i, ICH)
                return (t2, ic2, wi)

            def prev_ic2(tc, ic, wi):
                i = tc * ICH + ic - 1
                t2, ic2 = divmod(i, ICH)
                return (t2, ic2, wi)

            run_phase(nc, {"sync": p2_sync, "vector": p2_vec, "gpsimd": p2_gp,
                           "tensor": p2_ten}, sched, sems)

        p3es = contextlib.ExitStack()

        def SB3(name, shape, dt=F32):
            return p3es.enter_context(nc.sbuf_tensor(name, shape, dt))

        with p3es:
            wdnT_sb = SB3("wdnT_sb", [P, ICH * 1024], BF16)
            uia = [SB3(f"uia{i}", [P, TCH], I32) for i in range(2)]
            uib = [SB3(f"uib{i}", [P, TCH], I32) for i in range(2)]
            ui_t = uia[0]
            m_t = SB3("m_t", [P, TCH])
            x2t = SB3("x2t", [P, D])
            t4f = SB3("t4f", [P, D])
            outsb = SB3("outsb", [P, D])
            veps4 = SB3("veps4", [P, 1])
            rstd4 = SB3("rstd4", [P, 1]); nmr4 = SB3("nmr4", [P, 1])
            r4t = SB3("r4t", [P, 1])
            st3c = SB3("st3c", [P, 12])
            ag3c = SB3("ag3c", [P, 2])
            b2s = contextlib.ExitStack()
            wst3 = [b2s.enter_context(nc.sbuf_tensor(f"wst3{i}", [P, I3], F32))
                    for i in range(2)]
            am3 = b2s.enter_context(nc.sbuf_tensor("am3", [P, 96], F32))
            sblk3 = b2s.enter_context(nc.sbuf_tensor("sblk3", [P, 96], F32))
            invb3 = b2s.enter_context(nc.sbuf_tensor("invb3", [P, 96], F32))
            uq3 = b2s.enter_context(nc.sbuf_tensor("uq3", [P, I3], F32))
            iq3 = b2s.enter_context(nc.sbuf_tensor("iq3", [P, I3], I32))
            rq3 = b2s.enter_context(nc.sbuf_tensor("rq3", [P, I3], F32))
            wdq3 = b2s.enter_context(nc.sbuf_tensor("wdq3", [P, I3], BF16))
            ppt3 = b2s.enter_context(nc.psum_tensor("ppt3", [P, 1024], BF16))

            def b2_gp(e):
                e.wait("pSp", ("sp", NTC - 1, ICH - 1, 1))
                e.wait("pV", ("ev", NTC - 1, ICH - 1, 1))
                e.tensor_reduce(scal1[0:1, 0:1], acc["up"][:, :], AX.XYZWC,
                                ALU.max)
                e.ss(e.tensor_reduce(scal1[0:1, 1:2], acc["gt"][:, :],
                                     AX.XYZWC, ALU.max))
                e.inc(e.dma_start(out=cc_in[1][0:1, 0:2],
                                  in_=scal1[0:1, 0:2]), "gd", ("in", 2), 16)
                e.wait("gd", ("in", 2))
                e.inc(e.collective_compute(
                    "AllReduce", ALU.max, replica_groups=groups,
                    ins=[cc_in[1][0:1, 0:2]], outs=[cc_out[1][0:1, 0:2]]),
                    "gc", ("cc", 2), 1)
                e.wait("gc", ("cc", 2))

            def b2_sync(e):
                e.wait("gc", ("cc", 2))
                src = ap_mk(cc_out[1], 0, [[0, P], [1, 1]])
                e.inc(e.dma_start(out=sc["raw"][:, :], in_=src), "cs", ("c", 2, "up"), 16)
                src = ap_mk(cc_out[1], 1, [[0, P], [1, 1]])
                e.inc(e.dma_start(out=sc["raw2"][:, :], in_=src), "cs", ("c", 2, "gt"), 16)
                for mc in range(DCH):
                    if mc >= 2:
                        e.wait("dnV", ("dq", mc - 2))
                    e.inc(e.dma_start(out=wst3[mc % 2][:, :],
                                      in_=wd_d[mc * P:(mc + 1) * P, :]),
                          "dnL", ("w", mc), 16)

            def b2_vec(e):
                e.wait("cs", ("c", 2, "gt"))
                e.ss(e.tensor_tensor(sc["raw"][:, :], sc["raw"][:, :],
                                     sc["s_x"][:, :], ALU.mult))
                e.ss(e.tensor_scalar(sc["s_up"][:, :], sc["raw"][:, :],
                                     1.0 / QM, 1e-12, ALU.mult, ALU.max))
                e.ss(e.reciprocal(sc["rec"][:, :], sc["s_up"][:, :]))
                e.ss(e.tensor_tensor(sc["c_up"][:, :], sc["s_x"][:, :],
                                     sc["rec"][:, :], ALU.mult))
                e.ss(e.tensor_tensor(sc["raw2"][:, :], sc["raw2"][:, :],
                                     sc["s_x"][:, :], ALU.mult))
                e.ss(e.tensor_scalar(sc["s_gate"][:, :], sc["raw2"][:, :],
                                     1.0 / QM, 1e-12, ALU.mult, ALU.max))
                e.ss(e.reciprocal(sc["rec"][:, :], sc["s_gate"][:, :]))
                e.inc(e.tensor_tensor(sc["c_gt"][:, :], sc["s_x"][:, :],
                                      sc["rec"][:, :], ALU.mult),
                      "vs", ("s", 2), 1)
                for mc in range(DCH):
                    e.wait("dnL", ("w", mc))
                    w3 = ap_mk(wst3[mc % 2].ap(), 0,
                               [[pstep(wst3[mc % 2]), P], [32, 96], [1, 32]])
                    e.ss(e.tensor_reduce(am3[:, :], w3, AX.X, ALU.max,
                                         apply_absolute_value=True))
                    e.ss(e.tensor_scalar(sblk3[:, :], am3[:, :], 1.0 / WQM,
                                         1e-12, ALU.mult, ALU.max))
                    e.ss(e.reciprocal(invb3[:, :], sblk3[:, :]))
                    iv_b = ap_mk(invb3.ap(), 0, [[pstep(invb3), P], [1, 96], [0, 32]])
                    sb_b = ap_mk(sblk3.ap(), 0, [[pstep(sblk3), P], [1, 96], [0, 32]])
                    e.tensor_tensor(uq3[:, :], wst3[mc % 2][:, :], iv_b, ALU.mult)
                    e.tensor_scalar(iq3[:, :], uq3[:, :], 1.0, None,
                                    ALU.mult)
                    if mc >= 1:
                        e.wait("dnT", ("tr", mc - 1, 2))
                    e.inc(e.tensor_tensor(wdq3[:, :], iq3[:, :], sb_b,
                                          ALU.mult), "dnV", ("dq", mc), 1)
                    for g3 in range(3):
                        e.wait("dnT", ("tr", mc, g3))
                        dst = ap_mk(wdnT_sb.ap(), g3 * DCH * 1024 + mc * P,
                                    [[pstep(wdnT_sb), P], [1024, DCH], [1, P]])
                        srcp = ap_mk(ppt3.ap(), 0,
                                     [[pstep(ppt3), P], [P, DCH], [1, P]])
                        e.inc(e.tensor_copy(dst, srcp), "dnE",
                              ("ev", mc, g3), 1)

            def b2_ten(e):
                for mc in range(DCH):
                    e.wait("dnV", ("dq", mc))
                    for g3 in range(3):
                        if mc + g3 > 0:
                            pm, pg = (mc, g3 - 1) if g3 else (mc - 1, 2)
                            e.wait("dnE", ("ev", pm, pg))
                        tr = None
                        for c in range(DCH):
                            ic = g3 * DCH + c
                            tr = e.transpose(ppt3[:, c * P:(c + 1) * P],
                                             wdq3[:, ic * P:(ic + 1) * P],
                                             ident_sb[:, :])
                        e.inc(tr, "dnT", ("tr", mc, g3), 1)

            run_phase(nc, {"gpsimd": b2_gp, "sync": b2_sync, "vector": b2_vec,
                           "tensor": b2_ten}, sched, sems)
            b2s.close()

            abs_ = contextlib.ExitStack()
            gst = [abs_.enter_context(nc.sbuf_tensor(f"gst{i}", [P, TCH], BF16))
                   for i in range(2)]
            rg_t = None
            sg_t = abs_.enter_context(nc.sbuf_tensor("sg_t", [P, TCH], F32))
            rs_t = abs_.enter_context(nc.sbuf_tensor("rs_t", [P, TCH], F32))
            pt_t = [abs_.enter_context(nc.sbuf_tensor(f"pt_t{i}", [P, TCH], BF16))
                    for i in range(2)]
            ld_p = [abs_.enter_context(nc.sbuf_tensor(f"ld_p{i}", [P, TCH], BF16))
                    for i in range(2)]
            ld_u = [abs_.enter_context(nc.sbuf_tensor(f"ld_u{i}", [P, TCH], BF16))
                    for i in range(2)]
            qt_t = [abs_.enter_context(nc.sbuf_tensor(f"qt_t{i}", [P, TCH], BF16))
                    for i in range(2)]

            def a3_sync(e):
                for i in range(ICH * NTC):
                    if i >= 2:
                        e.wait("qI", ("a", i - 2))
                    ic, tc = divmod(i, NTC)
                    e.inc(e.dma_start(out=gst[i % 2][:, :],
                                      in_=gtr_sp[ic * P:(ic + 1) * P,
                                                 tc * TCH:(tc + 1) * TCH]),
                          "qL", ("g", i), 16)

            def a3_vec(e):
                e.memset(acc["g2"][:, :], 0.0)
                for i in range(ICH * NTC):
                    e.wait("qI", ("a", i))
                    e.wait("qI", ("b", i))
                    e.tensor_scalar(rs_t[:, :], uib[i % 2][:, :], 1.0,
                                    U16 - 1.0, ALU.mult, ALU.min)
                    if i >= 2:
                        e.wait("qSp", ("p", i - 2))
                    e.tensor_tensor(pt_t[i % 2][:, :], uia[i % 2][:, :],
                                    rs_t[:, :], ALU.mult)
                    e.ss(e.tensor_reduce(red1[:, :], pt_t[i % 2][:, :], AX.X,
                                         ALU.max, apply_absolute_value=True))
                    e.inc(e.tensor_tensor(acc["g2"][:, :], acc["g2"][:, :],
                                          red1[:, :], ALU.max),
                          "qV", ("p", i), 1)

            def a3_scl(e):
                e.wait("vs", ("s", 2))
                for i in range(ICH * NTC):
                    e.wait("qL", ("g", i))
                    if i >= 2:
                        e.wait("qV", ("p", i - 2))
                    e.inc(e.activation(uia[i % 2][:, :], gst[i % 2][:, :],
                                       ACT.Identity, bias=0.0,
                                       scale=sc["c_gt"][:, 0:1]),
                          "qI", ("a", i), 1)
                    e.wait("qI", ("a", i))
                    e.ss(e.activation(sg_t[:, :], uia[i % 2][:, :],
                                      ACT.Sigmoid, bias=0.0,
                                      scale=sc["s_gate"][:, 0:1]))
                    if i >= 2:
                        e.wait("qV", ("p", i - 2))
                    e.inc(e.activation(uib[i % 2][:, :], sg_t[:, :],
                                       ACT.Identity, bias=0.0, scale=U16),
                          "qI", ("b", i), 1)
                    e.wait("qI", ("b", i))

            def a3_gp(e):
                for i in range(ICH * NTC):
                    e.wait("qV", ("p", i))
                    ic, tc = divmod(i, NTC)
                    e.inc(e.dma_start(out=p_sp[ic * P:(ic + 1) * P,
                                               tc * TCH:(tc + 1) * TCH],
                                      in_=pt_t[i % 2][:, :]), "qSp", ("p", i), 16)

            run_phase(nc, {"sync": a3_sync, "vector": a3_vec, "scalar": a3_scl,
                           "gpsimd": a3_gp}, sched, sems)

            def b3_gp(e):
                e.wait("qSp", ("p", ICH * NTC - 1))
                e.ss(e.tensor_reduce(scal1[0:1, 0:1], acc["g2"][:, :],
                                     AX.XYZWC, ALU.max))
                e.inc(e.dma_start(out=cc_in[2][0:1, 0:1],
                                  in_=scal1[0:1, 0:1]), "gd", ("in", 3), 16)
                e.wait("gd", ("in", 3))
                e.inc(e.collective_compute(
                    "AllReduce", ALU.max, replica_groups=groups,
                    ins=[cc_in[2][0:1, 0:1]], outs=[cc_out[2][0:1, 0:1]]),
                    "gc", ("cc", 3), 1)
                e.wait("gc", ("cc", 3))

            def b3_sync(e):
                e.wait("gc", ("cc", 3))
                src = ap_mk(cc_out[2], 0, [[0, P], [1, 1]])
                e.inc(e.dma_start(out=sc["raw"][:, :], in_=src), "cs", ("c", 3), 16)

            def b3_vec(e):
                e.wait("cs", ("c", 3))
                e.ss(e.tensor_scalar(sc["f_g2"][:, :], sc["s_gate"][:, :],
                                     1.0 / U16, None, ALU.mult))
                e.ss(e.tensor_tensor(sc["raw"][:, :], sc["raw"][:, :],
                                     sc["f_g2"][:, :], ALU.mult))
                e.ss(e.tensor_scalar(sc["s_g2"][:, :], sc["raw"][:, :],
                                     1.0 / QM, 1e-12, ALU.mult, ALU.max))
                e.ss(e.reciprocal(sc["rec"][:, :], sc["s_g2"][:, :]))
                e.inc(e.tensor_tensor(sc["c_g2"][:, :], sc["f_g2"][:, :],
                                      sc["rec"][:, :], ALU.mult),
                      "vs", ("s", 3), 1)

            run_phase(nc, {"gpsimd": b3_gp, "sync": b3_sync, "vector": b3_vec},
                      sched, sems)

            def b_sync(e):
                for i in range(ICH * NTC):
                    if i >= 2:
                        e.wait("rI", ("a", i - 2))
                    ic, tc = divmod(i, NTC)
                    e.inc(e.dma_start(out=ld_p[i % 2][:, :],
                                      in_=p_sp[ic * P:(ic + 1) * P,
                                               tc * TCH:(tc + 1) * TCH]),
                          "rLp", ("p", i), 16)

            def b_scl(e):
                e.wait("vs", ("s", 3))
                for i in range(ICH * NTC):
                    e.wait("rLp", ("p", i))
                    if i >= 2:
                        e.wait("rV", ("q", i - 2))
                    e.inc(e.activation(uia[i % 2][:, :], ld_p[i % 2][:, :],
                                       ACT.Identity, bias=0.0,
                                       scale=sc["c_g2"][:, 0:1]),
                          "rI", ("a", i), 1)
                    e.wait("rLu", ("u", i))
                    e.inc(e.activation(uib[i % 2][:, :], ld_u[i % 2][:, :],
                                       ACT.Identity, bias=0.0,
                                       scale=sc["c_up"][:, 0:1]),
                          "rI", ("b", i), 1)

            def b_vec(e):
                e.memset(acc["o"][:, :], 0.0)
                for i in range(ICH * NTC):
                    e.wait("rI", ("a", i))
                    e.wait("rI", ("b", i))
                    if i >= 2:
                        e.wait("rSp", ("q", i - 2))
                    e.tensor_tensor(qt_t[i % 2][:, :], uia[i % 2][:, :],
                                    uib[i % 2][:, :], ALU.mult)
                    e.ss(e.tensor_reduce(red1[:, :], qt_t[i % 2][:, :], AX.X,
                                         ALU.max, apply_absolute_value=True))
                    e.inc(e.tensor_tensor(acc["o"][:, :], acc["o"][:, :],
                                          red1[:, :], ALU.max),
                          "rV", ("q", i), 1)

            def b_gp(e):
                N = ICH * NTC
                for i in range(N):
                    ic, tc = divmod(i, NTC)
                    if i >= 2:
                        e.wait("rI", ("b", i - 2))
                    e.inc(e.dma_start(out=ld_u[i % 2][:, :],
                                      in_=upr_sp[ic * P:(ic + 1) * P,
                                                 tc * TCH:(tc + 1) * TCH]),
                          "rLu", ("u", i), 16)
                    if i >= 1:
                        jc, js = divmod(i - 1, NTC)
                        e.wait("rV", ("q", i - 1))
                        e.inc(e.dma_start(out=qr_sp[jc * P:(jc + 1) * P,
                                                   js * TCH:(js + 1) * TCH],
                                          in_=qt_t[(i - 1) % 2][:, :]),
                              "rSp", ("q", i - 1), 16)
                e.wait("rV", ("q", N - 1))
                jc, js = divmod(N - 1, NTC)
                e.inc(e.dma_start(out=qr_sp[jc * P:(jc + 1) * P,
                                           js * TCH:(js + 1) * TCH],
                                  in_=qt_t[(N - 1) % 2][:, :]), "rSp", ("q", N - 1), 16)

            run_phase(nc, {"sync": b_sync, "scalar": b_scl, "vector": b_vec,
                           "gpsimd": b_gp}, sched, sems)

            def b4_gp(e):
                e.wait("rSp", ("q", ICH * NTC - 1))
                e.ss(e.tensor_reduce(scal1[0:1, 0:1], acc["o"][:, :],
                                     AX.XYZWC, ALU.max))
                e.inc(e.dma_start(out=cc_in[3][0:1, 0:1],
                                  in_=scal1[0:1, 0:1]), "gd", ("in", 4), 16)
                e.wait("gd", ("in", 4))
                e.inc(e.collective_compute(
                    "AllReduce", ALU.max, replica_groups=groups,
                    ins=[cc_in[3][0:1, 0:1]], outs=[cc_out[3][0:1, 0:1]]),
                    "gc", ("cc", 4), 1)
                e.wait("gc", ("cc", 4))

            def b4_sync(e):
                e.wait("gc", ("cc", 4))
                src = ap_mk(cc_out[3], 0, [[0, P], [1, 1]])
                e.inc(e.dma_start(out=sc["raw"][:, :], in_=src), "cs", ("c", 4), 16)

            def b4_vec(e):
                e.wait("cs", ("c", 4))
                e.ss(e.tensor_tensor(sc["f_o"][:, :], sc["s_g2"][:, :],
                                     sc["s_up"][:, :], ALU.mult))
                e.ss(e.tensor_tensor(sc["raw"][:, :], sc["raw"][:, :],
                                     sc["f_o"][:, :], ALU.mult))
                e.ss(e.tensor_scalar(sc["s_o"][:, :], sc["raw"][:, :],
                                     1.0 / QM, 1e-12, ALU.mult, ALU.max))
                e.ss(e.reciprocal(sc["rec"][:, :], sc["s_o"][:, :]))
                e.ss(e.tensor_tensor(sc["c_o"][:, :], sc["f_o"][:, :],
                                     sc["rec"][:, :], ALU.mult))
                e.inc(e.tensor_tensor(sc["so2"][:, :], sc["s_o"][:, :],
                                      sc["s_o"][:, :], ALU.mult), "vs", ("s", 4), 1)

            run_phase(nc, {"gpsimd": b4_gp, "sync": b4_sync, "vector": b4_vec},
                      sched, sems)
            abs_.close()

            csc = contextlib.ExitStack()
            qst = csc.enter_context(
                nc.sbuf_tensor("qst", [P, ICH * TCH], BF16))
            ppc = [csc.enter_context(nc.psum_tensor(f"ppc{i}", [P, 1024], F32))
                   for i in range(2)]
            NTT = TCH // P

            def c_sync(e):
                for tc in range(NTC):
                    for ic in range(ICH):
                        if tc >= 1:
                            e.wait("fM", ("mm", tc * NTT - 1))
                        lin = tc * ICH + ic
                        if lin >= 2:
                            e.wait("fV0", ("ro", *divmod(lin - 2, ICH)))
                        e.inc(e.dma_start(
                            out=qst[:, ic * TCH:(ic + 1) * TCH],
                            in_=qr_sp[ic * P:(ic + 1) * P,
                                      tc * TCH:(tc + 1) * TCH]),
                            "fL", ("q", tc, ic), 16)
                    for tt in range(NTT):
                        gt_ = tc * NTT + tt
                        if gt_ >= 1:
                            e.wait("fV3", ("out", gt_ - 1))
                        r0 = tc * TCH + tt * P
                        e.inc(e.dma_start(out=x2t[:, :],
                                          in_=x2_sp[r0:r0 + P, :]),
                              "fX", ("x2", gt_), 16)

            def c_vec(e):
                e.wait("vs", ("s", 4))
                for tc in range(NTC):
                    for ic in range(ICH):
                        e.wait("fL", ("q", tc, ic))
                        e.tensor_scalar(ui_t[:, :],
                                        qst[:, ic * TCH:(ic + 1) * TCH],
                                        sc["c_o"][:, 0:1], None, ALU.mult)
                        e.inc(e.tensor_scalar(
                            qst[:, ic * TCH:(ic + 1) * TCH], ui_t[:, :], 1.0,
                            None, ALU.mult),
                            "fV0", ("ro", tc, ic), 1)
                    for tt in range(NTT):
                        gt_ = tc * NTT + tt
                        e.wait("fM", ("mm", gt_))
                        e.bn_stats(st3c[:, 0:6], ppc[gt_ % 2][:, 0:512])
                        e.ss(e.bn_stats(st3c[:, 6:12], ppc[gt_ % 2][:, 512:1024]))
                        e.ss(e.bn_aggr(ag3c[:, :], st3c[:, :]))
                        e.inc(e.tensor_scalar(veps4[:, :], ag3c[:, 1:2],
                                              sc["so2"][:, 0:1], EPS, ALU.mult,
                                              ALU.add), "fV1", ("st", gt_), 1)
                        e.wait("fS1", ("r4", gt_))
                        e.ss(e.tensor_tensor(rstd4[:, :], r4t[:, :],
                                             sc["s_o"][:, :], ALU.mult))
                        e.inc(e.scalar_tensor_tensor(
                            nmr4[:, :], ag3c[:, 0:1], -1.0, rstd4[:, :],
                            ALU.mult, ALU.mult), "fV2", ("nmr", gt_), 1)
                        e.wait("fS2", ("t4", gt_))
                        e.wait("fX", ("x2", gt_))
                        if gt_ >= 1:
                            e.wait("fO", ("st", gt_ - 1))
                        e.inc(e.tensor_tensor(outsb[:, :], t4f[:, :],
                                              x2t[:, :], ALU.add),
                              "fV3", ("out", gt_), 1)

            def c_scl(e):
                for tc in range(NTC):
                    for tt in range(NTT):
                        gt_ = tc * NTT + tt
                        e.wait("fV1", ("st", gt_))
                        e.ss(e.activation(r4t[:, :], veps4[:, :], ACT.Ln))
                        e.inc(e.activation(r4t[:, :], r4t[:, :], ACT.Exp,
                                           bias=0.0, scale=-0.5),
                              "fS1", ("r4", gt_), 1)
                        e.wait("fV2", ("nmr", gt_))
                        e.inc(e.activation(t4f[:, :], ppc[gt_ % 2][:, :],
                                           ACT.Identity, bias=nmr4[:, :],
                                           scale=rstd4[:, :]),
                              "fS2", ("t4", gt_), 1)

            def c_ten(e):
                e.wait("dnE", ("ev", DCH - 1, 2))
                for tc in range(NTC):
                    for tt in range(NTT):
                        gt_ = tc * NTT + tt
                        if gt_ >= 2:
                            e.wait("fS2", ("t4", gt_ - 2))
                        mm = None
                        for ic in range(ICH):
                            if tt == 0:
                                e.wait("fV0", ("ro", tc, ic))
                            lt = qst[:, ic * TCH + tt * P:ic * TCH + tt * P + P]
                            e.ldweights(lt)
                            for hf in range(2):
                                mm = e.matmul(
                                    ppc[gt_ % 2][:, hf * 512:(hf + 1) * 512],
                                    lt,
                                    wdnT_sb[:, ic * 1024 + hf * 512:
                                            ic * 1024 + (hf + 1) * 512],
                                    start=(ic == 0), stop=(ic == ICH - 1))
                        e.inc(mm, "fM", ("mm", gt_), 1)

            def c_gp(e):
                for tc in range(NTC):
                    for tt in range(NTT):
                        gt_ = tc * NTT + tt
                        e.wait("fV3", ("out", gt_))
                        r0 = tc * TCH + tt * P
                        e.inc(e.dma_start(out=out_d[r0:r0 + P, :],
                                          in_=outsb[:, :]),
                              "fO", ("st", gt_), 16)
                e.wait_n("fO", 16 * NTILE)

            run_phase(nc, {"sync": c_sync, "vector": c_vec, "scalar": c_scl,
                           "tensor": c_ten, "gpsimd": c_gp}, sched, sems)
            csc.close()

    return nc


def gq_prev(b, qc, NQC):
    gq0 = (b * NQC + qc) * 4
    return gq0 - 1 if gq0 >= 1 else None


def make_tables(T):
    NT = T // P
    pos = np.arange(T, dtype=np.float64)
    dims = np.arange(32, dtype=np.float64)
    ang = np.power(10000.0, -dims / 32.0)
    rad = pos[:, None] * ang[None, :]
    cos = np.cos(rad); sin = np.sin(rad)
    cosl = cos.reshape(NT, P, 32).transpose(1, 0, 2).reshape(P, NT * 32)
    sinl = sin.reshape(NT, P, 32).transpose(1, 0, 2).reshape(P, NT * 32)
    k = np.arange(P)[:, None]
    q = np.arange(P)[None, :]
    cm = np.where(q < k, np.float32(-30000.0), np.float32(0.0)).astype(np.float32)
    import ml_dtypes
    idm = np.eye(P, dtype=np.float32).astype(ml_dtypes.bfloat16)
    return (np.ascontiguousarray(cosl.astype(np.float32)),
            np.ascontiguousarray(sinl.astype(np.float32)), cm, idm)


def make_in_map(xc, wq, wk, wv, wo, wg, wu, wd, T):
    cosl, sinl, cm, idm = make_tables(T)
    return {
        "x_tm": np.ascontiguousarray(xc.astype(np.float32)),
        "wq": np.ascontiguousarray(wq.reshape(D, D).astype(np.float32)),
        "wk": np.ascontiguousarray(wk.reshape(D, D).astype(np.float32)),
        "wv": np.ascontiguousarray(wv.reshape(D, D).astype(np.float32)),
        "wo": np.ascontiguousarray(wo.reshape(D, D).astype(np.float32)),
        "w_gate": np.ascontiguousarray(wg.astype(np.float32)),
        "w_up": np.ascontiguousarray(wu.astype(np.float32)),
        "w_down": np.ascontiguousarray(wd.astype(np.float32)),
        "rope_cos": cosl, "rope_sin": sinl, "cmask": cm, "ident": idm,
    }


def kernel(L0_attn_x, L0_attn_wq, L0_attn_wk, L0_attn_wv, L0_attn_wo,
           L0_ffn_w_gate, L0_ffn_w_up, L0_ffn_w_down):
    n_cores = 8
    B, T, _ = L0_attn_x.shape
    n_seq = B // n_cores
    x = np.asarray(L0_attn_x, dtype=np.float32)
    nc = build_program(n_seq=n_seq, T=T, n_cores=n_cores)
    in_maps = []
    for c in range(n_cores):
        xc = x[c * n_seq:(c + 1) * n_seq].reshape(n_seq * T, D)
        in_maps.append(make_in_map(xc, np.asarray(L0_attn_wq),
                                   np.asarray(L0_attn_wk),
                                   np.asarray(L0_attn_wv),
                                   np.asarray(L0_attn_wo),
                                   np.asarray(L0_ffn_w_gate),
                                   np.asarray(L0_ffn_w_up),
                                   np.asarray(L0_ffn_w_down), T))
    res = run_bass_kernel_spmd(nc, in_maps, core_ids=list(range(n_cores)))
    out = np.concatenate([np.asarray(r["out"], dtype=np.float32)
                          for r in res.results], axis=0)
    return out.reshape(B, T, D)


# version 15
# speedup vs baseline: 1.0313x; 1.0009x over previous
import contextlib
import numpy as np

import concourse.bass as bass
import concourse.mybir as mybir
from concourse.bass_utils import run_bass_kernel_spmd

F32 = mybir.dt.float32
BF16 = mybir.dt.bfloat16
I32 = mybir.dt.int32
ALU = mybir.AluOpType
ACT = mybir.ActivationFunctionType
AX = mybir.AxisListType

P = 128
D = 1024
H = 8
DH = 128
I3 = 3072
DCH = D // P
ICH = I3 // P
DSC = float(DH) ** -0.5
EPS = 1e-5
QM = 32767.0
U16 = 65536.0
WQM = 7.0


STRIPED = {"aL": 2, "aSp": 2, "uL": 2, "dnL": 2, "fL": 2, "pSp": 2,
           "qL": 2, "rLp": 2, "rLu": 2, "cK": 8}


class Sched:
    def __init__(self):
        self.tab = {}
        self.cnt = {}
        self.arr = {}
        self.seq = {}

    def ev(self, sem, key, k):
        r = STRIPED.get(sem, 1)
        a = self.arr.get(sem, 0)
        self.arr[sem] = a + 1
        st = a % r
        c = self.cnt.get((sem, st), 0) + k
        self.cnt[(sem, st)] = c
        assert (sem, key) not in self.tab, f"dup event {sem} {key}"
        self.tab[(sem, key)] = (st, c, k, a)
        self.seq.setdefault(sem, []).append((key, st, c))
        return c

    def waits_for(self, sem, key):
        st, c, k, a = self.tab[(sem, key)]
        if sem not in STRIPED:
            return [(0, c)]
        out = {}
        for i in range(a + 1):
            _, sti, ci = self.seq[sem][i]
            out[sti] = ci
        return sorted(out.items())


class _Dummy:
    def then_inc(self, *a, **k):
        return self


AUTO_SS = [False]
_SSOPS = frozenset((
    "activation", "tensor_tensor", "tensor_scalar", "scalar_tensor_tensor",
    "tensor_reduce", "tensor_copy", "bn_stats", "bn_aggr", "reciprocal",
    "memset", "matmul", "transpose", "ldweights", "iota"))


class Eng:

    def __init__(self, sched, sems, real=None, ename=None):
        self.sched = sched
        self.sems = sems
        self.real = real
        self.dry = real is None
        self.ename = ename

    def _settle(self):
        p = self.__dict__.get("_pend")
        if p is None:
            return
        self.__dict__["_pend"] = None
        ins, inc_info = p
        if inc_info is None:
            sem = self.sems["SS_" + self.ename]
            c = self.sched.cnt["SS_" + self.ename] = (
                self.sched.cnt.get("SS_" + self.ename, 0) + 1)
            ins.then_inc(sem, 1)
            self.real.wait_ge(sem, c)
        else:
            semname, st, cum = inc_info
            self.real.wait_ge(self._sem(semname, st), cum)

    def __getattr__(self, name):
        if self.dry:
            return lambda *a, **k: _Dummy()
        fn = getattr(self.real, name)
        if AUTO_SS[0]:
            def wrap(*a, **k):
                self._settle()
                ins = fn(*a, **k)
                if name in _SSOPS:
                    self.__dict__["_pend"] = (ins, None)
                return ins
            return wrap
        return fn

    def _sem(self, sem, st):
        return self.sems[(sem, st)] if (sem, st) in self.sems else self.sems[sem]

    def inc(self, instr, sem, key, k=1):
        if self.dry:
            self.sched.ev(sem, key, k)
        else:
            st, c, kk, a = self.sched.tab[(sem, key)]
            instr.then_inc(self._sem(sem, st), kk)
            p = self.__dict__.get("_pend")
            if p is not None and p[0] is instr and kk == 1:
                self.__dict__["_pend"] = (instr, (sem, st, c))
        return instr

    def wait(self, sem, key):
        if not self.dry:
            self._settle()
            for st, c in self.sched.waits_for(sem, key):
                self.real.wait_ge(self._sem(sem, st), c)

    def wait_n(self, sem, n):
        if not self.dry:
            self._settle()
            self.real.wait_ge(self.sems[sem], n)

    def ss(self, instr):
        if self.dry or AUTO_SS[0]:
            return instr
        sem = self.sems["SS_" + self.ename]
        c = self.sched.cnt["SS_" + self.ename] = (
            self.sched.cnt.get("SS_" + self.ename, 0) + 1)
        instr.then_inc(sem, 1)
        self.real.wait_ge(sem, c)
        return instr


def ap_mk(t_ap, off_elems, dims):
    return bass.AP(tensor=t_ap.tensor, offset=t_ap.offset + off_elems,
                   ap=[list(d) for d in dims])


def pstep(sb):
    return sb.ap().ap[0][0]


def run_phase(nc, fns, sched, sems):
    for name in ("sync", "gpsimd", "vector", "scalar", "tensor"):
        if name in fns:
            fns[name](Eng(sched, sems, None, name))
    def emit(eng, fn, name):
        e = Eng(sched, sems, eng, name)
        fn(e)
        e._settle()

    with nc.Block() as blk:
        for name in ("sync", "gpsimd", "vector", "scalar", "tensor"):
            if name in fns:
                fn = fns[name]
                getattr(blk, name)(
                    lambda eng, fn=fn, name=name: emit(eng, fn, name))


def build_program(n_seq=4, T=2048, n_cores=8, detect_races=False, dbg=False):
    KI = "ExternalOutput" if dbg else "Internal"
    assert T % 512 == 0
    NTOK = n_seq * T
    NT = T // P
    NQC = T // 512
    TCH = min(2048, NTOK)
    NTC = NTOK // TCH
    TSL = TCH // 512
    NTILE = NTOK // P

    nc = bass.Bass("TRN2", target_bir_lowering=False, debug=False,
                   detect_race_conditions=detect_races)

    x_tm = nc.dram_tensor("x_tm", [NTOK, D], F32, kind="ExternalInput").ap()
    wq_d = nc.dram_tensor("wq", [D, D], F32, kind="ExternalInput").ap()
    wk_d = nc.dram_tensor("wk", [D, D], F32, kind="ExternalInput").ap()
    wv_d = nc.dram_tensor("wv", [D, D], F32, kind="ExternalInput").ap()
    wo_d = nc.dram_tensor("wo", [D, D], F32, kind="ExternalInput").ap()
    wg_d = nc.dram_tensor("w_gate", [I3, D], F32, kind="ExternalInput").ap()
    wu_d = nc.dram_tensor("w_up", [I3, D], F32, kind="ExternalInput").ap()
    wd_d = nc.dram_tensor("w_down", [D, I3], F32, kind="ExternalInput").ap()
    ropeC = nc.dram_tensor("rope_cos", [P, NT * 32], F32, kind="ExternalInput").ap()
    ropeS = nc.dram_tensor("rope_sin", [P, NT * 32], F32, kind="ExternalInput").ap()
    cmask = nc.dram_tensor("cmask", [P, P], F32, kind="ExternalInput").ap()
    ident = nc.dram_tensor("ident", [P, P], BF16, kind="ExternalInput").ap()
    out_d = nc.dram_tensor("out", [NTOK, D], F32, kind="ExternalOutput").ap()

    xn_sp = nc.dram_tensor("xn_sp", [2 * T, D], BF16, kind=KI).ap()
    y_sp = nc.dram_tensor("y_sp", [NTOK, D], BF16, kind=KI).ap()
    x2_sp = nc.dram_tensor("x2_sp", [NTOK, D], F32, kind=KI).ap()
    q_sp = nc.dram_tensor("q_sp", [2 * T, D], BF16, kind=KI).ap()
    k_sp = nc.dram_tensor("k_sp", [2 * T, D], BF16, kind=KI).ap()
    upr_sp = nc.dram_tensor("upr_sp", [I3, NTOK], BF16, kind=KI).ap()
    gtr_sp = nc.dram_tensor("gtr_sp", [I3, NTOK], BF16, kind=KI).ap()
    p_sp = nc.dram_tensor("p_sp", [I3, NTOK], BF16, kind=KI).ap()
    qr_sp = nc.dram_tensor("qr_sp", [I3, NTOK], BF16, kind=KI).ap()
    cc_in = [nc.dram_tensor(f"cc_in{i}", [1, 2], F32, kind="Internal").ap()
             for i in range(4)]
    cc_out = [nc.dram_tensor(f"cc_out{i}", [1, 2], F32, kind="Internal").ap()
              for i in range(4)]
    groups = [list(range(n_cores))]

    sched = Sched()
    es = contextlib.ExitStack()

    def SB(name, shape, dt=F32):
        return es.enter_context(nc.sbuf_tensor(name, shape, dt))

    with es:
        sc = {n: SB("sc_" + n, [P, 1]) for n in
              ("raw", "s_x", "inv_sx", "s_up", "s_gate", "c_up", "c_gt",
               "s_g2", "c_g2", "f_g2", "s_o", "c_o", "f_o", "so2", "raw2",
               "rec")}
        acc = {n: SB("acc_" + n, [P, 1]) for n in ("mx", "up", "gt", "g2", "o")}
        red1 = SB("red1", [P, 1])
        scal1 = SB("scal1", [1, 2])
        ident_sb = SB("ident_sb", [P, P], BF16)

        semnames = (
            "tab", "w16", "aL", "aV1", "aS1", "aV2", "aSx", "aSp",
            "bX", "bM", "bQ", "bV", "bR", "bS",
            "cK", "dQ", "dS", "dM", "dE", "dP", "dZ", "dO",
            "eT", "eW", "eV1", "eS1", "eV2", "eX", "eS2", "eV3", "eS3",
            "eV4", "eS4", "eV5", "eSp",
            "gd", "gc", "cs", "vs", "uL", "uV", "uT", "uE",
            "pY", "pV", "pM", "pE", "pSp",
            "dnL", "dnV", "dnT", "dnE",
            "qL", "qI", "qV", "qG", "qSp",
            "rLp", "rLu", "rI", "rV", "rSp",
            "fL", "fV0", "fM", "fV1", "fS1", "fV2", "fX", "fS2", "fV3", "fO",
            "SS_vector", "SS_scalar", "SS_gpsimd", "SS_tensor",
        )
        sems = {}
        for n in semnames:
            if n in STRIPED:
                for p in range(STRIPED[n]):
                    sems[(n, p)] = es.enter_context(nc.semaphore(f"{n}_{p}"))
            else:
                sems[n] = es.enter_context(nc.semaphore(n))

        def ktmax(qc):
            return 4 * qc + 3

        def prev_h(b, qc, h):
            i = (b * NQC + qc) * H + h - 1
            bq, ph = divmod(i, H)
            pb, pqc = divmod(bq, NQC)
            return (pb, pqc, ph)

        p1es = contextlib.ExitStack()

        def SB1(name, shape, dt=F32):
            return p1es.enter_context(nc.sbuf_tensor(name, shape, dt))

        with p1es:
            pp = [p1es.enter_context(nc.psum_tensor(f"pp{i}", [P, 1024], F32))
                  for i in range(4)]
            wqkv_sb = SB1("wqkv_sb", [P, 3 * DCH * 1024], BF16)
            wo_sb = SB1("wo_sb", [P, DCH * 1024], BF16)
            ropC_sb = SB1("ropC_sb", [P, NT * 32])
            ropS_sb = SB1("ropS_sb", [P, NT * 32])
            cmask_sb = SB1("cmask_sb", [P, P])
            xa = [SB1(f"xa{i}", [P, D]) for i in range(2)]
            st6 = SB1("st6", [P, 12])
            ag2e = SB1("ag2e", [P, 2])
            vepsA = [SB1(f"vepsA{i}", [P, 1]) for i in range(2)]
            rstdA = [SB1(f"rstdA{i}", [P, 1]) for i in range(2)]
            ag2a = [SB1(f"ag2a{i}", [P, 2]) for i in range(2)]
            nmrA = SB1("nmrA", [P, 1])
            veps3 = SB1("veps3", [P, 1])
            rstd3 = SB1("rstd3", [P, 1]); nmr3 = SB1("nmr3", [P, 1])
            veps2 = SB1("veps2", [P, 1])
            rstd2 = SB1("rstd2", [P, 1]); nmr2 = SB1("nmr2", [P, 1])
            xn_tm = [SB1(f"xn_tm{i}", [P, D], BF16) for i in range(2)]
            xnf = SB1("xnf", [P, DCH * 512], BF16)
            q_tm = SB1("q_tm", [P, D], BF16)
            k_tm = SB1("k_tm", [P, D], BF16)
            rsc = SB1("rsc", [P, 1024], BF16)
            vaug = SB1("vaug", [P, NT * 1032], BF16)
            kfm = SB1("kfm", [P, H * T], BF16)
            qfm = SB1("qfm", [P, H * 512], BF16)
            et = [SB1(f"et{i}", [P, 512], BF16) for i in range(2)]
            zrec = SB1("zrec", [P, 4])
            o_tm = SB1("o_tm", [P, 4 * 1024], BF16)
            o_fm = SB1("o_fm", [P, 1024], BF16)
            xe = SB1("xe", [P, D])
            t4sb = SB1("t4sb", [P, D])
            x2sb = SB1("x2sb", [P, D])
            ysb = SB1("ysb", [P, D], BF16)

            def p1_sync(e):
                e.inc(e.dma_start(out=ropC_sb[:, :], in_=ropeC), "tab", "rc", 16)
                e.inc(e.dma_start(out=ropS_sb[:, :], in_=ropeS), "tab", "rs", 16)
                e.inc(e.dma_start(out=cmask_sb[:, :], in_=cmask), "tab", "cm", 16)
                e.inc(e.dma_start(out=ident_sb[:, :], in_=ident), "tab", "id", 16)
                def xload(g):
                    if g >= 2:
                        e.wait("aSx", ("xn", g - 2))
                    e.inc(e.dma_start(out=xa[g % 2][:, :],
                                      in_=x_tm[g * P:(g + 1) * P, :]),
                          "aL", ("x", g), 16)

                for t in range(NT):
                    xload(t)
                for b in range(n_seq):
                    if b + 1 < n_seq:
                        for t in range(NT):
                            xload((b + 1) * NT + t)
                    e.wait("bS", ("k", b, NT - 1))
                    if b >= 1:
                        e.wait("dS", ("s", b - 1, NQC - 1, H - 1,
                                      ktmax(NQC - 1)))
                    for h in range(H):
                        e.inc(e.dma_start_transpose(
                            out=kfm[:, h * T:(h + 1) * T],
                            in_=k_sp[(b % 2) * T:(b % 2) * T + T,
                                     h * DH:(h + 1) * DH]),
                            "cK", ("k", b, h), 16)
                    for qc in range(NQC):
                        if (b, qc) != (0, 0):
                            i = b * NQC + qc - 1
                            pb, pqc = divmod(i, NQC)
                            e.wait("dS", ("s", pb, pqc, H - 1, ktmax(pqc)))
                        for h in range(H):
                            e.inc(e.dma_start_transpose(
                                out=qfm[:, h * 512:(h + 1) * 512],
                                in_=q_sp[(b % 2) * T + qc * 512:
                                         (b % 2) * T + (qc + 1) * 512,
                                         h * DH:(h + 1) * DH]),
                                "dQ", ("q", b, qc, h), 16)
                        for qt in range(4):
                            gq = (b * NQC + qc) * 4 + qt
                            e.wait("dO", ("o", b, qc, H - 1))
                            if gq >= 1:
                                e.wait("eW", ("wo", gq - 1))
                            for c in range(DCH):
                                e.inc(e.dma_start_transpose(
                                    out=o_fm[:, c * P:(c + 1) * P],
                                    in_=o_tm[:, qt * 1024 + c * P:
                                             qt * 1024 + (c + 1) * P]),
                                    "eT", ("o", gq, c), 16)
                            if gq >= 1:
                                e.wait("eV3", ("x2st", gq - 1))
                            r0 = b * T + qc * 512 + qt * P
                            e.inc(e.dma_start(out=xe[:, :],
                                              in_=x_tm[r0:r0 + P, :]),
                                  "eX", ("x", gq), 16)

            def p1_gp(e):
                for wi, wd in enumerate((wq_d, wk_d, wv_d)):
                    src = ap_mk(wd, 0, [[D, P], [P * D, DCH], [1, 1024]])
                    e.inc(e.dma_start(out=wqkv_sb[:, wi * DCH * 1024:
                                                  (wi + 1) * DCH * 1024],
                                      in_=src), "w16", ("w", wi), 16)
                src = ap_mk(wo_d, 0, [[D, P], [P * D, DCH], [1, 1024]])
                e.inc(e.dma_start(out=wo_sb[:, :], in_=src), "w16", ("w", 3), 16)
                e.inc(e.memset(vaug[:, :], 1.0), "bV", "ones", 1)
                def xnspill(g):
                    bb, t = divmod(g, NT)
                    e.wait("aSx", ("xn", g))
                    e.inc(e.dma_start(
                        out=xn_sp[(bb % 2) * T + t * P:
                                  (bb % 2) * T + (t + 1) * P, :],
                        in_=xn_tm[g % 2][:, :]),
                        "aSp", ("xn", g), 16)

                for t in range(NT):
                    xnspill(t)
                for b in range(n_seq):
                    if b >= 2:
                        e.wait("cK", ("k", b - 2, H - 1))
                        e.wait("dQ", ("q", b - 2, NQC - 1, H - 1))
                    if b >= 1 and b + 1 < n_seq:
                        e.wait("bX", ("xn", b - 1, NQC - 1, DCH - 1))
                    for kt in range(NT):
                        e.wait("bR", ("rope", b, kt))
                        r0 = (b % 2) * T + kt * P
                        e.inc(e.dma_start(out=q_sp[r0:r0 + P, :],
                                          in_=q_tm[:, :]),
                              "bS", ("q", b, kt), 16)
                        e.inc(e.dma_start(out=k_sp[r0:r0 + P, :],
                                          in_=k_tm[:, :]),
                              "bS", ("k", b, kt), 16)
                        if b + 1 < n_seq:
                            xnspill((b + 1) * NT + kt)
                    for qc in range(NQC):
                        for qt in range(4):
                            gq = (b * NQC + qc) * 4 + qt
                            e.wait("eS4", ("y", gq))
                            e.wait("eV5", ("mx", gq))
                            r0 = b * T + qc * 512 + qt * P
                            e.inc(e.dma_start(out=y_sp[r0:r0 + P, :],
                                              in_=ysb[:, :]),
                                  "eSp", ("y", gq), 16)
                            e.inc(e.dma_start(out=x2_sp[r0:r0 + P, :],
                                              in_=x2sb[:, :]),
                                  "eSp", ("x2", gq), 16)

            def p1_vec(e):
                e.memset(acc["mx"][:, :], 0.0)
                e.wait("tab", "id")

                def sa_front(g):
                    e.wait("aL", ("x", g))
                    e.bn_stats(st6[:, 0:6], xa[g % 2][:, 0:512])
                    e.ss(e.bn_stats(st6[:, 6:12], xa[g % 2][:, 512:1024]))
                    e.ss(e.bn_aggr(ag2a[g % 2][:, :], st6[:, :]))
                    e.inc(e.tensor_scalar(vepsA[g % 2][:, :],
                                          ag2a[g % 2][:, 1:2], EPS,
                                          None, ALU.add),
                          "aV1", ("st", g), 1)

                def sa_back(g):
                    e.wait("aS1", ("rstd", g))
                    e.inc(e.scalar_tensor_tensor(
                        nmrA[:, :], ag2a[g % 2][:, 0:1], -1.0,
                        rstdA[g % 2][:, :],
                        ALU.mult, ALU.mult), "aV2", ("nmr", g), 1)
                    e.wait("aV2", ("nmr", g))
                    if g >= 2:
                        e.wait("aSp", ("xn", g - 2))
                    e.inc(e.tensor_scalar(xn_tm[g % 2][:, :], xa[g % 2][:, :],
                                          rstdA[g % 2][:, 0:1], nmrA[:, 0:1],
                                          ALU.mult, ALU.add),
                          "aSx", ("xn", g), 1)

                for t in range(NT):
                    sa_front(t)
                    if t >= 1:
                        sa_back(t - 1)
                sa_back(NT - 1)
                for b in range(n_seq):
                    if b >= 1:
                        e.wait("dP", ("pv", b - 1, NQC - 1, H - 1,
                                      ktmax(NQC - 1)))
                    for kt in range(NT):
                        ti = b * NT + kt
                        e.wait("bM", ("qkv", ti))
                        dst = ap_mk(vaug.ap(), kt * 1032,
                                    [[pstep(vaug), P], [129, H], [1, DH]])
                        srcp = ap_mk(pp[2].ap(), 0,
                                     [[pstep(pp[2]), P], [DH, H], [1, DH]])
                        e.inc(e.tensor_copy(dst, srcp), "bV", ("v", ti), 1)
                        if ti >= 1:
                            e.wait("bS", ("k", b, kt - 1) if kt
                                   else ("k", b - 1, NT - 1))
                        e.tensor_scalar(q_tm[:, :], pp[0][:, :], DSC, None,
                                        ALU.mult)
                        e.inc(e.tensor_scalar(k_tm[:, :], pp[1][:, :], DSC,
                                              None, ALU.mult),
                              "bQ", ("qk", ti), 1)
                        for ten in (q_tm, k_tm):
                            ev = ap_mk(ten.ap(), 0,
                                       [[pstep(ten), P], [DH, H], [1, 32]])
                            od = ap_mk(ten.ap(), 32,
                                       [[pstep(ten), P], [DH, H], [1, 32]])
                            cosap = ap_mk(ropC_sb.ap(), kt * 32,
                                          [[pstep(ropC_sb), P], [0, H], [1, 32]])
                            sinap = ap_mk(ropS_sb.ap(), kt * 32,
                                          [[pstep(ropS_sb), P], [0, H], [1, 32]])
                            t1 = rsc[:, 0:256]; t2 = rsc[:, 256:512]
                            t3 = rsc[:, 512:768]; t4_ = rsc[:, 768:1024]
                            e.tensor_tensor(t1, ev, cosap, ALU.mult)
                            e.tensor_tensor(t2, od, sinap, ALU.mult)
                            e.tensor_tensor(t3, ev, sinap, ALU.mult)
                            e.tensor_tensor(t4_, od, cosap, ALU.mult)
                            e.tensor_tensor(ev, t1, t2, ALU.subtract)
                            ins = e.tensor_tensor(od, t3, t4_, ALU.add)
                        e.inc(ins, "bR", ("rope", b, kt), 1)
                        if b + 1 < n_seq:
                            g2 = (b + 1) * NT + kt
                            sa_front(g2)
                            if kt >= 1:
                                sa_back(g2 - 1)
                    if b + 1 < n_seq:
                        sa_back((b + 1) * NT + NT - 1)
                    for qc in range(NQC):
                        for h in range(H):
                            for kt in range(4 * qc, ktmax(qc) + 1):
                                qs = kt - 4 * qc
                                e.wait("dS", ("s", b, qc, h, kt))
                                pb_ = pp[2][:, (kt % 2) * 512 + qs * P:
                                            (kt % 2) * 512 + (qs + 1) * P]
                                e.inc(e.tensor_tensor(pb_, pb_, cmask_sb[:, :],
                                                      ALU.add),
                                      "dM", ("m", b, qc, h, kt), 1)
                            e.wait("dP", ("pv", b, qc, h, ktmax(qc)))
                            for qs in range(4):
                                psoz = pp[qs // 2][:, (qs % 2) * 512 + 128:
                                                   (qs % 2) * 512 + 129]
                                e.ss(e.reciprocal(zrec[:, qs:qs + 1], psoz))
                            if h == 0 and gq_prev(b, qc, NQC) is not None:
                                e.wait("eT", ("o", gq_prev(b, qc, NQC),
                                              DCH - 1))
                            ins = None
                            for qs in range(4):
                                pso = pp[qs // 2][:, (qs % 2) * 512:
                                                  (qs % 2) * 512 + 128]
                                ins = e.tensor_scalar(
                                    o_tm[:, qs * 1024 + h * DH:
                                         qs * 1024 + (h + 1) * DH],
                                    pso, zrec[:, qs:qs + 1], None, ALU.mult)
                            e.inc(ins, "dO", ("o", b, qc, h), 1)
                        for qt in range(4):
                            gq = (b * NQC + qc) * 4 + qt
                            e.wait("eW", ("wo", gq))
                            e.bn_stats(st6[:, 0:6], pp[3][:, 0:512])
                            e.ss(e.bn_stats(st6[:, 6:12], pp[3][:, 512:1024]))
                            e.ss(e.bn_aggr(ag2e[:, :], st6[:, :]))
                            e.inc(e.tensor_scalar(veps3[:, :], ag2e[:, 1:2],
                                                  EPS, None, ALU.add),
                                  "eV1", ("st3", gq), 1)
                            e.wait("eS1", ("r3", gq))
                            e.inc(e.scalar_tensor_tensor(
                                nmr3[:, :], ag2e[:, 0:1], -1.0, rstd3[:, :],
                                ALU.mult, ALU.mult), "eV2", ("nmr3", gq), 1)
                            e.wait("eV2", ("nmr3", gq))
                            e.inc(e.tensor_scalar(t4sb[:, :], pp[3][:, :],
                                                  rstd3[:, 0:1], nmr3[:, 0:1],
                                                  ALU.mult, ALU.add),
                                  "eS2", ("t4", gq), 1)
                            e.wait("eX", ("x", gq))
                            if gq >= 1:
                                e.wait("eSp", ("x2", gq - 1))
                            e.tensor_tensor(x2sb[:, :], t4sb[:, :], xe[:, :],
                                            ALU.add)
                            e.bn_stats(st6[:, 0:6], x2sb[:, 0:512])
                            e.ss(e.bn_stats(st6[:, 6:12], x2sb[:, 512:1024]))
                            e.ss(e.bn_aggr(ag2e[:, :], st6[:, :]))
                            e.inc(e.tensor_scalar(veps2[:, :], ag2e[:, 1:2],
                                                  EPS, None, ALU.add),
                                  "eV3", ("x2st", gq), 1)
                            e.wait("eS3", ("r2", gq))
                            e.inc(e.scalar_tensor_tensor(
                                nmr2[:, :], ag2e[:, 0:1], -1.0, rstd2[:, :],
                                ALU.mult, ALU.mult), "eV4", ("nmr2", gq), 1)
                            e.wait("eV4", ("nmr2", gq))
                            e.inc(e.tensor_scalar(ysb[:, :], x2sb[:, :],
                                                  rstd2[:, 0:1], nmr2[:, 0:1],
                                                  ALU.mult, ALU.add),
                                  "eS4", ("y", gq), 1)
                            e.ss(e.tensor_reduce(red1[:, :], ysb[:, :], AX.X,
                                                 ALU.max,
                                                 apply_absolute_value=True))
                            e.inc(e.tensor_tensor(acc["mx"][:, :],
                                                  acc["mx"][:, :], red1[:, :],
                                                  ALU.max),
                                  "eV5", ("mx", gq), 1)

            def p1_scl(e):
                def chain(g):
                    e.wait("aV1", ("st", g))
                    e.ss(e.activation(rstdA[g % 2][:, :], vepsA[g % 2][:, :],
                                      ACT.Ln))
                    e.inc(e.activation(rstdA[g % 2][:, :], rstdA[g % 2][:, :],
                                       ACT.Exp, bias=0.0, scale=-0.5),
                          "aS1", ("rstd", g), 1)

                for t in range(NT):
                    chain(t)
                for b in range(n_seq):
                    e.wait("aSp", ("xn", b * NT + NT - 1))
                    for tc in range(NQC):
                        if (b, tc) != (0, 0):
                            i = b * NQC + tc - 1
                            pb2, ptc = divmod(i, NQC)
                            e.wait("bM", ("qkv", pb2 * NT + ptc * 4 + 3))
                        for dc in range(DCH):
                            e.inc(e.dma_start_transpose(
                                out=xnf[:, dc * 512:(dc + 1) * 512],
                                in_=xn_sp[(b % 2) * T + tc * 512:
                                          (b % 2) * T + (tc + 1) * 512,
                                          dc * P:(dc + 1) * P]),
                                "bX", ("xn", b, tc, dc), 16)
                        if b + 1 < n_seq:
                            for tt in range(4):
                                chain((b + 1) * NT + tc * 4 + tt)
                    for qc in range(NQC):
                        for h in range(H):
                            if not (b == 0 and qc == 0 and h == 0):
                                ph = prev_h(b, qc, h)
                                e.wait("dP", ("pv", *ph, ktmax(ph[1])))
                            for kt in range(ktmax(qc) + 1):
                                if kt >= 4 * qc:
                                    e.wait("dM", ("m", b, qc, h, kt))
                                else:
                                    e.wait("dS", ("s", b, qc, h, kt))
                                if kt >= 2:
                                    e.wait("dP", ("pv", b, qc, h, kt - 2))
                                e.inc(e.activation(
                                    et[kt % 2][:, :],
                                    pp[2][:, (kt % 2) * 512:(kt % 2 + 1) * 512],
                                    ACT.Exp), "dE", ("e", b, qc, h, kt), 1)
                        for qt in range(4):
                            gq = (b * NQC + qc) * 4 + qt
                            e.wait("eV1", ("st3", gq))
                            e.ss(e.activation(rstd3[:, :], veps3[:, :],
                                              ACT.Ln))
                            e.inc(e.activation(rstd3[:, :], rstd3[:, :],
                                               ACT.Exp, bias=0.0, scale=-0.5),
                                  "eS1", ("r3", gq), 1)
                            e.wait("eV3", ("x2st", gq))
                            e.ss(e.activation(rstd2[:, :], veps2[:, :],
                                              ACT.Ln))
                            e.inc(e.activation(rstd2[:, :], rstd2[:, :],
                                               ACT.Exp, bias=0.0, scale=-0.5),
                                  "eS3", ("r2", gq), 1)

            def p1_ten(e):
                e.wait("w16", ("w", 3))
                for b in range(n_seq):
                    if b >= 1:
                        e.wait("dE", ("e", b - 1, NQC - 1, H - 1,
                                      ktmax(NQC - 1)))
                        e.wait("dO", ("o", b - 1, NQC - 1, H - 1))
                    for tc in range(NQC):
                        for tt in range(4):
                            ti = b * NT + tc * 4 + tt
                            e.wait("bX", ("xn", b, tc, DCH - 1))
                            if ti >= 1:
                                e.wait("bQ", ("qk", ti - 1))
                                e.wait("bV", ("v", ti - 1))
                            else:
                                e.wait("bV", "ones")
                            mm = None
                            for dc in range(DCH):
                                lt = xnf[:, dc * 512 + tt * P:
                                         dc * 512 + tt * P + P]
                                e.ldweights(lt)
                                for wi in range(3):
                                    for hf in range(2):
                                        mm = e.matmul(
                                            pp[wi][:, hf * 512:(hf + 1) * 512],
                                            lt,
                                            wqkv_sb[:, wi * DCH * 1024 +
                                                    dc * 1024 + hf * 512:
                                                    wi * DCH * 1024 +
                                                    dc * 1024 + (hf + 1) * 512],
                                            start=(dc == 0),
                                            stop=(dc == DCH - 1))
                            e.inc(mm, "bM", ("qkv", ti), 1)
                    e.wait("bV", ("v", b * NT + NT - 1))
                    e.wait("bQ", ("qk", b * NT + NT - 1))
                    for qc in range(NQC):
                        e.wait("dQ", ("q", b, qc, H - 1))
                        for h in range(H):
                            if qc == 0:
                                e.wait("cK", ("k", b, h))
                            if not (b == 0 and qc == 0 and h == 0):
                                ph = prev_h(b, qc, h)
                                e.wait("dO", ("o", *ph))
                                e.wait("dE", ("e", *ph, ktmax(ph[1])))

                            def s_mm(kt):
                                if kt >= 2:
                                    e.wait("dE", ("e", b, qc, h, kt - 2))
                                ks = kfm[:, h * T + kt * P:
                                         h * T + (kt + 1) * P]
                                e.ldweights(ks)
                                e.inc(e.matmul(pp[2][:, (kt % 2) * 512:
                                                     (kt % 2 + 1) * 512],
                                               ks,
                                               qfm[:, h * 512:(h + 1) * 512],
                                               start=True, stop=True),
                                      "dS", ("s", b, qc, h, kt), 1)

                            def pv_mm(kt):
                                e.wait("dE", ("e", b, qc, h, kt))
                                mm = None
                                for qs in range(max(0, kt - 4 * qc), 4):
                                    els = et[kt % 2][:, qs * P:(qs + 1) * P]
                                    e.ldweights(els)
                                    mm = e.matmul(
                                        pp[qs // 2][:, (qs % 2) * 512:
                                                    (qs % 2) * 512 + 129],
                                        els,
                                        vaug[:, kt * 1032 + h * 129:
                                             kt * 1032 + (h + 1) * 129],
                                        start=(kt == 0),
                                        stop=(kt == 4 * qc + qs))
                                e.inc(mm, "dP", ("pv", b, qc, h, kt), 1)

                            K = ktmax(qc) + 1
                            s_mm(0)
                            for kt in range(1, K):
                                s_mm(kt)
                                pv_mm(kt - 1)
                            pv_mm(K - 1)
                        for qt in range(4):
                            gq = (b * NQC + qc) * 4 + qt
                            e.wait("eT", ("o", gq, DCH - 1))
                            if gq >= 1:
                                e.wait("eS2", ("t4", gq - 1))
                            mm = None
                            for c in range(DCH):
                                ofs = o_fm[:, c * P:(c + 1) * P]
                                e.ldweights(ofs)
                                for hf in range(2):
                                    mm = e.matmul(
                                        pp[3][:, hf * 512:(hf + 1) * 512],
                                        ofs,
                                        wo_sb[:, c * 1024 + hf * 512:
                                              c * 1024 + (hf + 1) * 512],
                                        start=(c == 0), stop=(c == DCH - 1))
                            e.inc(mm, "eW", ("wo", gq), 1)

            run_phase(nc, {"sync": p1_sync, "gpsimd": p1_gp, "vector": p1_vec,
                           "scalar": p1_scl, "tensor": p1_ten}, sched, sems)

        p2es = contextlib.ExitStack()

        def SB2(name, shape, dt=F32):
            return p2es.enter_context(nc.sbuf_tensor(name, shape, dt))

        with p2es:
            b1ps = contextlib.ExitStack()
            ppt = b1ps.enter_context(nc.psum_tensor("ppt", [P, 1024], BF16))
            wupT_sb = SB2("wupT_sb", [P, DCH * I3], BF16)
            wgtT_sb = SB2("wgtT_sb", [P, DCH * I3], BF16)
            wst = [SB2(f"wst{i}", [P, D]) for i in range(2)]
            am_t = SB2("am_t", [P, 32])
            sblk = SB2("sblk", [P, 32])
            invb = SB2("invb", [P, 32])
            uq_t = SB2("uq_t", [P, D])
            iq_t = SB2("iq_t", [P, D], I32)
            rq_t = SB2("rq_t", [P, D])
            wdq_t = SB2("wdq_t", [P, D], BF16)
            yf = SB2("yf", [P, DCH * TCH], BF16)
            ux = SB2("ux", [P, TCH], I32)
            rx = SB2("rx", [P, DCH * TCH], BF16)
            upst = SB2("upst", [P, TCH], BF16)
            gtst = SB2("gtst", [P, TCH], BF16)
            redh2 = SB2("redh2", [P, TCH // 2], BF16)

            def b1_gp(e):
                e.wait("eSp", ("x2", NTILE - 1))
                e.wait("eV5", ("mx", NTILE - 1))
                e.ss(e.tensor_reduce(scal1[0:1, 0:1], acc["mx"][:, :],
                                     AX.XYZWC, ALU.max))
                e.inc(e.dma_start(out=cc_in[0][0:1, 0:1],
                                  in_=scal1[0:1, 0:1]), "gd", ("in", 1), 16)
                e.wait("gd", ("in", 1))
                e.inc(e.collective_compute(
                    "AllReduce", ALU.max, replica_groups=groups,
                    ins=[cc_in[0][0:1, 0:1]], outs=[cc_out[0][0:1, 0:1]]),
                    "gc", ("cc", 1), 1)
                e.wait("gc", ("cc", 1))

            def b1_sync(e):
                e.wait("gc", ("cc", 1))
                src = ap_mk(cc_out[0], 0, [[0, P], [1, 1]])
                e.inc(e.dma_start(out=sc["raw"][:, :], in_=src), "cs", ("c", 1), 16)
                for wi, wd in enumerate((wu_d, wg_d)):
                    for ic in range(ICH):
                        j = wi * ICH + ic
                        if j >= 2:
                            e.wait("uV", ("dq", j - 2))
                        e.inc(e.dma_start(out=wst[j % 2][:, :],
                                          in_=wd[ic * P:(ic + 1) * P, :]),
                              "uL", ("w", j), 16)

            def b1_vec(e):
                e.wait("cs", ("c", 1))
                e.ss(e.tensor_scalar(sc["s_x"][:, :], sc["raw"][:, :],
                                     1.0 / QM, 1e-12, ALU.mult, ALU.max))
                e.inc(e.reciprocal(sc["inv_sx"][:, :], sc["s_x"][:, :]),
                      "vs", ("s", 1), 1)
                for j in range(2 * ICH):
                    e.wait("uL", ("w", j))
                    w3 = ap_mk(wst[j % 2].ap(), 0,
                               [[pstep(wst[j % 2]), P], [32, 32], [1, 32]])
                    e.ss(e.tensor_reduce(am_t[:, :], w3, AX.X, ALU.max,
                                         apply_absolute_value=True))
                    e.ss(e.tensor_scalar(sblk[:, :], am_t[:, :], 1.0 / WQM,
                                         1e-12, ALU.mult, ALU.max))
                    e.ss(e.reciprocal(invb[:, :], sblk[:, :]))
                    iv_b = ap_mk(invb.ap(), 0, [[pstep(invb), P], [1, 32], [0, 32]])
                    sb_b = ap_mk(sblk.ap(), 0, [[pstep(sblk), P], [1, 32], [0, 32]])
                    e.tensor_tensor(uq_t[:, :], wst[j % 2][:, :], iv_b, ALU.mult)
                    e.tensor_scalar(iq_t[:, :], uq_t[:, :], 1.0, None,
                                    ALU.mult)
                    if j >= 1:
                        e.wait("uT", ("tr", j - 1))
                    e.inc(e.tensor_tensor(wdq_t[:, :], iq_t[:, :], sb_b,
                                          ALU.mult), "uV", ("dq", j), 1)
                    e.wait("uT", ("tr", j))
                    wT = wupT_sb if j < ICH else wgtT_sb
                    ic = j % ICH
                    dst = ap_mk(wT.ap(), ic * P,
                                [[pstep(wT), P], [ICH * P, DCH], [1, P]])
                    srcp = ap_mk(ppt.ap(), 0,
                                 [[pstep(ppt), P], [P, DCH], [1, P]])
                    e.inc(e.tensor_copy(dst, srcp), "uE", ("ev", j), 1)

            def b1_ten(e):
                e.wait("tab", "id")
                for j in range(2 * ICH):
                    e.wait("uV", ("dq", j))
                    if j >= 1:
                        e.wait("uE", ("ev", j - 1))
                    tr = None
                    for c in range(DCH):
                        tr = e.transpose(ppt[:, c * P:(c + 1) * P],
                                         wdq_t[:, c * P:(c + 1) * P],
                                         ident_sb[:, :])
                    e.inc(tr, "uT", ("tr", j), 1)

            run_phase(nc, {"gpsimd": b1_gp, "sync": b1_sync, "vector": b1_vec,
                           "tensor": b1_ten}, sched, sems)
            b1ps.close()
            pp = [p2es.enter_context(nc.psum_tensor(f"pq{i}", [P, 1024], F32))
                  for i in range(4)]

            def p2_sync(e):
                e.wait("eSp", ("x2", NTOK // P - 1))
                for tc in range(NTC):
                    if tc >= 1:
                        e.wait("pV", ("rx", tc - 1))
                    for dc in range(DCH):
                        e.inc(e.dma_start_transpose(
                            out=yf[:, dc * TCH:(dc + 1) * TCH],
                            in_=y_sp[tc * TCH:(tc + 1) * TCH,
                                     dc * P:(dc + 1) * P]),
                            "pY", ("y", tc, dc), 16)

            def p2_vec(e):
                e.memset(acc["up"][:, :], 0.0)
                e.memset(acc["gt"][:, :], 0.0)
                e.wait("vs", ("s", 1))
                for tc in range(NTC):
                    e.wait("pY", ("y", tc, DCH - 1))
                    for dc in range(DCH):
                        e.tensor_scalar(ux[:, :], yf[:, dc * TCH:(dc + 1) * TCH],
                                        sc["inv_sx"][:, 0:1], None, ALU.mult)
                        ins = e.tensor_scalar(rx[:, dc * TCH:(dc + 1) * TCH],
                                              ux[:, :], 1.0, None,
                                              ALU.mult)
                    e.inc(ins, "pV", ("rx", tc), 1)
                    for ic in range(ICH):
                        for wi in range(2):
                            stg = upst if wi == 0 else gtst
                            a_ = acc["up"] if wi == 0 else acc["gt"]
                            e.wait("pM", ("mm", tc, ic, wi))
                            if tc + ic > 0:
                                e.wait("pSp", ("sp", *prev_icw(tc, ic, wi)))
                            ppa, ppb = (pp[0], pp[1]) if wi == 0 else (pp[2], pp[3])
                            e.tensor_copy(stg[:, 0:min(1024, TCH)], ppa[:, 0:min(1024, TCH)])
                            if TCH > 1024:
                                e.tensor_copy(stg[:, 1024:2048], ppb[:, 0:1024])
                            e.ss(e.tensor_reduce(red1[:, :], stg[:, :], AX.X,
                                                 ALU.max,
                                                 apply_absolute_value=True))
                            e.inc(e.tensor_tensor(a_[:, :], a_[:, :],
                                                  red1[:, :], ALU.max),
                                  "pV", ("ev", tc, ic, wi), 1)

            def p2_gp(e):
                for tc in range(NTC):
                    for ic in range(ICH):
                        for wi in range(2):
                            stg = upst if wi == 0 else gtst
                            spd = upr_sp if wi == 0 else gtr_sp
                            e.wait("pV", ("ev", tc, ic, wi))
                            e.inc(e.dma_start(
                                out=spd[ic * P:(ic + 1) * P,
                                        tc * TCH:(tc + 1) * TCH],
                                in_=stg[:, :]), "pSp", ("sp", tc, ic, wi), 16)

            def p2_ten(e):
                e.wait("uE", ("ev", 2 * ICH - 1))
                for tc in range(NTC):
                    e.wait("pV", ("rx", tc))
                    for ic in range(ICH):
                        for wi in range(2):
                            wT = wupT_sb if wi == 0 else wgtT_sb
                            ppa, ppb = (pp[0], pp[1]) if wi == 0 else (pp[2], pp[3])
                            if tc + ic > 0:
                                e.wait("pV", ("ev", *prev_ic2(tc, ic, wi)))
                            mm = None
                            for dc in range(DCH):
                                lt = wT[:, (dc * ICH + ic) * P:
                                        (dc * ICH + ic + 1) * P]
                                e.ldweights(lt)
                                for sl in range(TSL):
                                    ppx = ppa if sl < 2 else ppb
                                    mm = e.matmul(
                                        ppx[:, (sl % 2) * 512:(sl % 2 + 1) * 512],
                                        lt,
                                        rx[:, dc * TCH + sl * 512:
                                           dc * TCH + (sl + 1) * 512],
                                        start=(dc == 0), stop=(dc == DCH - 1))
                            e.inc(mm, "pM", ("mm", tc, ic, wi), 1)

            def prev_icw(tc, ic, wi):
                i = tc * ICH + ic - 1
                t2, ic2 = divmod(i, ICH)
                return (t2, ic2, wi)

            def prev_ic2(tc, ic, wi):
                i = tc * ICH + ic - 1
                t2, ic2 = divmod(i, ICH)
                return (t2, ic2, wi)

            run_phase(nc, {"sync": p2_sync, "vector": p2_vec, "gpsimd": p2_gp,
                           "tensor": p2_ten}, sched, sems)

        p3es = contextlib.ExitStack()

        def SB3(name, shape, dt=F32):
            return p3es.enter_context(nc.sbuf_tensor(name, shape, dt))

        with p3es:
            wdnT_sb = SB3("wdnT_sb", [P, ICH * 1024], BF16)
            uia = [SB3(f"uia{i}", [P, TCH], I32) for i in range(3)]
            uib = [SB3(f"uib{i}", [P, TCH], I32) for i in range(3)]
            ui_t = uia[0]
            x2t = SB3("x2t", [P, D])
            t4f = SB3("t4f", [P, D])
            outsb = SB3("outsb", [P, D])
            veps4 = SB3("veps4", [P, 1])
            rstd4 = SB3("rstd4", [P, 1]); nmr4 = SB3("nmr4", [P, 1])
            r4t = SB3("r4t", [P, 1])
            st3c = SB3("st3c", [P, 12])
            ag3c = SB3("ag3c", [P, 2])
            b2s = contextlib.ExitStack()
            wst3 = [b2s.enter_context(nc.sbuf_tensor(f"wst3{i}", [P, I3], F32))
                    for i in range(2)]
            am3 = b2s.enter_context(nc.sbuf_tensor("am3", [P, 96], F32))
            sblk3 = b2s.enter_context(nc.sbuf_tensor("sblk3", [P, 96], F32))
            invb3 = b2s.enter_context(nc.sbuf_tensor("invb3", [P, 96], F32))
            uq3 = b2s.enter_context(nc.sbuf_tensor("uq3", [P, I3], F32))
            iq3 = b2s.enter_context(nc.sbuf_tensor("iq3", [P, I3], I32))
            rq3 = b2s.enter_context(nc.sbuf_tensor("rq3", [P, I3], F32))
            wdq3 = b2s.enter_context(nc.sbuf_tensor("wdq3", [P, I3], BF16))
            ppt3 = b2s.enter_context(nc.psum_tensor("ppt3", [P, 1024], BF16))

            def b2_gp(e):
                e.wait("pSp", ("sp", NTC - 1, ICH - 1, 1))
                e.wait("pV", ("ev", NTC - 1, ICH - 1, 1))
                e.tensor_reduce(scal1[0:1, 0:1], acc["up"][:, :], AX.XYZWC,
                                ALU.max)
                e.ss(e.tensor_reduce(scal1[0:1, 1:2], acc["gt"][:, :],
                                     AX.XYZWC, ALU.max))
                e.inc(e.dma_start(out=cc_in[1][0:1, 0:2],
                                  in_=scal1[0:1, 0:2]), "gd", ("in", 2), 16)
                e.wait("gd", ("in", 2))
                e.inc(e.collective_compute(
                    "AllReduce", ALU.max, replica_groups=groups,
                    ins=[cc_in[1][0:1, 0:2]], outs=[cc_out[1][0:1, 0:2]]),
                    "gc", ("cc", 2), 1)
                e.wait("gc", ("cc", 2))

            def b2_sync(e):
                e.wait("gc", ("cc", 2))
                src = ap_mk(cc_out[1], 0, [[0, P], [1, 1]])
                e.inc(e.dma_start(out=sc["raw"][:, :], in_=src), "cs", ("c", 2, "up"), 16)
                src = ap_mk(cc_out[1], 1, [[0, P], [1, 1]])
                e.inc(e.dma_start(out=sc["raw2"][:, :], in_=src), "cs", ("c", 2, "gt"), 16)
                for mc in range(DCH):
                    if mc >= 2:
                        e.wait("dnV", ("dq", mc - 2))
                    e.inc(e.dma_start(out=wst3[mc % 2][:, :],
                                      in_=wd_d[mc * P:(mc + 1) * P, :]),
                          "dnL", ("w", mc), 16)

            def b2_vec(e):
                e.wait("cs", ("c", 2, "gt"))
                e.ss(e.tensor_tensor(sc["raw"][:, :], sc["raw"][:, :],
                                     sc["s_x"][:, :], ALU.mult))
                e.ss(e.tensor_scalar(sc["s_up"][:, :], sc["raw"][:, :],
                                     1.0 / QM, 1e-12, ALU.mult, ALU.max))
                e.ss(e.reciprocal(sc["rec"][:, :], sc["s_up"][:, :]))
                e.ss(e.tensor_tensor(sc["c_up"][:, :], sc["s_x"][:, :],
                                     sc["rec"][:, :], ALU.mult))
                e.ss(e.tensor_tensor(sc["raw2"][:, :], sc["raw2"][:, :],
                                     sc["s_x"][:, :], ALU.mult))
                e.ss(e.tensor_scalar(sc["s_gate"][:, :], sc["raw2"][:, :],
                                     1.0 / QM, 1e-12, ALU.mult, ALU.max))
                e.ss(e.reciprocal(sc["rec"][:, :], sc["s_gate"][:, :]))
                e.inc(e.tensor_tensor(sc["c_gt"][:, :], sc["s_x"][:, :],
                                      sc["rec"][:, :], ALU.mult),
                      "vs", ("s", 2), 1)
                for mc in range(DCH):
                    e.wait("dnL", ("w", mc))
                    w3 = ap_mk(wst3[mc % 2].ap(), 0,
                               [[pstep(wst3[mc % 2]), P], [32, 96], [1, 32]])
                    e.ss(e.tensor_reduce(am3[:, :], w3, AX.X, ALU.max,
                                         apply_absolute_value=True))
                    e.ss(e.tensor_scalar(sblk3[:, :], am3[:, :], 1.0 / WQM,
                                         1e-12, ALU.mult, ALU.max))
                    e.ss(e.reciprocal(invb3[:, :], sblk3[:, :]))
                    iv_b = ap_mk(invb3.ap(), 0, [[pstep(invb3), P], [1, 96], [0, 32]])
                    sb_b = ap_mk(sblk3.ap(), 0, [[pstep(sblk3), P], [1, 96], [0, 32]])
                    e.tensor_tensor(uq3[:, :], wst3[mc % 2][:, :], iv_b, ALU.mult)
                    e.tensor_scalar(iq3[:, :], uq3[:, :], 1.0, None,
                                    ALU.mult)
                    if mc >= 1:
                        e.wait("dnT", ("tr", mc - 1, 2))
                    e.inc(e.tensor_tensor(wdq3[:, :], iq3[:, :], sb_b,
                                          ALU.mult), "dnV", ("dq", mc), 1)
                    for g3 in range(3):
                        e.wait("dnT", ("tr", mc, g3))
                        dst = ap_mk(wdnT_sb.ap(), g3 * DCH * 1024 + mc * P,
                                    [[pstep(wdnT_sb), P], [1024, DCH], [1, P]])
                        srcp = ap_mk(ppt3.ap(), 0,
                                     [[pstep(ppt3), P], [P, DCH], [1, P]])
                        e.inc(e.tensor_copy(dst, srcp), "dnE",
                              ("ev", mc, g3), 1)

            def b2_ten(e):
                for mc in range(DCH):
                    e.wait("dnV", ("dq", mc))
                    for g3 in range(3):
                        if mc + g3 > 0:
                            pm, pg = (mc, g3 - 1) if g3 else (mc - 1, 2)
                            e.wait("dnE", ("ev", pm, pg))
                        tr = None
                        for c in range(DCH):
                            ic = g3 * DCH + c
                            tr = e.transpose(ppt3[:, c * P:(c + 1) * P],
                                             wdq3[:, ic * P:(ic + 1) * P],
                                             ident_sb[:, :])
                        e.inc(tr, "dnT", ("tr", mc, g3), 1)

            run_phase(nc, {"gpsimd": b2_gp, "sync": b2_sync, "vector": b2_vec,
                           "tensor": b2_ten}, sched, sems)
            b2s.close()

            abs_ = contextlib.ExitStack()
            gst = [abs_.enter_context(nc.sbuf_tensor(f"gst{i}", [P, TCH], BF16))
                   for i in range(2)]
            rg_t = None
            sg_t = abs_.enter_context(nc.sbuf_tensor("sg_t", [P, TCH], F32))
            rs_t = abs_.enter_context(nc.sbuf_tensor("rs_t", [P, TCH], F32))
            pt_t = [abs_.enter_context(nc.sbuf_tensor(f"pt_t{i}", [P, TCH], BF16))
                    for i in range(2)]
            ld_p = [abs_.enter_context(nc.sbuf_tensor(f"ld_p{i}", [P, TCH], BF16))
                    for i in range(2)]
            ld_u = [abs_.enter_context(nc.sbuf_tensor(f"ld_u{i}", [P, TCH], BF16))
                    for i in range(2)]
            qt_t = [abs_.enter_context(nc.sbuf_tensor(f"qt_t{i}", [P, TCH], BF16))
                    for i in range(2)]

            def a3_sync(e):
                for i in range(ICH * NTC):
                    if i >= 2:
                        e.wait("qI", ("a", i - 2))
                    ic, tc = divmod(i, NTC)
                    e.inc(e.dma_start(out=gst[i % 2][:, :],
                                      in_=gtr_sp[ic * P:(ic + 1) * P,
                                                 tc * TCH:(tc + 1) * TCH]),
                          "qL", ("g", i), 16)

            def a3_vec(e):
                e.memset(acc["g2"][:, :], 0.0)
                for i in range(ICH * NTC):
                    e.wait("qI", ("a", i))
                    e.wait("qI", ("b", i))
                    e.tensor_scalar(rs_t[:, :], uib[i % 3][:, :], 1.0,
                                    U16 - 1.0, ALU.mult, ALU.min)
                    if i >= 2:
                        e.wait("qSp", ("p", i - 2))
                    e.tensor_tensor(pt_t[i % 2][:, :], uia[i % 3][:, :],
                                    rs_t[:, :], ALU.mult)
                    e.ss(e.tensor_reduce(red1[:, :], pt_t[i % 2][:, :], AX.X,
                                         ALU.max, apply_absolute_value=True))
                    e.inc(e.tensor_tensor(acc["g2"][:, :], acc["g2"][:, :],
                                          red1[:, :], ALU.max),
                          "qV", ("p", i), 1)

            def a3_scl(e):
                e.wait("vs", ("s", 2))
                for i in range(ICH * NTC):
                    e.wait("qL", ("g", i))
                    if i >= 3:
                        e.wait("qV", ("p", i - 3))
                    e.inc(e.activation(uia[i % 3][:, :], gst[i % 2][:, :],
                                       ACT.Identity, bias=0.0,
                                       scale=sc["c_gt"][:, 0:1]),
                          "qI", ("a", i), 1)
                    e.wait("qI", ("a", i))
                    e.ss(e.activation(sg_t[:, :], uia[i % 3][:, :],
                                      ACT.Sigmoid, bias=0.0,
                                      scale=sc["s_gate"][:, 0:1]))
                    if i >= 3:
                        e.wait("qV", ("p", i - 3))
                    e.inc(e.activation(uib[i % 3][:, :], sg_t[:, :],
                                       ACT.Identity, bias=0.0, scale=U16),
                          "qI", ("b", i), 1)
                    e.wait("qI", ("b", i))

            def a3_gp(e):
                for i in range(ICH * NTC):
                    e.wait("qV", ("p", i))
                    ic, tc = divmod(i, NTC)
                    e.inc(e.dma_start(out=p_sp[ic * P:(ic + 1) * P,
                                               tc * TCH:(tc + 1) * TCH],
                                      in_=pt_t[i % 2][:, :]), "qSp", ("p", i), 16)

            run_phase(nc, {"sync": a3_sync, "vector": a3_vec, "scalar": a3_scl,
                           "gpsimd": a3_gp}, sched, sems)

            def b3_gp(e):
                e.wait("qSp", ("p", ICH * NTC - 1))
                e.ss(e.tensor_reduce(scal1[0:1, 0:1], acc["g2"][:, :],
                                     AX.XYZWC, ALU.max))
                e.inc(e.dma_start(out=cc_in[2][0:1, 0:1],
                                  in_=scal1[0:1, 0:1]), "gd", ("in", 3), 16)
                e.wait("gd", ("in", 3))
                e.inc(e.collective_compute(
                    "AllReduce", ALU.max, replica_groups=groups,
                    ins=[cc_in[2][0:1, 0:1]], outs=[cc_out[2][0:1, 0:1]]),
                    "gc", ("cc", 3), 1)
                e.wait("gc", ("cc", 3))

            def b3_sync(e):
                e.wait("gc", ("cc", 3))
                src = ap_mk(cc_out[2], 0, [[0, P], [1, 1]])
                e.inc(e.dma_start(out=sc["raw"][:, :], in_=src), "cs", ("c", 3), 16)

            def b3_vec(e):
                e.wait("cs", ("c", 3))
                e.ss(e.tensor_scalar(sc["f_g2"][:, :], sc["s_gate"][:, :],
                                     1.0 / U16, None, ALU.mult))
                e.ss(e.tensor_tensor(sc["raw"][:, :], sc["raw"][:, :],
                                     sc["f_g2"][:, :], ALU.mult))
                e.ss(e.tensor_scalar(sc["s_g2"][:, :], sc["raw"][:, :],
                                     1.0 / QM, 1e-12, ALU.mult, ALU.max))
                e.ss(e.reciprocal(sc["rec"][:, :], sc["s_g2"][:, :]))
                e.inc(e.tensor_tensor(sc["c_g2"][:, :], sc["f_g2"][:, :],
                                      sc["rec"][:, :], ALU.mult),
                      "vs", ("s", 3), 1)

            run_phase(nc, {"gpsimd": b3_gp, "sync": b3_sync, "vector": b3_vec},
                      sched, sems)

            def b_sync(e):
                for i in range(ICH * NTC):
                    if i >= 2:
                        e.wait("rI", ("a", i - 2))
                    ic, tc = divmod(i, NTC)
                    e.inc(e.dma_start(out=ld_p[i % 2][:, :],
                                      in_=p_sp[ic * P:(ic + 1) * P,
                                               tc * TCH:(tc + 1) * TCH]),
                          "rLp", ("p", i), 16)

            def b_scl(e):
                e.wait("vs", ("s", 3))
                for i in range(ICH * NTC):
                    e.wait("rLp", ("p", i))
                    if i >= 3:
                        e.wait("rV", ("q", i - 3))
                    e.inc(e.activation(uia[i % 3][:, :], ld_p[i % 2][:, :],
                                       ACT.Identity, bias=0.0,
                                       scale=sc["c_g2"][:, 0:1]),
                          "rI", ("a", i), 1)
                    e.wait("rLu", ("u", i))
                    e.inc(e.activation(uib[i % 3][:, :], ld_u[i % 2][:, :],
                                       ACT.Identity, bias=0.0,
                                       scale=sc["c_up"][:, 0:1]),
                          "rI", ("b", i), 1)

            def b_vec(e):
                e.memset(acc["o"][:, :], 0.0)
                for i in range(ICH * NTC):
                    e.wait("rI", ("a", i))
                    e.wait("rI", ("b", i))
                    if i >= 2:
                        e.wait("rSp", ("q", i - 2))
                    e.tensor_tensor(qt_t[i % 2][:, :], uia[i % 3][:, :],
                                    uib[i % 3][:, :], ALU.mult)
                    e.ss(e.tensor_reduce(red1[:, :], qt_t[i % 2][:, :], AX.X,
                                         ALU.max, apply_absolute_value=True))
                    e.inc(e.tensor_tensor(acc["o"][:, :], acc["o"][:, :],
                                          red1[:, :], ALU.max),
                          "rV", ("q", i), 1)

            def b_gp(e):
                N = ICH * NTC
                for i in range(N):
                    ic, tc = divmod(i, NTC)
                    if i >= 2:
                        e.wait("rI", ("b", i - 2))
                    e.inc(e.dma_start(out=ld_u[i % 2][:, :],
                                      in_=upr_sp[ic * P:(ic + 1) * P,
                                                 tc * TCH:(tc + 1) * TCH]),
                          "rLu", ("u", i), 16)
                    if i >= 1:
                        jc, js = divmod(i - 1, NTC)
                        e.wait("rV", ("q", i - 1))
                        e.inc(e.dma_start(out=qr_sp[jc * P:(jc + 1) * P,
                                                   js * TCH:(js + 1) * TCH],
                                          in_=qt_t[(i - 1) % 2][:, :]),
                              "rSp", ("q", i - 1), 16)
                e.wait("rV", ("q", N - 1))
                jc, js = divmod(N - 1, NTC)
                e.inc(e.dma_start(out=qr_sp[jc * P:(jc + 1) * P,
                                           js * TCH:(js + 1) * TCH],
                                  in_=qt_t[(N - 1) % 2][:, :]), "rSp", ("q", N - 1), 16)

            run_phase(nc, {"sync": b_sync, "scalar": b_scl, "vector": b_vec,
                           "gpsimd": b_gp}, sched, sems)

            def b4_gp(e):
                e.wait("rSp", ("q", ICH * NTC - 1))
                e.ss(e.tensor_reduce(scal1[0:1, 0:1], acc["o"][:, :],
                                     AX.XYZWC, ALU.max))
                e.inc(e.dma_start(out=cc_in[3][0:1, 0:1],
                                  in_=scal1[0:1, 0:1]), "gd", ("in", 4), 16)
                e.wait("gd", ("in", 4))
                e.inc(e.collective_compute(
                    "AllReduce", ALU.max, replica_groups=groups,
                    ins=[cc_in[3][0:1, 0:1]], outs=[cc_out[3][0:1, 0:1]]),
                    "gc", ("cc", 4), 1)
                e.wait("gc", ("cc", 4))

            def b4_sync(e):
                e.wait("gc", ("cc", 4))
                src = ap_mk(cc_out[3], 0, [[0, P], [1, 1]])
                e.inc(e.dma_start(out=sc["raw"][:, :], in_=src), "cs", ("c", 4), 16)

            def b4_vec(e):
                e.wait("cs", ("c", 4))
                e.ss(e.tensor_tensor(sc["f_o"][:, :], sc["s_g2"][:, :],
                                     sc["s_up"][:, :], ALU.mult))
                e.ss(e.tensor_tensor(sc["raw"][:, :], sc["raw"][:, :],
                                     sc["f_o"][:, :], ALU.mult))
                e.ss(e.tensor_scalar(sc["s_o"][:, :], sc["raw"][:, :],
                                     1.0 / QM, 1e-12, ALU.mult, ALU.max))
                e.ss(e.reciprocal(sc["rec"][:, :], sc["s_o"][:, :]))
                e.ss(e.tensor_tensor(sc["c_o"][:, :], sc["f_o"][:, :],
                                     sc["rec"][:, :], ALU.mult))
                e.inc(e.tensor_tensor(sc["so2"][:, :], sc["s_o"][:, :],
                                      sc["s_o"][:, :], ALU.mult), "vs", ("s", 4), 1)

            run_phase(nc, {"gpsimd": b4_gp, "sync": b4_sync, "vector": b4_vec},
                      sched, sems)
            abs_.close()

            csc = contextlib.ExitStack()
            qst = csc.enter_context(
                nc.sbuf_tensor("qst", [P, ICH * TCH], BF16))
            ppc = [csc.enter_context(nc.psum_tensor(f"ppc{i}", [P, 1024], F32))
                   for i in range(2)]
            NTT = TCH // P

            def c_sync(e):
                for tc in range(NTC):
                    for ic in range(ICH):
                        if tc >= 1:
                            e.wait("fM", ("mm", tc * NTT - 1))
                        lin = tc * ICH + ic
                        if lin >= 2:
                            e.wait("fV0", ("ro", *divmod(lin - 2, ICH)))
                        e.inc(e.dma_start(
                            out=qst[:, ic * TCH:(ic + 1) * TCH],
                            in_=qr_sp[ic * P:(ic + 1) * P,
                                      tc * TCH:(tc + 1) * TCH]),
                            "fL", ("q", tc, ic), 16)
                    for tt in range(NTT):
                        gt_ = tc * NTT + tt
                        if gt_ >= 1:
                            e.wait("fV3", ("out", gt_ - 1))
                        r0 = tc * TCH + tt * P
                        e.inc(e.dma_start(out=x2t[:, :],
                                          in_=x2_sp[r0:r0 + P, :]),
                              "fX", ("x2", gt_), 16)

            def c_vec(e):
                e.wait("vs", ("s", 4))
                for tc in range(NTC):
                    for ic in range(ICH):
                        e.wait("fL", ("q", tc, ic))
                        e.tensor_scalar(ui_t[:, :],
                                        qst[:, ic * TCH:(ic + 1) * TCH],
                                        sc["c_o"][:, 0:1], None, ALU.mult)
                        e.inc(e.tensor_scalar(
                            qst[:, ic * TCH:(ic + 1) * TCH], ui_t[:, :], 1.0,
                            None, ALU.mult),
                            "fV0", ("ro", tc, ic), 1)
                    for tt in range(NTT):
                        gt_ = tc * NTT + tt
                        e.wait("fM", ("mm", gt_))
                        e.bn_stats(st3c[:, 0:6], ppc[gt_ % 2][:, 0:512])
                        e.ss(e.bn_stats(st3c[:, 6:12], ppc[gt_ % 2][:, 512:1024]))
                        e.ss(e.bn_aggr(ag3c[:, :], st3c[:, :]))
                        e.inc(e.tensor_scalar(veps4[:, :], ag3c[:, 1:2],
                                              sc["so2"][:, 0:1], EPS, ALU.mult,
                                              ALU.add), "fV1", ("st", gt_), 1)
                        e.wait("fS1", ("r4", gt_))
                        e.ss(e.tensor_tensor(rstd4[:, :], r4t[:, :],
                                             sc["s_o"][:, :], ALU.mult))
                        e.inc(e.scalar_tensor_tensor(
                            nmr4[:, :], ag3c[:, 0:1], -1.0, rstd4[:, :],
                            ALU.mult, ALU.mult), "fV2", ("nmr", gt_), 1)
                        e.wait("fS2", ("t4", gt_))
                        e.wait("fX", ("x2", gt_))
                        if gt_ >= 1:
                            e.wait("fO", ("st", gt_ - 1))
                        e.inc(e.tensor_tensor(outsb[:, :], t4f[:, :],
                                              x2t[:, :], ALU.add),
                              "fV3", ("out", gt_), 1)

            def c_scl(e):
                for tc in range(NTC):
                    for tt in range(NTT):
                        gt_ = tc * NTT + tt
                        e.wait("fV1", ("st", gt_))
                        e.ss(e.activation(r4t[:, :], veps4[:, :], ACT.Ln))
                        e.inc(e.activation(r4t[:, :], r4t[:, :], ACT.Exp,
                                           bias=0.0, scale=-0.5),
                              "fS1", ("r4", gt_), 1)
                        e.wait("fV2", ("nmr", gt_))
                        e.inc(e.activation(t4f[:, :], ppc[gt_ % 2][:, :],
                                           ACT.Identity, bias=nmr4[:, :],
                                           scale=rstd4[:, :]),
                              "fS2", ("t4", gt_), 1)

            def c_ten(e):
                e.wait("dnE", ("ev", DCH - 1, 2))
                for tc in range(NTC):
                    for tt in range(NTT):
                        gt_ = tc * NTT + tt
                        if gt_ >= 2:
                            e.wait("fS2", ("t4", gt_ - 2))
                        mm = None
                        for ic in range(ICH):
                            if tt == 0:
                                e.wait("fV0", ("ro", tc, ic))
                            lt = qst[:, ic * TCH + tt * P:ic * TCH + tt * P + P]
                            e.ldweights(lt)
                            for hf in range(2):
                                mm = e.matmul(
                                    ppc[gt_ % 2][:, hf * 512:(hf + 1) * 512],
                                    lt,
                                    wdnT_sb[:, ic * 1024 + hf * 512:
                                            ic * 1024 + (hf + 1) * 512],
                                    start=(ic == 0), stop=(ic == ICH - 1))
                        e.inc(mm, "fM", ("mm", gt_), 1)

            def c_gp(e):
                for tc in range(NTC):
                    for tt in range(NTT):
                        gt_ = tc * NTT + tt
                        e.wait("fV3", ("out", gt_))
                        r0 = tc * TCH + tt * P
                        e.inc(e.dma_start(out=out_d[r0:r0 + P, :],
                                          in_=outsb[:, :]),
                              "fO", ("st", gt_), 16)
                e.wait_n("fO", 16 * NTILE)

            run_phase(nc, {"sync": c_sync, "vector": c_vec, "scalar": c_scl,
                           "tensor": c_ten, "gpsimd": c_gp}, sched, sems)
            csc.close()

    return nc


def gq_prev(b, qc, NQC):
    gq0 = (b * NQC + qc) * 4
    return gq0 - 1 if gq0 >= 1 else None


def make_tables(T):
    NT = T // P
    pos = np.arange(T, dtype=np.float64)
    dims = np.arange(32, dtype=np.float64)
    ang = np.power(10000.0, -dims / 32.0)
    rad = pos[:, None] * ang[None, :]
    cos = np.cos(rad); sin = np.sin(rad)
    cosl = cos.reshape(NT, P, 32).transpose(1, 0, 2).reshape(P, NT * 32)
    sinl = sin.reshape(NT, P, 32).transpose(1, 0, 2).reshape(P, NT * 32)
    k = np.arange(P)[:, None]
    q = np.arange(P)[None, :]
    cm = np.where(q < k, np.float32(-30000.0), np.float32(0.0)).astype(np.float32)
    import ml_dtypes
    idm = np.eye(P, dtype=np.float32).astype(ml_dtypes.bfloat16)
    return (np.ascontiguousarray(cosl.astype(np.float32)),
            np.ascontiguousarray(sinl.astype(np.float32)), cm, idm)


def make_in_map(xc, wq, wk, wv, wo, wg, wu, wd, T):
    cosl, sinl, cm, idm = make_tables(T)
    return {
        "x_tm": np.ascontiguousarray(xc.astype(np.float32)),
        "wq": np.ascontiguousarray(wq.reshape(D, D).astype(np.float32)),
        "wk": np.ascontiguousarray(wk.reshape(D, D).astype(np.float32)),
        "wv": np.ascontiguousarray(wv.reshape(D, D).astype(np.float32)),
        "wo": np.ascontiguousarray(wo.reshape(D, D).astype(np.float32)),
        "w_gate": np.ascontiguousarray(wg.astype(np.float32)),
        "w_up": np.ascontiguousarray(wu.astype(np.float32)),
        "w_down": np.ascontiguousarray(wd.astype(np.float32)),
        "rope_cos": cosl, "rope_sin": sinl, "cmask": cm, "ident": idm,
    }


def kernel(L0_attn_x, L0_attn_wq, L0_attn_wk, L0_attn_wv, L0_attn_wo,
           L0_ffn_w_gate, L0_ffn_w_up, L0_ffn_w_down):
    n_cores = 8
    B, T, _ = L0_attn_x.shape
    n_seq = B // n_cores
    x = np.asarray(L0_attn_x, dtype=np.float32)
    nc = build_program(n_seq=n_seq, T=T, n_cores=n_cores)
    in_maps = []
    for c in range(n_cores):
        xc = x[c * n_seq:(c + 1) * n_seq].reshape(n_seq * T, D)
        in_maps.append(make_in_map(xc, np.asarray(L0_attn_wq),
                                   np.asarray(L0_attn_wk),
                                   np.asarray(L0_attn_wv),
                                   np.asarray(L0_attn_wo),
                                   np.asarray(L0_ffn_w_gate),
                                   np.asarray(L0_ffn_w_up),
                                   np.asarray(L0_ffn_w_down), T))
    res = run_bass_kernel_spmd(nc, in_maps, core_ids=list(range(n_cores)))
    out = np.concatenate([np.asarray(r["out"], dtype=np.float32)
                          for r in res.results], axis=0)
    return out.reshape(B, T, D)
